# version 41
# baseline (speedup 1.0000x reference)
import numpy as np
import ml_dtypes
from contextlib import ExitStack

import concourse.bass as bass
import concourse.mybir as mybir
from concourse.bass_utils import run_bass_kernel_spmd

F32 = mybir.dt.float32
BF16 = mybir.dt.bfloat16
AF = mybir.ActivationFunctionType
OP = mybir.AluOpType

D = 1024
NC = 8
NH = 8
T = 4096
NT = 32
QT = 17
NQ = QT * 128
EPS = 1e-5
A_SLICES = [(i * 256, 256) for i in range(8)] + [(2048, 128)]
Q_PROJ_SLICES = [(0, 512), (512, 512), (1024, 512), (1536, 512), (2048, 128)]
TCHUNKS = [(i * 256, 256) for i in range(8)] + [(2048, 128)]

SEM_NAMES = ("d_x", "d_cst", "d_xt0", "d_xt1", "d_xt2", "d_xt3", "d_wa", "d_spk", "d_spv", "d_spq",
             "d_q0", "d_q1", "d_k0", "d_k1", "d_v0", "d_v1", "d_wf0", "d_wf1",
             "d_out", "v_ln", "s_sd", "s_xn", "t_stw", "c_stv", "s_exp", "v_du",
             "t_proj", "v_proj", "t_st", "t_ctx", "v_ctx", "s_cpx", "t_ffn", "s_act",
             "v_sq", "v_mix", "v_fin", "v_self", "v_den", "s_rcp", "s_self", "g_rcp",
             "s_cpy", "d_ws0", "d_tw")


def build_nc():
    nc = bass.Bass("TRN2", target_bir_lowering=False)

    x_h = nc.dram_tensor("x", [T, D], F32, kind="ExternalInput")
    wq_h = nc.dram_tensor("wq", [D, D], F32, kind="ExternalInput")
    wk_h = nc.dram_tensor("wk", [D, D], F32, kind="ExternalInput")
    wv_h = nc.dram_tensor("wv", [D, D], F32, kind="ExternalInput")
    wr2_h = nc.dram_tensor("wr2", [D, D], F32, kind="ExternalInput")
    wk2_h = nc.dram_tensor("wk2", [D, D], F32, kind="ExternalInput")
    wv2_h = nc.dram_tensor("wv2", [D, D], F32, kind="ExternalInput")
    cvec_h = nc.dram_tensor("cvec", [128, 40], F32, kind="ExternalInput")
    ident_h = nc.dram_tensor("ident", [128, 128], BF16, kind="ExternalInput")
    identf_h = nc.dram_tensor("identf", [128, 128], F32, kind="ExternalInput")
    out_h = nc.dram_tensor("out", [NQ, D], F32, kind="ExternalOutput")
    wspill = nc.dram_tensor("wspill", [3 * D, D], BF16)
    kspill = nc.dram_tensor("kspill", [128, NH * T], BF16)
    vtspill = nc.dram_tensor("vtspill", [128, NH * T], BF16)
    qspill = nc.dram_tensor("qspill", [128, NH * NQ], BF16)

    es = ExitStack()

    def sbuf(name, cols, dt):
        return es.enter_context(nc.sbuf_tensor(name, [128, cols], dt))

    def psum(name, cols, dt):
        return es.enter_context(nc.psum_tensor(name, [128, cols], dt))

    x_res = sbuf("x_res", QT * 1024, F32)
    xtmp = sbuf("xtmp", 4 * 1024, BF16)
    arena = sbuf("arena", 32768, BF16)
    xn_sb = sbuf("xn_sb", 2 * 1024, F32)
    scratch = sbuf("scratch", 10368, BF16)
    scratch_f = sbuf("scratch_f", 1536, F32)
    ctxs = sbuf("ctxs", 2048, F32)
    wkbuf = sbuf("wkbuf", 4096, BF16)
    wa = sbuf("wa", 6144, BF16)
    cvec = sbuf("cvecsb", 40, F32)
    ident = sbuf("identsb", 128, BF16)
    identf = sbuf("identfsb", 128, F32)
    stats = sbuf("stats", 72, F32)
    csts = sbuf("csts", 2, F32)
    fstats = sbuf("fstats", 36, F32)
    carry = sbuf("carry", 8, BF16)

    stAB = psum("stAB", 2048, F32)
    ctxP = psum("ctxP", 2048, F32)

    sems = {n: es.enter_context(nc.semaphore(n)) for n in SEM_NAMES}
    sched = {"tensor": [], "vector": [], "scalar": [], "sync": [], "gpsimd": []}
    cnt = {n: 0 for n in SEM_NAMES}

    def ev(eng, waits, inc, op, amt=1):
        waits = [(sems[n], v) for (n, v) in waits if v > 0]
        incsem = sems[inc] if inc else None

        def fn(e, waits=tuple(waits), incsem=incsem, op=op, amt=amt):
            for s, v in waits:
                e.wait_ge(s, v)
            r = op(e)
            if incsem is not None:
                r.then_inc(incsem, amt)
        sched[eng].append(fn)
        if inc:
            cnt[inc] += amt

    def ev_noinc(eng, waits, op):
        ev(eng, waits, None, op)

    def drain(eng):
        sched[eng].append(lambda e: e.drain())

    def ap3(t_sb, offset, dims):
        return bass.AP(t_sb, offset, [[t_sb.shape[1], 128]] + dims)

    A = arena.shape[1]

    def arena_ap(offset, dims):
        return bass.AP(arena, offset, [[A, 128]] + dims)

    stsub = [stAB[:, i * 512:(i + 1) * 512] for i in range(4)]
    stsub_mark = [("d_cst", 0)] * 4
    stsub_rot = [0]

    def next_stsub():
        i = stsub_rot[0] % 4
        stsub_rot[0] += 1
        return i

    tr_mark = [0, 0]
    xn_tr_mark = {}
    ln_done = [0]

    ev("sync", [], "d_cst", lambda e: e.dma_start(cvec[:], cvec_h.ap()), 16)
    ev("sync", [], "d_cst", lambda e: e.dma_start(ident[:], ident_h.ap()), 16)
    ev("sync", [], "d_cst", lambda e: e.dma_start(identf[:], identf_h.ap()), 16)
    x_tiled = x_h.ap().rearrange("(t p) c -> p t c", p=128)
    ev("sync", [], "d_x",
       lambda e: e.dma_start(ap3(x_res, 0, [[1024, QT], [1, 1024]]), x_tiled[:, 0:QT, :]), 16)
    ev_noinc("vector", [], lambda e: e.memset(carry[:, 0:NC], 0.0))
    ev_noinc("vector", [], lambda e: e.memset(csts[:, 0:1], EPS))
    ev_noinc("vector", [], lambda e: e.memset(csts[:, 1:2], -1.0))

    def op_warm0(e):
        r = None
        for _ in range(48):
            r = e.matmul(ctxP[:, 0:128], ident[:], ident[:], start=True, stop=True)
        return r
    ev("tensor", [("d_cst", 48)], None, op_warm0)

    def emit_xtmp_dma(t):
        par = t % 4
        waits = [("s_xn", t - 3)] if t - 4 >= QT else []
        ev("gpsimd", waits, "d_xt%d" % par,
           lambda e, par=par, t=t: e.dma_start(
               xtmp[:, par * 1024:(par + 1) * 1024], x_tiled[:, t, :]), 16)

    def emit_wa_dma(g, wi, w_h):
        src = w_h.ap().rearrange("(cj p) d -> p cj d", p=128)[:, :, g * 128:(g + 1) * 128]
        if g >= 2:
            waits = [("t_proj", 21 * (g - 1) + 1)]
        elif g == 1:
            waits = [("t_proj", 1)]
        elif wi > 0:
            waits = [("d_x", 16)]
        else:
            waits = []
        ev("gpsimd", waits, "d_wa",
           lambda e, g=g, wi=wi, src=src: e.dma_start(
               bass.AP(wa, (g % 2) * 3072 + wi * 1024, [[wa.shape[1], 128], [128, NC], [1, 128]]),
               src), 16)

    emit_xtmp_dma(17)
    emit_xtmp_dma(18)
    for wi, w_h in enumerate([wq_h, wk_h, wv_h]):
        emit_wa_dma(0, wi, w_h)
    for t in range(19, NT):
        emit_xtmp_dma(t)

    def emit_ln(src_ap, src_wait, dst_ap, extra_norm_wait):
        n = ln_done[0]
        par = n % 4
        so = par * 18
        bn12 = stats[:, so:so + 12]
        mv = stats[:, so + 12:so + 14]
        sd = stats[:, so + 14:so + 15]
        inv = stats[:, so + 15:so + 16]
        waits = list(src_wait)
        if n - 4 >= 0:
            waits.append(("s_sd", n - 3))
        ev_noinc("vector", waits, lambda e: e.bn_stats(bn12[:, 0:6], src_ap[:, 0:512]))
        ev("vector", [], "v_self", lambda e: e.bn_stats(bn12[:, 6:12], src_ap[:, 512:1024]))
        ev("vector", [("v_self", cnt["v_self"])], "v_ln", lambda e: e.bn_aggr(mv, bn12))
        ev("scalar", [("v_ln", cnt["v_ln"])], "s_sd",
           lambda e: e.activation(sd, mv[:, 1:2], AF.Sqrt, bias=csts[:, 0:1]))
        ev("vector", [("s_sd", cnt["s_sd"])], "v_self", lambda e: e.reciprocal(inv, sd))
        nwaits = list(extra_norm_wait) + [("v_self", cnt["v_self"])]
        if n - 2 in xn_tr_mark:
            nwaits.append(("t_stw", xn_tr_mark[n - 2]))
        ev("vector", nwaits, "s_xn",
           lambda e: e.tensor_scalar(dst_ap, src_ap, mv[:, 0:1], inv,
                                     op0=OP.subtract, op1=OP.mult))
        ln_done[0] += 1
        return n

    def emit_tile_transposes(src_ap, half, idf, extra_waits=()):
        tw = [("c_stv", tr_mark[half])] + list(extra_waits)

        def op(e, src_ap=src_ap, half=half, idf=idf):
            r = None
            for j in range(NC):
                r = e.transpose(ctxP[:, half * 1024 + j * 128: half * 1024 + (j + 1) * 128],
                                src_ap[:, j * 128:(j + 1) * 128], idf)
            return r
        ev("tensor", tw, "t_stw", op)

    kg = scratch[:, 0:4096]
    vtb = scratch[:, 4096:8192]
    qg = scratch[:, 8192:10368]
    vsb = scratch_f[:, 0:1024]
    tile_evac_mark = {}
    vsb_tr_mark = [0, 0]

    def emit_proj_group(g, kind, dst_ap, s0, w, bias_col, wi, spill_wait=()):
        si = next_stsub()
        ps = stsub[si]
        req = (s0 + w - 1) // 128
        tw = [("d_wa", 48 * (g + 1)),
              (stsub_mark[si][0], stsub_mark[si][1]),
              ("c_stv", tile_evac_mark[req])]

        def op(e, ps=ps, s0=s0, w=w, wi=wi, g=g):
            r = None
            for cj in range(NC):
                r = e.matmul(ps[:, 0:w],
                             wa[:, (g % 2) * 3072 + wi * 1024 + cj * 128:
                                 (g % 2) * 3072 + wi * 1024 + (cj + 1) * 128],
                             arena_ap(cj * 4096 + s0, [[1, w]]),
                             start=(cj == 0), stop=(cj == NC - 1))
            return r
        ev("tensor", tw, "t_proj", op)
        vw = [("t_proj", cnt["t_proj"])] + list(spill_wait)
        if kind == "v":
            par2 = dst_ap
            vw.append(("t_stw", vsb_tr_mark[par2]))
            dst = vsb[:, par2 * 512:(par2 + 1) * 512]

            def opv(e, dst=dst, ps=ps, w=w, bias_col=bias_col):
                return e.tensor_scalar(dst, ps[:, 0:w],
                                       cvec[:, bias_col:bias_col + 1],
                                       None, op0=OP.add)
            ev("vector", vw + [("d_cst", 48)], "v_proj", opv)
            stsub_mark[si] = ("v_proj", cnt["v_proj"])
            return
        dst = dst_ap

        def opc(e, dst=dst, ps=ps, w=w):
            return e.activation(dst, ps[:, 0:w], AF.Copy)
        ev("scalar", vw, "s_cpy", opc)
        stsub_mark[si] = ("s_cpy", cnt["s_cpy"])
        ev("vector", [("s_cpy", cnt["s_cpy"]), ("d_cst", 48)], "v_proj",
           lambda e, dst=dst, bias_col=bias_col: e.tensor_scalar(
               dst, dst, cvec[:, bias_col:bias_col + 1], None, op0=OP.add))

    def emit_v_post(g, vs, vp_mark=None):
        par2 = vs % 2
        half = vs % 2
        tw = [vp_mark or ("v_proj", cnt["v_proj"]), ("c_stv", tr_mark[half])]

        def op_trv(e, par2=par2, half=half):
            r = None
            for j in range(4):
                r = e.transpose(
                    ctxP[:, half * 1024 + j * 128: half * 1024 + (j + 1) * 128],
                    vsb[:, par2 * 512 + j * 128: par2 * 512 + (j + 1) * 128],
                    identf[:])
            return r
        ev("tensor", tw, "t_stw", op_trv)
        vsb_tr_mark[par2] = cnt["t_stw"]
        ev("scalar", [("t_stw", cnt["t_stw"]), ("d_spv", 16 * g)], "c_stv",
           lambda e, vs=vs, half=half: e.activation(
               vtb[:, vs * 512:(vs + 1) * 512],
               ctxP[:, half * 1024: half * 1024 + 512], AF.Copy))
        tr_mark[half] = cnt["c_stv"]

    def emit_spills(g):
        ev("sync", [("v_proj", cnt["v_proj"])], "d_spq",
           lambda e, g=g: e.dma_start(
               bass.AP(qspill, g * NQ, [[NH * NQ, 128], [1, NQ]]), qg[:, 0:NQ]), 16)
        ev("sync", [("v_proj", cnt["v_proj"])], "d_spk",
           lambda e, g=g: e.dma_start(
               bass.AP(kspill, g * T, [[NH * T, 128], [1, T]]), kg[:, 0:T]), 16)
        ev("sync", [("c_stv", cnt["c_stv"])], "d_spv",
           lambda e, g=g: e.dma_start(
               bass.AP(vtspill, g * T, [[NH * T, 128], [1, T]]), vtb[:, 0:T]), 16)

    pending_vpost = []

    def drain_vpost():
        while pending_vpost:
            pending_vpost.pop(0)()

    def run_job(fn):
        n0 = len(pending_vpost)
        fn()
        for _ in range(min(n0, len(pending_vpost))):
            pending_vpost.pop(0)()

    def v_job(g, vs):
        emit_proj_group(g, "v", vs % 2, vs * 512, 512, 16 + g, 2,
                        spill_wait=[("d_spv", 16 * g)])
        vp_mark = ("v_proj", cnt["v_proj"])
        pending_vpost.append(lambda g=g, vs=vs, vp_mark=vp_mark:
                             emit_v_post(g, vs, vp_mark))

    jobs = []
    for g in range(NH):
        if g + 1 < NH:
            jobs.append((0, lambda g=g: [emit_wa_dma(g + 1, wi, w_h)
                                         for wi, w_h in enumerate([wq_h, wk_h, wv_h])]))
        for i in range(8):
            if i < len(Q_PROJ_SLICES):
                s0, w = Q_PROJ_SLICES[i]
                jobs.append(((s0 + w - 1) // 128,
                             lambda g=g, s0=s0, w=w: emit_proj_group(
                                 g, "q", qg[:, s0:s0 + w], s0, w, 0 + g, 0,
                                 spill_wait=[("d_spq", 16 * g)])))
            jobs.append((4 * i + 3,
                         lambda g=g, ks=i: emit_proj_group(
                             g, "k", kg[:, ks * 512:(ks + 1) * 512], ks * 512, 512,
                             8 + g, 1, spill_wait=[("d_spk", 16 * g)])))
            jobs.append((4 * i + 3, lambda g=g, vs=i: v_job(g, vs)))
        jobs.append((31, lambda g=g: (drain_vpost(), emit_spills(g))))
    jobs.reverse()

    ln_norm_mark = {}
    ln_dst = {}

    def emit_tr_evac(t):
        dst = ln_dst[t]
        half = t % 2
        extra = [("d_cst", 48), ("s_xn", ln_norm_mark[t])]
        emit_tile_transposes(dst, half, identf[:], extra)
        xn_tr_mark[t] = cnt["t_stw"]
        ev("scalar", [("t_stw", cnt["t_stw"])], "c_stv",
           lambda e, t=t, half=half: e.activation(
               arena_ap(t * 128, [[4096, NC], [1, 128]]),
               ctxP[:, half * 1024: half * 1024 + 1024], AF.Copy))
        tr_mark[half] = cnt["c_stv"]
        tile_evac_mark[t] = cnt["c_stv"]

    def emit_ln_tile(t):
        par = t % 2
        if t < QT:
            src2 = x_res[:, t * 1024:(t + 1) * 1024]
            src_wait = [("d_x", 16)]
        else:
            par4 = t % 4
            src2 = xtmp[:, par4 * 1024:(par4 + 1) * 1024]
            src_wait = [("d_xt%d" % par4, 16 * ((t - 17) // 4 + 1))]
        dst = xn_sb[:, par * 1024:(par + 1) * 1024]
        emit_ln(src2, src_wait, dst, [])
        ln_norm_mark[t] = cnt["s_xn"]
        ln_dst[t] = dst

    def emit_wcast(ci):
        wi, q4 = ci // 4, ci % 4
        w_h2 = [wr2_h, wk2_h, wv2_h][wi]
        ev("gpsimd", [], "d_ws0",
           lambda e, wi=wi, q4=q4, w_h2=w_h2: e.dma_start(
               bass.AP(wspill, (wi * 1024 + q4 * 256) * D, [[D, 256], [1, D]]),
               bass.AP(w_h2, q4 * 256 * D, [[D, 256], [1, D]])), 16)

    emit_ln_tile(0)
    emit_ln_tile(1)
    for t in range(1, NT):
        emit_tr_evac(t - 1)
        if t + 1 < NT:
            emit_ln_tile(t + 1)
        if t >= 8 and t % 2 == 0 and (t - 8) // 2 < 12:
            emit_wcast((t - 8) // 2)
        done = 0
        while jobs and jobs[-1][0] <= t - 1 and done < 8:
            run_job(jobs.pop()[1])
            done += 1
    emit_tr_evac(NT - 1)
    while jobs:
        run_job(jobs.pop()[1])
    drain_vpost()

    all_proj = cnt["t_proj"]
    all_proj_v = cnt["v_proj"]
    mark_vt_all = cnt["c_stv"]

    KQ0, VQ0, QS0, EA0, UB0 = 0, 8192, 16384, 18432, 26624
    d1 = scratch[:, 0:1024]
    d2 = scratch[:, 1024:1536]
    den = scratch_f[:, 0:512]
    lbuf = scratch_f[:, 512:768]
    rrc = scratch[:, 1536:2560]

    def emit_warmup(n_mm, waits):
        def op(e):
            r = None
            for i in range(n_mm):
                r = e.matmul(ctxP[:, 0:128], ident[:], ident[:],
                             start=True, stop=True)
            return r
        ev("tensor", waits, None, op)

    emit_warmup(48, [("c_stv", mark_vt_all), ("d_cst", 48)])

    iters = []
    for sl, (s0, w) in enumerate(A_SLICES):
        for quad in range(8):
            for mcq in range(4):
                iters.append((sl, quad, mcq, s0, w))
    NIT = len(iters)

    m_stA, m_stB, m_expA, m_expB = {}, {}, {}, {}
    m_L3, m_ln, m_exp2, m_U, m_ctx = {}, {}, {}, {}, {}
    k_load_mark, v_load_mark, q_load_mark = {}, {}, {}
    last_vec = [None]
    last_sc = [None]

    def vec_op(waits, inc, op):
        wts = list(waits)
        if last_vec[0] is not None:
            wts.append(last_vec[0])
        ev("vector", wts, inc, op)
        last_vec[0] = (inc, cnt[inc])

    def sc_op(waits, inc, op):
        wts = list(waits)
        if last_sc[0] is not None:
            wts.append(last_sc[0])
        ev("scalar", wts, inc, op)
        last_sc[0] = (inc, cnt[inc])

    def stage_loads(i):
        sl, quad, mcq, s0, w = iters[i]
        if mcq != 0:
            return
        par_k = quad % 2
        if quad == 0:
            par_q = sl % 2
            qbase = QS0 if par_q == 0 else 30720
            qw = [("d_spq", 16 * NH)]
            if sl >= 2:
                qw.append(("t_st", m_stB[(sl - 1) * 32 - 1]))
            ev("gpsimd", qw, "d_q%d" % par_q,
               lambda e, s0=s0, w=w, qbase=qbase: e.dma_start(
                   arena_ap(qbase, [[w, NH], [1, w]]),
                   bass.AP(qspill, s0, [[NH * NQ, 128], [NQ, NH], [1, w]])), 16)
            q_load_mark[sl] = ("d_q%d" % par_q, 16 * (sl // 2 + 1))
        gi0 = sl * 32 + quad * 4
        kw = [("d_spk", 16 * NH)]
        vw = [("d_spv", 16 * NH)]
        if gi0 - 8 >= 0:
            kw.append(("t_st", m_stB[gi0 - 8 + 3]))
            vw.append(("t_ctx", m_ctx[gi0 - 8 + 3]))
        ev("sync", kw, "d_k%d" % par_k,
           lambda e, quad=quad, par_k=par_k: e.dma_start(
               arena_ap(KQ0 + par_k * 4096, [[512, NH], [1, 512]]),
               bass.AP(kspill, quad * 512, [[NH * T, 128], [T, NH], [1, 512]])), 16)
        ev("gpsimd", vw, "d_v%d" % par_k,
           lambda e, quad=quad, par_k=par_k: e.dma_start(
               arena_ap(VQ0 + par_k * 4096, [[512, NH], [1, 512]]),
               bass.AP(vtspill, quad * 512, [[NH * T, 128], [T, NH], [1, 512]])), 16)
        kq_count = (sl * 8 + quad) // 2 + 1
        k_load_mark[(sl, quad)] = ("d_k%d" % par_k, 16 * kq_count)
        v_load_mark[(sl, quad)] = ("d_v%d" % par_k, 16 * kq_count)

    def stage_st(i):
        sl, quad, mcq, s0, w = iters[i]
        par_k = quad % 2
        for hf in range(2):
            tw = [k_load_mark[(sl, quad)], q_load_mark[sl]]
            if i - 1 >= 0:
                tw.append(("s_exp", (m_expA if hf == 0 else m_expB)[i - 1]))
            if i == 0:
                tw.append(("v_proj", all_proj_v))

            qbase2 = QS0 if sl % 2 == 0 else 30720

            def op_st(e, w=w, par_k=par_k, mcq=mcq, hf=hf, qbase2=qbase2):
                r = None
                for g in range(hf * 4, hf * 4 + 4):
                    r = e.matmul(
                        stAB[:, g * w:(g + 1) * w],
                        arena_ap(KQ0 + par_k * 4096 + g * 512 + mcq * 128, [[1, 128]]),
                        arena_ap(qbase2 + g * w, [[1, w]]),
                        start=True, stop=True)
                return r
            ev("tensor", tw, "t_st", op_st)
            if hf == 0:
                m_stA[i] = cnt["t_st"]
            else:
                m_stB[i] = cnt["t_st"]

    def e_ap(par, off, dims):
        if par < 4:
            return arena_ap(EA0 + par * 2048 + off, dims)
        return bass.AP(scratch, 2560 + (par - 4) * 2048 + off,
                       [[scratch.shape[1], 128]] + dims)

    def stage_exp(i):
        sl, quad, mcq, s0, w = iters[i]
        par_e = i % 6
        sw = [("t_st", m_stA[i])]
        if i - 6 >= 0:
            sw.append(("v_du", m_U[i - 6]))
        if i in (4, 5):
            sw.append(("d_spv", 16 * NH))

        def op_expA(e, w=w, par_e=par_e):
            return e.activation(
                e_ap(par_e, 0, [[1, 4 * w]]),
                stAB[:, 0:4 * w], AF.Exp, scale=1.0 / 32.0)
        ev("scalar", sw, "s_exp", op_expA)
        m_expA[i] = cnt["s_exp"]

        def op_expB(e, w=w, par_e=par_e):
            return e.activation(
                e_ap(par_e, 4 * w, [[1, 4 * w]]),
                stAB[:, 4 * w:NH * w], AF.Exp, scale=1.0 / 32.0)
        ev("scalar", [("t_st", m_stB[i]), ("s_exp", m_expA[i])], "s_exp", op_expB)
        last_sc[0] = ("s_exp", cnt["s_exp"])
        m_expB[i] = cnt["s_exp"]

    def stage_den(i):
        sl, quad, mcq, s0, w = iters[i]
        par_e = i % 6
        par_d = i % 2

        def op_d1(e, w=w, par_e=par_e):
            return e.tensor_tensor(d1[:, 0:4 * w],
                                   e_ap(par_e, 0, [[1, 4 * w]]),
                                   e_ap(par_e, 4 * w, [[1, 4 * w]]),
                                   op=OP.add)
        vec_op([("s_exp", m_expB[i])], "v_self", op_d1)
        vec_op([], "v_self", lambda e, w=w: e.tensor_tensor(
            d2[:, 0:2 * w], d1[:, 0:2 * w], d1[:, 2 * w:4 * w], op=OP.add))
        w3 = [("s_self", m_ln[i - 2])] if i - 2 >= 0 else []
        vec_op(w3, "v_den", lambda e, w=w, par_d=par_d: e.tensor_tensor(
            den[:, par_d * 256: par_d * 256 + w], d2[:, 0:w], d2[:, w:2 * w], op=OP.add))
        m_L3[i] = cnt["v_den"]

    def stage_lnexp(i):
        sl, quad, mcq, s0, w = iters[i]
        par_d = i % 2
        par_r = i % 4
        lw = [("v_den", m_L3[i])]
        if i - 1 >= 0:
            lw.append(("s_rcp", m_exp2[i - 1]))
        ev("scalar", lw, "s_self",
           lambda e, w=w, par_d=par_d: e.activation(
               lbuf[:, 0:w], den[:, par_d * 256: par_d * 256 + w], AF.Ln))
        last_sc[0] = ("s_self", cnt["s_self"])
        m_ln[i] = cnt["s_self"]
        w2 = [("v_du", m_U[i - 4])] if i - 4 >= 0 else []
        sc_op(w2, "s_rcp",
              lambda e, w=w, par_r=par_r: e.activation(
                  rrc[:, par_r * 256: par_r * 256 + w], lbuf[:, 0:w], AF.Exp,
                  scale=-1.0))
        m_exp2[i] = cnt["s_rcp"]

    def stage_U(i):
        sl, quad, mcq, s0, w = iters[i]
        par_e = i % 6
        par_u = i % 2
        par_r = i % 4
        wts = [("s_rcp", m_exp2[i])]
        if i - 2 >= 0:
            wts.append(("t_ctx", m_ctx[i - 2]))
        vec_op(wts, "v_du",
               lambda e, w=w, par_u=par_u, par_e=par_e, par_r=par_r: e.tensor_tensor(
                   arena_ap(UB0 + par_u * 2048, [[1, NH * w]]),
                   e_ap(par_e, 0, [[1, NH * w]]),
                   bass.AP(scratch, 1536 + par_r * 256,
                           [[scratch.shape[1], 128], [0, NH], [1, w]]),
                   op=OP.mult))
        m_U[i] = cnt["v_du"]

    def stage_ctx(i):
        sl, quad, mcq, s0, w = iters[i]
        par_k = quad % 2
        par_u = i % 2
        nts = w // 128
        k_sl = quad * 4 + mcq
        tw = [("v_du", m_U[i]), v_load_mark[(sl, quad)]]
        if k_sl == 0:
            tw.append(("s_cpx", cnt["s_cpx"]))
            if sl == 0:
                tw.append(("c_stv", mark_vt_all))

        def op_ctx(e, w=w, par_u=par_u, par_k=par_k, mcq=mcq, nts=nts,
                   first=(k_sl == 0), last=(k_sl == 31)):
            r = None
            for g in range(NH):
                for nt in range(nts):
                    reg = g * nts + nt
                    start = first and (reg % 4 == 0)
                    stop = last and (reg % 4 == 3 or reg == NH * nts - 1)
                    r = e.matmul(
                        ctxP[:, reg * 128:(reg + 1) * 128],
                        arena_ap(UB0 + par_u * 2048 + g * w + nt * 128, [[1, 128]]),
                        arena_ap(VQ0 + par_k * 4096 + g * 512 + mcq * 128, [[1, 128]]),
                        start=start, stop=stop)
            return r
        ev("tensor", tw, "t_ctx", op_ctx)
        m_ctx[i] = cnt["t_ctx"]
        if k_sl == 31:
            cw = [("t_ctx", m_ctx[i])]
            if ctxs_free[0] is not None:
                cw.append(ctxs_free[0])
            sc_op(cw, "s_cpx",
                  lambda e, nts=nts: e.activation(
                      ctxs[:, 0:NH * nts * 128], ctxP[:, 0:NH * nts * 128], AF.Copy))
            cpx_mark = ("s_cpx", cnt["s_cpx"])
            ti0 = s0 // 128
            for j in range(4):
                if nts == 2:
                    src = bass.AP(ctxs, j * 2 * 256,
                                  [[ctxs.shape[1], 128], [256, 2], [128, 2], [1, 128]])
                    dst = bass.AP(x_res, ti0 * 1024 + j * 2 * 128,
                                  [[x_res.shape[1], 128], [128, 2], [1024, 2], [1, 128]])
                else:
                    src = bass.AP(ctxs, j * 2 * 128,
                                  [[ctxs.shape[1], 128], [128, 2], [1, 128]])
                    dst = bass.AP(x_res, ti0 * 1024 + j * 2 * 128,
                                  [[x_res.shape[1], 128], [128, 2], [1, 128]])
                wts = [cpx_mark] if j == 0 else []
                pending_evac.append((wts, j == 3,
                                     lambda e, dst=dst, src=src: e.tensor_tensor(
                                         dst, dst, src, op=OP.add)))
            for nt2 in range(nts):
                ti = s0 // 128 + nt2
                so = (ti % 4) * 18
                bn12 = stats[:, so:so + 12]
                src_t = x_res[:, ti * 1024:(ti + 1) * 1024]
                pending_fstats.append(([],
                                       lambda e, bn12=bn12, src_t=src_t: e.bn_stats(
                                           bn12[:, 0:6], src_t[:, 0:512])))
                pending_fstats.append(([],
                                       lambda e, bn12=bn12, src_t=src_t: e.bn_stats(
                                           bn12[:, 6:12], src_t[:, 512:1024])))
                pending_fstats.append(([],
                                       lambda e, ti=ti, bn12=bn12: e.bn_aggr(
                                           fstats[:, ti * 2:ti * 2 + 2], bn12)))

    pending_evac = []
    pending_fstats = []
    ctxs_free = [None]

    def drain_evac(n):
        for _ in range(min(n, len(pending_evac))):
            wts, is_last, op = pending_evac.pop(0)
            vec_op(wts, "v_ctx" if is_last else "v_self", op)
            if is_last:
                ctxs_free[0] = (last_vec[0][0], last_vec[0][1])

    def drain_fstats(n):
        if pending_evac:
            return
        for _ in range(min(n, len(pending_fstats))):
            wts, op = pending_fstats.pop(0)
            vec_op(wts, "v_self", op)

    for slot in range(NIT + 4):
        if slot < NIT:
            stage_loads(slot)
            stage_st(slot)
        if 0 <= slot - 2 < NIT:
            stage_lnexp(slot - 2)
        if slot < NIT:
            stage_exp(slot)
        if 0 <= slot - 1 < NIT:
            stage_den(slot - 1)
        if 0 <= slot - 3 < NIT:
            stage_U(slot - 3)
        drain_evac(1)
        drain_fstats(1)
        if 0 <= slot - 4 < NIT:
            stage_ctx(slot - 4)

    drain_evac(len(pending_evac))
    drain_fstats(len(pending_fstats))
    total_ctx = cnt["t_ctx"]
    total_vctx = cnt["v_ctx"]
    total_cpx = cnt["s_cpx"]
    for si in range(4):
        stsub_mark[si] = ("s_exp", cnt["s_exp"])

    total_st = cnt["t_st"]
    total_du = cnt["v_du"]
    for cj in range(4, NC):
        ev("sync", [("d_ws0", 16 * 12)], "d_tw",
           lambda e, cj=cj: e.dma_start_transpose(
               bass.AP(wkbuf, (cj - 4) * 1024, [[wkbuf.shape[1], 128], [1, 1024]]),
               bass.AP(wspill, 8 * 128 * D + cj * 128, [[D, 1024], [1, 128]])), 16)
    for cj in range(4):
        ev("sync", [("t_ctx", m_ctx[NIT - 5]), ("d_ws0", 16 * 12)], "d_tw",
           lambda e, cj=cj: e.dma_start_transpose(
               arena_ap(8192 + cj * 1024, [[1, 1024]]),
               bass.AP(wspill, 8 * 128 * D + cj * 128, [[D, 1024], [1, 128]])), 16)
    for wi, gate in [(0, ("t_ffn", 1)), (2, ("t_ffn", 9))]:
        for cj in range(NC):
            gw = [("t_ctx", total_ctx), ("v_du", total_du), ("t_st", total_st),
                  ("d_ws0", 16 * 12), gate]
            ev("sync", gw, "d_tw",
               lambda e, wi=wi, cj=cj: e.dma_start_transpose(
                   arena_ap(wi * 8192 + cj * 1024, [[1, 1024]]),
                   bass.AP(wspill, wi * 8 * 128 * D + cj * 128, [[D, 1024], [1, 128]])), 16)

    ev("scalar", [("v_self", cnt["v_self"])], "s_sd",
       lambda e: e.activation(stats[:, 0:QT],
                              bass.AP(fstats, 1, [[36, 128], [2, QT]]),
                              AF.Sqrt, bias=csts[:, 0:1]))
    ev("vector", [("s_sd", cnt["s_sd"])], "v_self",
       lambda e: e.reciprocal(bass.AP(fstats, 1, [[36, 128], [2, QT]]),
                              stats[:, 0:QT]))
    fstats_done = cnt["v_self"]

    kact = scratch[:, 8192:10240]
    rsb = arena[:, 24576:25600]
    prevbuf = arena[:, 25600:26624]
    relu_b = arena[:, 26624:27648]
    t1b = arena[:, 27648:28672]
    tsb = scratch_f[:, 1024:1536]

    ffn_mark = {}
    mix_mark = {}
    first_ffn_mm = [True]
    first_relu = [True]

    ln2_pend = {}

    ln2_par = [0]

    def ln2_chains(tc):
        tc0, wch = TCHUNKS[tc]
        ntiles = wch // 128
        t0 = tc0 // 128
        pend = []
        for tt in range(ntiles):
            ti = t0 + tt
            src2 = x_res[:, ti * 1024:(ti + 1) * 1024]
            par = ln2_par[0] % 2
            ln2_par[0] += 1
            dst = xn_sb[:, par * 1024:(par + 1) * 1024]
            n = 32 + ti
            nwaits = [("v_self", fstats_done), ("t_stw", xn_tr_mark[31])]
            if n - 2 in xn_tr_mark:
                nwaits.append(("t_stw", xn_tr_mark[n - 2]))
            ev("vector", nwaits, "s_xn",
               lambda e, dst=dst, src2=src2, ti=ti: e.tensor_scalar(
                   dst, src2, fstats[:, ti * 2:ti * 2 + 1],
                   fstats[:, ti * 2 + 1:ti * 2 + 2],
                   op0=OP.subtract, op1=OP.mult))
            pend.append((n, dst, cnt["s_xn"]))
        ln2_pend[tc] = pend

    def mix_part(tc, tiles=None):
        tc0, wch = TCHUNKS[tc]
        ntiles = wch // 128
        cp = tc % 2
        xr_sl = scratch[:, cp * 2048:(cp + 1) * 2048]
        xk_sl = scratch[:, 4096 + cp * 2048: 4096 + (cp + 1) * 2048]
        for tt in (range(ntiles) if tiles is None else tiles):
            n, dst, sxn_mark = ln2_pend[tc][tt]
            half = n % 2
            emit_tile_transposes(dst, half, identf[:],
                                 [("s_xn", sxn_mark), ("v_ctx", total_vctx)])
            xn_tr_mark[n] = cnt["t_stw"]
            cbase = half * 1024
            mw = [("t_stw", cnt["t_stw"]), ("v_mix", cnt["v_mix"])]
            if tc - 2 >= 0:
                mw.append(("t_ffn", ffn_mark[tc - 2]))
            ev_noinc("vector", mw, lambda e, cbase=cbase: e.tensor_copy(
                bass.AP(arena, 25600 + 1, [[A, 128], [128, NC], [1, 127]]),
                bass.AP(ctxP, cbase, [[2048, 128], [128, NC], [1, 127]])))
            ev("vector", [], "v_self", lambda e: e.tensor_copy(
                bass.AP(arena, 25600, [[A, 128], [128, NC]]), carry[:, 0:NC]))
            ev("vector", [("v_self", cnt["v_self"]), ("s_act", cnt["s_act"])], "v_self",
               lambda e, cbase=cbase: e.tensor_tensor(
                   t1b[:, 0:1024], ctxP[:, cbase:cbase + 1024],
                   prevbuf[:, 0:1024], op=OP.subtract))
            t1_chain = cnt["v_self"]
            ev("vector", [("v_self", t1_chain)], "c_stv", lambda e, cbase=cbase: e.tensor_copy(
                carry[:, 0:NC], bass.AP(ctxP, cbase + 127, [[2048, 128], [128, NC]])))
            tr_mark[half] = cnt["c_stv"]
            mixtmp = arena[:, 28672:29696]
            xr_off = cp * 2048 + tt * 128
            xk_off = 4096 + cp * 2048 + tt * 128
            ev("vector", [("v_self", t1_chain)], "v_self",
               lambda e: e.tensor_tensor(
                   mixtmp[:, 0:1024], t1b[:, 0:1024],
                   bass.AP(cvec, 24, [[40, 128], [1, 8], [0, 128]]),
                   op=OP.mult))
            ev("vector", [("v_self", cnt["v_self"])] , "v_self",
               lambda e, xr_off=xr_off: e.tensor_tensor(
                   bass.AP(scratch, xr_off, [[scratch.shape[1], 128], [256, NC], [1, 128]]),
                   mixtmp[:, 0:1024], prevbuf[:, 0:1024], op=OP.add))
            ev("vector", [("v_self", cnt["v_self"])], "v_self",
               lambda e: e.tensor_tensor(
                   mixtmp[:, 0:1024], t1b[:, 0:1024],
                   bass.AP(cvec, 32, [[40, 128], [1, 8], [0, 128]]),
                   op=OP.mult))
            ev("vector", [("v_self", cnt["v_self"])], "v_mix",
               lambda e, xk_off=xk_off: e.tensor_tensor(
                   bass.AP(scratch, xk_off, [[scratch.shape[1], 128], [256, NC], [1, 128]]),
                   mixtmp[:, 0:1024], prevbuf[:, 0:1024], op=OP.add))
        mix_mark[tc] = cnt["v_mix"]

    def ffn_matmuls(tc, after_k=None, after_tile=None):
        tc0, wch = TCHUNKS[tc]
        ntiles = wch // 128
        t0 = tc0 // 128
        cp = tc % 2
        xr_sl = scratch[:, cp * 2048:(cp + 1) * 2048]
        xk_sl = scratch[:, 4096 + cp * 2048: 4096 + (cp + 1) * 2048]
        for dj in range(NC):
            si = next_stsub()
            ps = stsub[si]
            tw = [("v_mix", mix_mark[tc]), (stsub_mark[si][0], stsub_mark[si][1])]
            if first_ffn_mm[0]:
                tw.append(("d_tw", 16 * 8))
                first_ffn_mm[0] = False

            def op_k(e, ps=ps, dj=dj, wch=wch, xk_sl=xk_sl):
                r = None
                for cj in range(NC):
                    if cj < 4:
                        wsrc2 = arena_ap((8 + cj) * 1024 + dj * 128, [[1, 128]])
                    else:
                        wsrc2 = bass.AP(wkbuf, (cj - 4) * 1024 + dj * 128,
                                        [[wkbuf.shape[1], 128], [1, 128]])
                    r = e.matmul(ps[:, 0:wch], wsrc2,
                                 xk_sl[:, cj * 256: cj * 256 + wch],
                                 start=(cj == 0), stop=(cj == NC - 1))
                return r
            ev("tensor", tw, "t_ffn", op_k)
            sw = [("t_ffn", cnt["t_ffn"]), ("v_sq", max(0, cnt["v_sq"] - 1))]
            if first_relu[0]:
                sw.append(("t_ctx", total_ctx))
                first_relu[0] = False
            pj = dj % 2
            ev("scalar", sw, "s_act",
               lambda e, ps=ps, wch=wch, pj=pj: e.activation(
                   relu_b[:, pj * 512: pj * 512 + wch], ps[:, 0:wch], AF.Relu))
            stsub_mark[si] = ("s_act", cnt["s_act"])
            ev("vector", [("s_act", cnt["s_act"])], "v_sq",
               lambda e, pj=pj, dj=dj, wch=wch: e.tensor_tensor(
                   kact[:, dj * 256: dj * 256 + wch],
                   relu_b[:, pj * 512: pj * 512 + wch],
                   relu_b[:, pj * 512: pj * 512 + wch], op=OP.mult))
        ksq_done = cnt["v_sq"]
        if after_k is not None:
            after_k()

        for tt in range(ntiles):
            if after_tile is not None and tt > 0:
                after_tile(tt - 1)
            ti = t0 + tt
            fin_before = cnt["v_fin"]
            for ds in range(2):
                si = next_stsub()
                ps = stsub[si]
                tw = [(stsub_mark[si][0], stsub_mark[si][1])]
                if tt == 0 and ds == 0:
                    tw.append(("d_tw", 16 * 16))

                def op_r(e, ps=ps, tt=tt, ds=ds, xr_sl=xr_sl):
                    r = None
                    for cj in range(NC):
                        r = e.matmul(ps[:, 0:512],
                                     xr_sl[:, cj * 256 + tt * 128: cj * 256 + (tt + 1) * 128],
                                     arena_ap(cj * 1024 + ds * 512, [[1, 512]]),
                                     start=(cj == 0), stop=(cj == NC - 1))
                    return r
                ev("tensor", tw, "t_ffn", op_r)
                ev("scalar", [("t_ffn", cnt["t_ffn"]), ("v_fin", fin_before)], "s_act",
                   lambda e, ps=ps, ds=ds: e.activation(
                       rsb[:, ds * 512:(ds + 1) * 512], ps[:, 0:512], AF.Sigmoid))
                stsub_mark[si] = ("s_act", cnt["s_act"])
            sig_done = cnt["s_act"]
            for ds in range(2):
                si = next_stsub()
                ps = stsub[si]
                tw = [("v_sq", ksq_done), (stsub_mark[si][0], stsub_mark[si][1])]
                if tt == 0 and ds == 0:
                    tw.append(("d_tw", 16 * 24))

                def op_v(e, ps=ps, tt=tt, ds=ds):
                    r = None
                    for cj in range(NC):
                        r = e.matmul(ps[:, 0:512],
                                     kact[:, cj * 256 + tt * 128: cj * 256 + (tt + 1) * 128],
                                     arena_ap((16 + cj) * 1024 + ds * 512, [[1, 512]]),
                                     start=(cj == 0), stop=(cj == NC - 1))
                    return r
                ev("tensor", tw, "t_ffn", op_v)
                ev("vector", [("t_ffn", cnt["t_ffn"]), ("s_act", sig_done),
                              ("v_self", cnt["v_self"])], "v_self",
                   lambda e, ps=ps, ds=ds: e.tensor_tensor(
                       tsb[:, 0:512], ps[:, 0:512],
                       rsb[:, ds * 512:(ds + 1) * 512], op=OP.mult))
                dst = x_res[:, ti * 1024 + ds * 512: ti * 1024 + ds * 512 + 512]
                is_last = (ds == 1)
                ev("vector", [("v_self", cnt["v_self"])],
                   "v_fin" if is_last else "v_self",
                   lambda e, dst=dst: e.tensor_tensor(dst, dst, tsb[:, 0:512], op=OP.add))
                stsub_mark[si] = ("v_fin", cnt["v_fin"] if is_last else cnt["v_fin"] + 1)
        ffn_mark[tc] = cnt["t_ffn"]

        ev("sync", [("v_fin", cnt["v_fin"])], "d_out",
           lambda e, t0=t0, ntiles=ntiles: e.dma_start(
               bass.AP(out_h, t0 * 128 * 1024,
                       [[1024, 128], [128 * 1024, ntiles], [1, 1024]]),
               ap3(x_res, t0 * 1024, [[1024, ntiles], [1, 1024]])), 16)

    emit_warmup(48, [("s_cpx", total_cpx)])
    ln2_chains(0)
    mix_part(0)
    for tc in range(len(TCHUNKS)):
        if tc + 1 < len(TCHUNKS):
            nt_next = TCHUNKS[tc + 1][1] // 128
            ln2_chains(tc + 1)
            ffn_matmuls(tc,
                        after_k=lambda tc=tc: mix_part(tc + 1, [0]),
                        after_tile=lambda tt, tc=tc, n=nt_next:
                            mix_part(tc + 1, [tt + 1]) if tt + 1 < n else None)
            if nt_next > TCHUNKS[tc][1] // 128:
                mix_part(tc + 1, list(range(TCHUNKS[tc][1] // 128, nt_next)))
        else:
            ffn_matmuls(tc)

    ev_noinc("sync", [("d_out", cnt["d_out"])], lambda e: e.nop())

    with nc.Block() as block:
        @block.sync
        def _(e):
            for fn in sched["sync"]:
                fn(e)

        @block.gpsimd
        def _(e):
            for fn in sched["gpsimd"]:
                fn(e)

        @block.tensor
        def _(e):
            for fn in sched["tensor"]:
                fn(e)

        @block.scalar
        def _(e):
            for fn in sched["scalar"]:
                fn(e)

        @block.vector
        def _(e):
            for fn in sched["vector"]:
                fn(e)

    es.close()
    return nc


def _in_maps(inputs):
    x = np.asarray(inputs["L0_attn_hidden_states"], dtype=np.float32)
    cvec = np.empty((128, 40), dtype=np.float32)
    for i, k in enumerate(["L0_attn_bq", "L0_attn_bk", "L0_attn_bv",
                           "L0_ffn_mu_r", "L0_ffn_mu_k"]):
        cvec[:, i * 8:(i + 1) * 8] = np.asarray(inputs[k], np.float32).reshape(8, 128).T
    ident = np.eye(128, dtype=ml_dtypes.bfloat16)
    identf = np.eye(128, dtype=np.float32)
    base = {
        "wq": np.asarray(inputs["L0_attn_Wq"], np.float32),
        "wk": np.asarray(inputs["L0_attn_Wk"], np.float32),
        "wv": np.asarray(inputs["L0_attn_Wv"], np.float32),
        "wr2": np.asarray(inputs["L0_ffn_W_r"], np.float32),
        "wk2": np.asarray(inputs["L0_ffn_W_k"], np.float32),
        "wv2": np.asarray(inputs["L0_ffn_W_v"], np.float32),
        "cvec": cvec, "ident": ident, "identf": identf,
    }
    maps = []
    for c in range(8):
        bi, half = c // 2, c % 2
        xb = x[bi] if half == 0 else np.roll(x[bi], -1920, axis=0)
        m = dict(base)
        m["x"] = np.ascontiguousarray(xb)
        maps.append(m)
    return maps


def kernel(**inputs):
    maps = _in_maps(inputs)
    nc = build_nc()
    res = run_bass_kernel_spmd(nc, maps, core_ids=list(range(8)))
    out = np.empty((4, 4096, 1024), dtype=np.float32)
    for c in range(8):
        bi, half = c // 2, c % 2
        o = np.asarray(res.results[c]["out"])
        if half == 0:
            out[bi, 0:2048] = o[0:2048]
        else:
            out[bi, 2048:4096] = o[128:2176]
    return out


# version 42
# speedup vs baseline: 1.0009x; 1.0009x over previous
import numpy as np
import ml_dtypes
from contextlib import ExitStack

import concourse.bass as bass
import concourse.mybir as mybir
from concourse.bass_utils import run_bass_kernel_spmd

F32 = mybir.dt.float32
BF16 = mybir.dt.bfloat16
AF = mybir.ActivationFunctionType
OP = mybir.AluOpType

D = 1024
NC = 8
NH = 8
T = 4096
NT = 32
QT = 17
NQ = QT * 128
EPS = 1e-5
A_SLICES = [(i * 256, 256) for i in range(8)] + [(2048, 128)]
Q_PROJ_SLICES = [(0, 512), (512, 512), (1024, 512), (1536, 512), (2048, 128)]
TCHUNKS = [(i * 256, 256) for i in range(8)] + [(2048, 128)]

SEM_NAMES = ("d_x", "d_cst", "d_xt0", "d_xt1", "d_xt2", "d_xt3", "d_wa", "d_spk", "d_spv", "d_spq",
             "d_q0", "d_q1", "d_k0", "d_k1", "d_v0", "d_v1", "d_wf0", "d_wf1",
             "d_out", "v_ln", "s_sd", "s_xn", "t_stw", "c_stv", "s_exp", "v_du",
             "t_proj", "v_proj", "t_st", "t_ctx", "v_ctx", "s_cpx", "t_ffn", "s_act",
             "v_sq", "v_mix", "v_fin", "v_self", "v_den", "s_rcp", "s_self", "g_rcp",
             "s_cpy", "d_ws0", "d_tw")


def build_nc():
    nc = bass.Bass("TRN2", target_bir_lowering=False)

    x_h = nc.dram_tensor("x", [T, D], F32, kind="ExternalInput")
    wq_h = nc.dram_tensor("wq", [D, D], F32, kind="ExternalInput")
    wk_h = nc.dram_tensor("wk", [D, D], F32, kind="ExternalInput")
    wv_h = nc.dram_tensor("wv", [D, D], F32, kind="ExternalInput")
    wr2_h = nc.dram_tensor("wr2", [D, D], F32, kind="ExternalInput")
    wk2_h = nc.dram_tensor("wk2", [D, D], F32, kind="ExternalInput")
    wv2_h = nc.dram_tensor("wv2", [D, D], F32, kind="ExternalInput")
    cvec_h = nc.dram_tensor("cvec", [128, 40], F32, kind="ExternalInput")
    ident_h = nc.dram_tensor("ident", [128, 128], BF16, kind="ExternalInput")
    identf_h = nc.dram_tensor("identf", [128, 128], F32, kind="ExternalInput")
    out_h = nc.dram_tensor("out", [NQ, D], F32, kind="ExternalOutput")
    wspill = nc.dram_tensor("wspill", [3 * D, D], BF16)
    kspill = nc.dram_tensor("kspill", [128, NH * T], BF16)
    vtspill = nc.dram_tensor("vtspill", [128, NH * T], BF16)
    qspill = nc.dram_tensor("qspill", [128, NH * NQ], BF16)

    es = ExitStack()

    def sbuf(name, cols, dt):
        return es.enter_context(nc.sbuf_tensor(name, [128, cols], dt))

    def psum(name, cols, dt):
        return es.enter_context(nc.psum_tensor(name, [128, cols], dt))

    x_res = sbuf("x_res", QT * 1024, F32)
    xtmp = sbuf("xtmp", 4 * 1024, BF16)
    arena = sbuf("arena", 32768, BF16)
    xn_sb = sbuf("xn_sb", 2 * 1024, F32)
    scratch = sbuf("scratch", 10368, BF16)
    scratch_f = sbuf("scratch_f", 1536, F32)
    ctxs = sbuf("ctxs", 2048, F32)
    wkbuf = sbuf("wkbuf", 4096, BF16)
    wa = sbuf("wa", 6144, BF16)
    cvec = sbuf("cvecsb", 40, F32)
    ident = sbuf("identsb", 128, BF16)
    identf = sbuf("identfsb", 128, F32)
    stats = sbuf("stats", 72, F32)
    csts = sbuf("csts", 2, F32)
    fstats = sbuf("fstats", 36, F32)
    carry = sbuf("carry", 8, BF16)

    stAB = psum("stAB", 2048, F32)
    ctxP = psum("ctxP", 2048, F32)

    sems = {n: es.enter_context(nc.semaphore(n)) for n in SEM_NAMES}
    sched = {"tensor": [], "vector": [], "scalar": [], "sync": [], "gpsimd": []}
    cnt = {n: 0 for n in SEM_NAMES}

    def ev(eng, waits, inc, op, amt=1):
        waits = [(sems[n], v) for (n, v) in waits if v > 0]
        incsem = sems[inc] if inc else None

        def fn(e, waits=tuple(waits), incsem=incsem, op=op, amt=amt):
            for s, v in waits:
                e.wait_ge(s, v)
            r = op(e)
            if incsem is not None:
                r.then_inc(incsem, amt)
        sched[eng].append(fn)
        if inc:
            cnt[inc] += amt

    def ev_noinc(eng, waits, op):
        ev(eng, waits, None, op)

    def drain(eng):
        sched[eng].append(lambda e: e.drain())

    def ap3(t_sb, offset, dims):
        return bass.AP(t_sb, offset, [[t_sb.shape[1], 128]] + dims)

    A = arena.shape[1]

    def arena_ap(offset, dims):
        return bass.AP(arena, offset, [[A, 128]] + dims)

    stsub = [stAB[:, i * 512:(i + 1) * 512] for i in range(4)]
    stsub_mark = [("d_cst", 0)] * 4
    stsub_rot = [0]

    def next_stsub():
        i = stsub_rot[0] % 4
        stsub_rot[0] += 1
        return i

    tr_mark = [0, 0]
    xn_tr_mark = {}
    ln_done = [0]

    ev("sync", [], "d_cst", lambda e: e.dma_start(cvec[:], cvec_h.ap()), 16)
    ev("sync", [], "d_cst", lambda e: e.dma_start(ident[:], ident_h.ap()), 16)
    ev("sync", [], "d_cst", lambda e: e.dma_start(identf[:], identf_h.ap()), 16)
    x_tiled = x_h.ap().rearrange("(t p) c -> p t c", p=128)
    ev("sync", [], "d_x",
       lambda e: e.dma_start(ap3(x_res, 0, [[1024, QT], [1, 1024]]), x_tiled[:, 0:QT, :]), 16)
    ev_noinc("vector", [], lambda e: e.memset(carry[:, 0:NC], 0.0))
    ev_noinc("vector", [], lambda e: e.memset(csts[:, 0:1], EPS))
    ev_noinc("vector", [], lambda e: e.memset(csts[:, 1:2], -1.0))

    def op_warm0(e):
        r = None
        for _ in range(48):
            r = e.matmul(ctxP[:, 0:128], ident[:], ident[:], start=True, stop=True)
        return r
    ev("tensor", [("d_cst", 48)], None, op_warm0)

    def emit_xtmp_dma(t):
        par = t % 4
        waits = [("s_xn", t - 3)] if t - 4 >= QT else []
        ev("gpsimd", waits, "d_xt%d" % par,
           lambda e, par=par, t=t: e.dma_start(
               xtmp[:, par * 1024:(par + 1) * 1024], x_tiled[:, t, :]), 16)

    def emit_wa_dma(g, wi, w_h):
        src = w_h.ap().rearrange("(cj p) d -> p cj d", p=128)[:, :, g * 128:(g + 1) * 128]
        if g >= 2:
            waits = [("t_proj", 21 * (g - 1) + 1)]
        elif g == 1:
            waits = [("t_proj", 1)]
        elif wi > 0:
            waits = [("d_x", 16)]
        else:
            waits = []
        ev("gpsimd", waits, "d_wa",
           lambda e, g=g, wi=wi, src=src: e.dma_start(
               bass.AP(wa, (g % 2) * 3072 + wi * 1024, [[wa.shape[1], 128], [128, NC], [1, 128]]),
               src), 16)

    emit_xtmp_dma(17)
    emit_xtmp_dma(18)
    for wi, w_h in enumerate([wq_h, wk_h, wv_h]):
        emit_wa_dma(0, wi, w_h)
    for t in range(19, NT):
        emit_xtmp_dma(t)

    def emit_ln(src_ap, src_wait, dst_ap, extra_norm_wait):
        n = ln_done[0]
        par = n % 4
        so = par * 18
        bn12 = stats[:, so:so + 12]
        mv = stats[:, so + 12:so + 14]
        sd = stats[:, so + 14:so + 15]
        inv = stats[:, so + 15:so + 16]
        waits = list(src_wait)
        if n - 4 >= 0:
            waits.append(("s_sd", n - 3))
        ev_noinc("vector", waits, lambda e: e.bn_stats(bn12[:, 0:6], src_ap[:, 0:512]))
        ev("vector", [], "v_self", lambda e: e.bn_stats(bn12[:, 6:12], src_ap[:, 512:1024]))
        ev("vector", [("v_self", cnt["v_self"])], "v_ln", lambda e: e.bn_aggr(mv, bn12))
        ev("scalar", [("v_ln", cnt["v_ln"])], "s_sd",
           lambda e: e.activation(sd, mv[:, 1:2], AF.Sqrt, bias=csts[:, 0:1]))
        ev("vector", [("s_sd", cnt["s_sd"])], "v_self", lambda e: e.reciprocal(inv, sd))
        nwaits = list(extra_norm_wait) + [("v_self", cnt["v_self"])]
        if n - 2 in xn_tr_mark:
            nwaits.append(("t_stw", xn_tr_mark[n - 2]))
        ev("vector", nwaits, "s_xn",
           lambda e: e.tensor_scalar(dst_ap, src_ap, mv[:, 0:1], inv,
                                     op0=OP.subtract, op1=OP.mult))
        ln_done[0] += 1
        return n

    def emit_tile_transposes(src_ap, half, idf, extra_waits=()):
        tw = [("c_stv", tr_mark[half])] + list(extra_waits)

        def op(e, src_ap=src_ap, half=half, idf=idf):
            r = None
            for j in range(NC):
                r = e.transpose(ctxP[:, half * 1024 + j * 128: half * 1024 + (j + 1) * 128],
                                src_ap[:, j * 128:(j + 1) * 128], idf)
            return r
        ev("tensor", tw, "t_stw", op)

    kg = scratch[:, 0:4096]
    vtb = scratch[:, 4096:8192]
    qg = scratch[:, 8192:10368]
    vsb = scratch_f[:, 0:1024]
    tile_evac_mark = {}
    vsb_tr_mark = [0, 0]

    def emit_proj_group(g, kind, dst_ap, s0, w, bias_col, wi, spill_wait=()):
        si = next_stsub()
        ps = stsub[si]
        req = (s0 + w - 1) // 128
        tw = [("d_wa", 48 * (g + 1)),
              (stsub_mark[si][0], stsub_mark[si][1]),
              ("c_stv", tile_evac_mark[req])]

        def op(e, ps=ps, s0=s0, w=w, wi=wi, g=g):
            r = None
            for cj in range(NC):
                r = e.matmul(ps[:, 0:w],
                             wa[:, (g % 2) * 3072 + wi * 1024 + cj * 128:
                                 (g % 2) * 3072 + wi * 1024 + (cj + 1) * 128],
                             arena_ap(cj * 4096 + s0, [[1, w]]),
                             start=(cj == 0), stop=(cj == NC - 1))
            return r
        ev("tensor", tw, "t_proj", op)
        vw = [("t_proj", cnt["t_proj"])] + list(spill_wait)
        if kind == "v":
            par2 = dst_ap
            vw.append(("t_stw", vsb_tr_mark[par2]))
            dst = vsb[:, par2 * 512:(par2 + 1) * 512]

            def opv(e, dst=dst, ps=ps, w=w, bias_col=bias_col):
                return e.tensor_scalar(dst, ps[:, 0:w],
                                       cvec[:, bias_col:bias_col + 1],
                                       None, op0=OP.add)
            ev("vector", vw + [("d_cst", 48)], "v_proj", opv)
            stsub_mark[si] = ("v_proj", cnt["v_proj"])
            return
        dst = dst_ap

        def opc(e, dst=dst, ps=ps, w=w):
            return e.activation(dst, ps[:, 0:w], AF.Copy)
        ev("scalar", vw, "s_cpy", opc)
        stsub_mark[si] = ("s_cpy", cnt["s_cpy"])
        ev("vector", [("s_cpy", cnt["s_cpy"]), ("d_cst", 48)], "v_proj",
           lambda e, dst=dst, bias_col=bias_col: e.tensor_scalar(
               dst, dst, cvec[:, bias_col:bias_col + 1], None, op0=OP.add))

    def emit_v_post(g, vs, vp_mark=None):
        par2 = vs % 2
        half = vs % 2
        tw = [vp_mark or ("v_proj", cnt["v_proj"]), ("c_stv", tr_mark[half])]

        def op_trv(e, par2=par2, half=half):
            r = None
            for j in range(4):
                r = e.transpose(
                    ctxP[:, half * 1024 + j * 128: half * 1024 + (j + 1) * 128],
                    vsb[:, par2 * 512 + j * 128: par2 * 512 + (j + 1) * 128],
                    identf[:])
            return r
        ev("tensor", tw, "t_stw", op_trv)
        vsb_tr_mark[par2] = cnt["t_stw"]
        ev("scalar", [("t_stw", cnt["t_stw"]), ("d_spv", 16 * g)], "c_stv",
           lambda e, vs=vs, half=half: e.activation(
               vtb[:, vs * 512:(vs + 1) * 512],
               ctxP[:, half * 1024: half * 1024 + 512], AF.Copy))
        tr_mark[half] = cnt["c_stv"]

    def emit_spills(g):
        ev("sync", [("v_proj", cnt["v_proj"])], "d_spq",
           lambda e, g=g: e.dma_start(
               bass.AP(qspill, g * NQ, [[NH * NQ, 128], [1, NQ]]), qg[:, 0:NQ]), 16)
        ev("sync", [("v_proj", cnt["v_proj"])], "d_spk",
           lambda e, g=g: e.dma_start(
               bass.AP(kspill, g * T, [[NH * T, 128], [1, T]]), kg[:, 0:T]), 16)
        ev("sync", [("c_stv", cnt["c_stv"])], "d_spv",
           lambda e, g=g: e.dma_start(
               bass.AP(vtspill, g * T, [[NH * T, 128], [1, T]]), vtb[:, 0:T]), 16)

    pending_vpost = []

    def drain_vpost():
        while pending_vpost:
            pending_vpost.pop(0)()

    def run_job(fn):
        n0 = len(pending_vpost)
        fn()
        for _ in range(min(n0, len(pending_vpost))):
            pending_vpost.pop(0)()

    def v_job(g, vs):
        emit_proj_group(g, "v", vs % 2, vs * 512, 512, 16 + g, 2,
                        spill_wait=[("d_spv", 16 * g)])
        vp_mark = ("v_proj", cnt["v_proj"])
        pending_vpost.append(lambda g=g, vs=vs, vp_mark=vp_mark:
                             emit_v_post(g, vs, vp_mark))

    jobs = []
    for g in range(NH):
        if g + 1 < NH:
            jobs.append((0, lambda g=g: [emit_wa_dma(g + 1, wi, w_h)
                                         for wi, w_h in enumerate([wq_h, wk_h, wv_h])]))
        for i in range(8):
            if i < len(Q_PROJ_SLICES):
                s0, w = Q_PROJ_SLICES[i]
                jobs.append(((s0 + w - 1) // 128,
                             lambda g=g, s0=s0, w=w: emit_proj_group(
                                 g, "q", qg[:, s0:s0 + w], s0, w, 0 + g, 0,
                                 spill_wait=[("d_spq", 16 * g)])))
            jobs.append((4 * i + 3,
                         lambda g=g, ks=i: emit_proj_group(
                             g, "k", kg[:, ks * 512:(ks + 1) * 512], ks * 512, 512,
                             8 + g, 1, spill_wait=[("d_spk", 16 * g)])))
            jobs.append((4 * i + 3, lambda g=g, vs=i: v_job(g, vs)))
        jobs.append((31, lambda g=g: (drain_vpost(), emit_spills(g))))
    jobs.reverse()

    ln_norm_mark = {}
    ln_dst = {}

    def emit_tr_evac(t):
        dst = ln_dst[t]
        half = t % 2
        extra = [("d_cst", 48), ("s_xn", ln_norm_mark[t])]
        emit_tile_transposes(dst, half, identf[:], extra)
        xn_tr_mark[t] = cnt["t_stw"]
        ev("scalar", [("t_stw", cnt["t_stw"])], "c_stv",
           lambda e, t=t, half=half: e.activation(
               arena_ap(t * 128, [[4096, NC], [1, 128]]),
               ctxP[:, half * 1024: half * 1024 + 1024], AF.Copy))
        tr_mark[half] = cnt["c_stv"]
        tile_evac_mark[t] = cnt["c_stv"]

    def emit_ln_tile(t):
        par = t % 2
        if t < QT:
            src2 = x_res[:, t * 1024:(t + 1) * 1024]
            src_wait = [("d_x", 16)]
        else:
            par4 = t % 4
            src2 = xtmp[:, par4 * 1024:(par4 + 1) * 1024]
            src_wait = [("d_xt%d" % par4, 16 * ((t - 17) // 4 + 1))]
        dst = xn_sb[:, par * 1024:(par + 1) * 1024]
        emit_ln(src2, src_wait, dst, [])
        ln_norm_mark[t] = cnt["s_xn"]
        ln_dst[t] = dst

    def emit_wcast(ci):
        wi, q4 = ci // 4, ci % 4
        w_h2 = [wr2_h, wk2_h, wv2_h][wi]
        ev("gpsimd", [], "d_ws0",
           lambda e, wi=wi, q4=q4, w_h2=w_h2: e.dma_start(
               bass.AP(wspill, (wi * 1024 + q4 * 256) * D, [[D, 256], [1, D]]),
               bass.AP(w_h2, q4 * 256 * D, [[D, 256], [1, D]])), 16)

    emit_ln_tile(0)
    emit_ln_tile(1)
    for t in range(1, NT):
        emit_tr_evac(t - 1)
        if t + 1 < NT:
            emit_ln_tile(t + 1)
        if t >= 8 and t % 2 == 0 and (t - 8) // 2 < 12:
            emit_wcast((t - 8) // 2)
        done = 0
        while jobs and jobs[-1][0] <= t - 1 and done < 8:
            run_job(jobs.pop()[1])
            done += 1
    emit_tr_evac(NT - 1)
    while jobs:
        run_job(jobs.pop()[1])
    drain_vpost()

    all_proj = cnt["t_proj"]
    all_proj_v = cnt["v_proj"]
    mark_vt_all = cnt["c_stv"]

    KQ0, VQ0, QS0, EA0, UB0 = 0, 8192, 16384, 18432, 26624
    d1 = scratch[:, 0:1024]
    d2 = scratch[:, 1024:1536]
    den = scratch_f[:, 0:512]
    lbuf = scratch_f[:, 512:768]
    rrc = scratch[:, 1536:2560]

    def emit_warmup(n_mm, waits):
        def op(e):
            r = None
            for i in range(n_mm):
                r = e.matmul(ctxP[:, 0:128], ident[:], ident[:],
                             start=True, stop=True)
            return r
        ev("tensor", waits, None, op)

    emit_warmup(48, [("c_stv", mark_vt_all), ("d_cst", 48)])

    iters = []
    for sl, (s0, w) in enumerate(A_SLICES):
        for quad in range(8):
            for mcq in range(4):
                iters.append((sl, quad, mcq, s0, w))
    NIT = len(iters)

    m_stA, m_stB, m_expA, m_expB = {}, {}, {}, {}
    m_L3, m_ln, m_exp2, m_U, m_ctx = {}, {}, {}, {}, {}
    k_load_mark, v_load_mark, q_load_mark = {}, {}, {}
    last_vec = [None]
    last_sc = [None]

    def vec_op(waits, inc, op):
        wts = list(waits)
        if last_vec[0] is not None:
            wts.append(last_vec[0])
        ev("vector", wts, inc, op)
        last_vec[0] = (inc, cnt[inc])

    def sc_op(waits, inc, op):
        wts = list(waits)
        if last_sc[0] is not None:
            wts.append(last_sc[0])
        ev("scalar", wts, inc, op)
        last_sc[0] = (inc, cnt[inc])

    def stage_loads(i):
        sl, quad, mcq, s0, w = iters[i]
        if mcq != 0:
            return
        par_k = quad % 2
        if quad == 0:
            par_q = sl % 2
            qbase = QS0 if par_q == 0 else 30720
            qw = [("d_spq", 16 * NH)]
            if sl >= 2:
                qw.append(("t_st", m_stB[(sl - 1) * 32 - 1]))
            ev("gpsimd", qw, "d_q%d" % par_q,
               lambda e, s0=s0, w=w, qbase=qbase: e.dma_start(
                   arena_ap(qbase, [[w, NH], [1, w]]),
                   bass.AP(qspill, s0, [[NH * NQ, 128], [NQ, NH], [1, w]])), 16)
            q_load_mark[sl] = ("d_q%d" % par_q, 16 * (sl // 2 + 1))
        gi0 = sl * 32 + quad * 4
        kw = [("d_spk", 16 * NH)]
        vw = [("d_spv", 16 * NH)]
        if gi0 - 8 >= 0:
            kw.append(("t_st", m_stB[gi0 - 8 + 3]))
            vw.append(("t_ctx", m_ctx[gi0 - 8 + 3]))
        ev("sync", kw, "d_k%d" % par_k,
           lambda e, quad=quad, par_k=par_k: e.dma_start(
               arena_ap(KQ0 + par_k * 4096, [[512, NH], [1, 512]]),
               bass.AP(kspill, quad * 512, [[NH * T, 128], [T, NH], [1, 512]])), 16)
        ev("gpsimd", vw, "d_v%d" % par_k,
           lambda e, quad=quad, par_k=par_k: e.dma_start(
               arena_ap(VQ0 + par_k * 4096, [[512, NH], [1, 512]]),
               bass.AP(vtspill, quad * 512, [[NH * T, 128], [T, NH], [1, 512]])), 16)
        kq_count = (sl * 8 + quad) // 2 + 1
        k_load_mark[(sl, quad)] = ("d_k%d" % par_k, 16 * kq_count)
        v_load_mark[(sl, quad)] = ("d_v%d" % par_k, 16 * kq_count)

    def stage_st(i):
        sl, quad, mcq, s0, w = iters[i]
        par_k = quad % 2
        for hf in range(2):
            tw = [k_load_mark[(sl, quad)], q_load_mark[sl]]
            if i - 1 >= 0:
                tw.append(("s_exp", (m_expA if hf == 0 else m_expB)[i - 1]))
            if i == 0:
                tw.append(("v_proj", all_proj_v))

            qbase2 = QS0 if sl % 2 == 0 else 30720

            def op_st(e, w=w, par_k=par_k, mcq=mcq, hf=hf, qbase2=qbase2):
                r = None
                for g in range(hf * 4, hf * 4 + 4):
                    r = e.matmul(
                        stAB[:, g * w:(g + 1) * w],
                        arena_ap(KQ0 + par_k * 4096 + g * 512 + mcq * 128, [[1, 128]]),
                        arena_ap(qbase2 + g * w, [[1, w]]),
                        start=True, stop=True)
                return r
            ev("tensor", tw, "t_st", op_st)
            if hf == 0:
                m_stA[i] = cnt["t_st"]
            else:
                m_stB[i] = cnt["t_st"]

    def e_ap(par, off, dims):
        if par < 4:
            return arena_ap(EA0 + par * 2048 + off, dims)
        return bass.AP(scratch, 2560 + (par - 4) * 2048 + off,
                       [[scratch.shape[1], 128]] + dims)

    def stage_expA(i):
        sl, quad, mcq, s0, w = iters[i]
        par_e = i % 6
        sw = [("t_st", m_stA[i])]
        if i - 6 >= 0:
            sw.append(("v_du", m_U[i - 6]))
        if i in (4, 5):
            sw.append(("d_spv", 16 * NH))

        def op_expA(e, w=w, par_e=par_e):
            return e.activation(
                e_ap(par_e, 0, [[1, 4 * w]]),
                stAB[:, 0:4 * w], AF.Exp, scale=1.0 / 32.0)
        ev("scalar", sw, "s_exp", op_expA)
        m_expA[i] = cnt["s_exp"]

    def stage_expB(i):
        sl, quad, mcq, s0, w = iters[i]
        par_e = i % 6

        def op_expB(e, w=w, par_e=par_e):
            return e.activation(
                e_ap(par_e, 4 * w, [[1, 4 * w]]),
                stAB[:, 4 * w:NH * w], AF.Exp, scale=1.0 / 32.0)
        ev("scalar", [("t_st", m_stB[i]), ("s_exp", m_expA[i])], "s_exp", op_expB)
        last_sc[0] = ("s_exp", cnt["s_exp"])
        m_expB[i] = cnt["s_exp"]

    def stage_den(i):
        sl, quad, mcq, s0, w = iters[i]
        par_e = i % 6
        par_d = i % 2

        def op_d1(e, w=w, par_e=par_e):
            return e.tensor_tensor(d1[:, 0:4 * w],
                                   e_ap(par_e, 0, [[1, 4 * w]]),
                                   e_ap(par_e, 4 * w, [[1, 4 * w]]),
                                   op=OP.add)
        vec_op([("s_exp", m_expB[i])], "v_self", op_d1)
        vec_op([], "v_self", lambda e, w=w: e.tensor_tensor(
            d2[:, 0:2 * w], d1[:, 0:2 * w], d1[:, 2 * w:4 * w], op=OP.add))
        w3 = [("s_self", m_ln[i - 2])] if i - 2 >= 0 else []
        vec_op(w3, "v_den", lambda e, w=w, par_d=par_d: e.tensor_tensor(
            den[:, par_d * 256: par_d * 256 + w], d2[:, 0:w], d2[:, w:2 * w], op=OP.add))
        m_L3[i] = cnt["v_den"]

    def stage_lnexp(i):
        sl, quad, mcq, s0, w = iters[i]
        par_d = i % 2
        par_r = i % 4
        lw = [("v_den", m_L3[i])]
        if i - 1 >= 0:
            lw.append(("s_rcp", m_exp2[i - 1]))
        ev("scalar", lw, "s_self",
           lambda e, w=w, par_d=par_d: e.activation(
               lbuf[:, 0:w], den[:, par_d * 256: par_d * 256 + w], AF.Ln))
        last_sc[0] = ("s_self", cnt["s_self"])
        m_ln[i] = cnt["s_self"]
        w2 = [("v_du", m_U[i - 4])] if i - 4 >= 0 else []
        sc_op(w2, "s_rcp",
              lambda e, w=w, par_r=par_r: e.activation(
                  rrc[:, par_r * 256: par_r * 256 + w], lbuf[:, 0:w], AF.Exp,
                  scale=-1.0))
        m_exp2[i] = cnt["s_rcp"]

    def stage_U(i):
        sl, quad, mcq, s0, w = iters[i]
        par_e = i % 6
        par_u = i % 2
        par_r = i % 4
        wts = [("s_rcp", m_exp2[i])]
        if i - 2 >= 0:
            wts.append(("t_ctx", m_ctx[i - 2]))
        vec_op(wts, "v_du",
               lambda e, w=w, par_u=par_u, par_e=par_e, par_r=par_r: e.tensor_tensor(
                   arena_ap(UB0 + par_u * 2048, [[1, NH * w]]),
                   e_ap(par_e, 0, [[1, NH * w]]),
                   bass.AP(scratch, 1536 + par_r * 256,
                           [[scratch.shape[1], 128], [0, NH], [1, w]]),
                   op=OP.mult))
        m_U[i] = cnt["v_du"]

    def stage_ctx(i):
        sl, quad, mcq, s0, w = iters[i]
        par_k = quad % 2
        par_u = i % 2
        nts = w // 128
        k_sl = quad * 4 + mcq
        tw = [("v_du", m_U[i]), v_load_mark[(sl, quad)]]
        if k_sl == 0:
            tw.append(("s_cpx", cnt["s_cpx"]))
            if sl == 0:
                tw.append(("c_stv", mark_vt_all))

        def op_ctx(e, w=w, par_u=par_u, par_k=par_k, mcq=mcq, nts=nts,
                   first=(k_sl == 0), last=(k_sl == 31)):
            r = None
            for g in range(NH):
                for nt in range(nts):
                    reg = g * nts + nt
                    start = first and (reg % 4 == 0)
                    stop = last and (reg % 4 == 3 or reg == NH * nts - 1)
                    r = e.matmul(
                        ctxP[:, reg * 128:(reg + 1) * 128],
                        arena_ap(UB0 + par_u * 2048 + g * w + nt * 128, [[1, 128]]),
                        arena_ap(VQ0 + par_k * 4096 + g * 512 + mcq * 128, [[1, 128]]),
                        start=start, stop=stop)
            return r
        ev("tensor", tw, "t_ctx", op_ctx)
        m_ctx[i] = cnt["t_ctx"]
        if k_sl == 31:
            cw = [("t_ctx", m_ctx[i])]
            if ctxs_free[0] is not None:
                cw.append(ctxs_free[0])
            sc_op(cw, "s_cpx",
                  lambda e, nts=nts: e.activation(
                      ctxs[:, 0:NH * nts * 128], ctxP[:, 0:NH * nts * 128], AF.Copy))
            cpx_mark = ("s_cpx", cnt["s_cpx"])
            ti0 = s0 // 128
            for j in range(4):
                if nts == 2:
                    src = bass.AP(ctxs, j * 2 * 256,
                                  [[ctxs.shape[1], 128], [256, 2], [128, 2], [1, 128]])
                    dst = bass.AP(x_res, ti0 * 1024 + j * 2 * 128,
                                  [[x_res.shape[1], 128], [128, 2], [1024, 2], [1, 128]])
                else:
                    src = bass.AP(ctxs, j * 2 * 128,
                                  [[ctxs.shape[1], 128], [128, 2], [1, 128]])
                    dst = bass.AP(x_res, ti0 * 1024 + j * 2 * 128,
                                  [[x_res.shape[1], 128], [128, 2], [1, 128]])
                wts = [cpx_mark] if j == 0 else []
                pending_evac.append((wts, j == 3,
                                     lambda e, dst=dst, src=src: e.tensor_tensor(
                                         dst, dst, src, op=OP.add)))
            for nt2 in range(nts):
                ti = s0 // 128 + nt2
                so = (ti % 4) * 18
                bn12 = stats[:, so:so + 12]
                src_t = x_res[:, ti * 1024:(ti + 1) * 1024]
                pending_fstats.append(([],
                                       lambda e, bn12=bn12, src_t=src_t: e.bn_stats(
                                           bn12[:, 0:6], src_t[:, 0:512])))
                pending_fstats.append(([],
                                       lambda e, bn12=bn12, src_t=src_t: e.bn_stats(
                                           bn12[:, 6:12], src_t[:, 512:1024])))
                pending_fstats.append(([],
                                       lambda e, ti=ti, bn12=bn12: e.bn_aggr(
                                           fstats[:, ti * 2:ti * 2 + 2], bn12)))

    pending_evac = []
    pending_fstats = []
    ctxs_free = [None]

    def drain_evac(n):
        for _ in range(min(n, len(pending_evac))):
            wts, is_last, op = pending_evac.pop(0)
            vec_op(wts, "v_ctx" if is_last else "v_self", op)
            if is_last:
                ctxs_free[0] = (last_vec[0][0], last_vec[0][1])

    def drain_fstats(n):
        if pending_evac:
            return
        for _ in range(min(n, len(pending_fstats))):
            wts, op = pending_fstats.pop(0)
            vec_op(wts, "v_self", op)

    for slot in range(NIT + 4):
        if slot < NIT:
            stage_loads(slot)
            stage_st(slot)
        if slot < NIT:
            stage_expA(slot)
        if 0 <= slot - 2 < NIT:
            stage_lnexp(slot - 2)
        if slot < NIT:
            stage_expB(slot)
        if 0 <= slot - 1 < NIT:
            stage_den(slot - 1)
        if 0 <= slot - 3 < NIT:
            stage_U(slot - 3)
        drain_evac(1)
        drain_fstats(1)
        if 0 <= slot - 4 < NIT:
            stage_ctx(slot - 4)

    drain_evac(len(pending_evac))
    drain_fstats(len(pending_fstats))
    total_ctx = cnt["t_ctx"]
    total_vctx = cnt["v_ctx"]
    total_cpx = cnt["s_cpx"]
    for si in range(4):
        stsub_mark[si] = ("s_exp", cnt["s_exp"])

    total_st = cnt["t_st"]
    total_du = cnt["v_du"]
    for cj in range(4, NC):
        ev("sync", [("d_ws0", 16 * 12)], "d_tw",
           lambda e, cj=cj: e.dma_start_transpose(
               bass.AP(wkbuf, (cj - 4) * 1024, [[wkbuf.shape[1], 128], [1, 1024]]),
               bass.AP(wspill, 8 * 128 * D + cj * 128, [[D, 1024], [1, 128]])), 16)
    for cj in range(4):
        ev("sync", [("t_ctx", m_ctx[NIT - 5]), ("d_ws0", 16 * 12)], "d_tw",
           lambda e, cj=cj: e.dma_start_transpose(
               arena_ap(8192 + cj * 1024, [[1, 1024]]),
               bass.AP(wspill, 8 * 128 * D + cj * 128, [[D, 1024], [1, 128]])), 16)
    for wi, gate in [(0, ("t_ffn", 1)), (2, ("t_ffn", 9))]:
        for cj in range(NC):
            gw = [("t_ctx", total_ctx), ("v_du", total_du), ("t_st", total_st),
                  ("d_ws0", 16 * 12), gate]
            ev("sync", gw, "d_tw",
               lambda e, wi=wi, cj=cj: e.dma_start_transpose(
                   arena_ap(wi * 8192 + cj * 1024, [[1, 1024]]),
                   bass.AP(wspill, wi * 8 * 128 * D + cj * 128, [[D, 1024], [1, 128]])), 16)

    ev("scalar", [("v_self", cnt["v_self"])], "s_sd",
       lambda e: e.activation(stats[:, 0:QT],
                              bass.AP(fstats, 1, [[36, 128], [2, QT]]),
                              AF.Sqrt, bias=csts[:, 0:1]))
    ev("vector", [("s_sd", cnt["s_sd"])], "v_self",
       lambda e: e.reciprocal(bass.AP(fstats, 1, [[36, 128], [2, QT]]),
                              stats[:, 0:QT]))
    fstats_done = cnt["v_self"]

    kact = scratch[:, 8192:10240]
    rsb = arena[:, 24576:25600]
    prevbuf = arena[:, 25600:26624]
    relu_b = arena[:, 26624:27648]
    t1b = arena[:, 27648:28672]
    tsb = scratch_f[:, 1024:1536]

    ffn_mark = {}
    mix_mark = {}
    first_ffn_mm = [True]
    first_relu = [True]

    ln2_pend = {}

    ln2_par = [0]

    def ln2_chains(tc):
        tc0, wch = TCHUNKS[tc]
        ntiles = wch // 128
        t0 = tc0 // 128
        pend = []
        for tt in range(ntiles):
            ti = t0 + tt
            src2 = x_res[:, ti * 1024:(ti + 1) * 1024]
            par = ln2_par[0] % 2
            ln2_par[0] += 1
            dst = xn_sb[:, par * 1024:(par + 1) * 1024]
            n = 32 + ti
            nwaits = [("v_self", fstats_done), ("t_stw", xn_tr_mark[31])]
            if n - 2 in xn_tr_mark:
                nwaits.append(("t_stw", xn_tr_mark[n - 2]))
            ev("vector", nwaits, "s_xn",
               lambda e, dst=dst, src2=src2, ti=ti: e.tensor_scalar(
                   dst, src2, fstats[:, ti * 2:ti * 2 + 1],
                   fstats[:, ti * 2 + 1:ti * 2 + 2],
                   op0=OP.subtract, op1=OP.mult))
            pend.append((n, dst, cnt["s_xn"]))
        ln2_pend[tc] = pend

    def mix_part(tc, tiles=None):
        tc0, wch = TCHUNKS[tc]
        ntiles = wch // 128
        cp = tc % 2
        xr_sl = scratch[:, cp * 2048:(cp + 1) * 2048]
        xk_sl = scratch[:, 4096 + cp * 2048: 4096 + (cp + 1) * 2048]
        for tt in (range(ntiles) if tiles is None else tiles):
            n, dst, sxn_mark = ln2_pend[tc][tt]
            half = n % 2
            emit_tile_transposes(dst, half, identf[:],
                                 [("s_xn", sxn_mark), ("v_ctx", total_vctx)])
            xn_tr_mark[n] = cnt["t_stw"]
            cbase = half * 1024
            mw = [("t_stw", cnt["t_stw"]), ("v_mix", cnt["v_mix"])]
            if tc - 2 >= 0:
                mw.append(("t_ffn", ffn_mark[tc - 2]))
            ev_noinc("vector", mw, lambda e, cbase=cbase: e.tensor_copy(
                bass.AP(arena, 25600 + 1, [[A, 128], [128, NC], [1, 127]]),
                bass.AP(ctxP, cbase, [[2048, 128], [128, NC], [1, 127]])))
            ev("vector", [], "v_self", lambda e: e.tensor_copy(
                bass.AP(arena, 25600, [[A, 128], [128, NC]]), carry[:, 0:NC]))
            ev("vector", [("v_self", cnt["v_self"]), ("s_act", cnt["s_act"])], "v_self",
               lambda e, cbase=cbase: e.tensor_tensor(
                   t1b[:, 0:1024], ctxP[:, cbase:cbase + 1024],
                   prevbuf[:, 0:1024], op=OP.subtract))
            t1_chain = cnt["v_self"]
            ev("vector", [("v_self", t1_chain)], "c_stv", lambda e, cbase=cbase: e.tensor_copy(
                carry[:, 0:NC], bass.AP(ctxP, cbase + 127, [[2048, 128], [128, NC]])))
            tr_mark[half] = cnt["c_stv"]
            mixtmp = arena[:, 28672:29696]
            xr_off = cp * 2048 + tt * 128
            xk_off = 4096 + cp * 2048 + tt * 128
            ev("vector", [("v_self", t1_chain)], "v_self",
               lambda e: e.tensor_tensor(
                   mixtmp[:, 0:1024], t1b[:, 0:1024],
                   bass.AP(cvec, 24, [[40, 128], [1, 8], [0, 128]]),
                   op=OP.mult))
            ev("vector", [("v_self", cnt["v_self"])] , "v_self",
               lambda e, xr_off=xr_off: e.tensor_tensor(
                   bass.AP(scratch, xr_off, [[scratch.shape[1], 128], [256, NC], [1, 128]]),
                   mixtmp[:, 0:1024], prevbuf[:, 0:1024], op=OP.add))
            ev("vector", [("v_self", cnt["v_self"])], "v_self",
               lambda e: e.tensor_tensor(
                   mixtmp[:, 0:1024], t1b[:, 0:1024],
                   bass.AP(cvec, 32, [[40, 128], [1, 8], [0, 128]]),
                   op=OP.mult))
            ev("vector", [("v_self", cnt["v_self"])], "v_mix",
               lambda e, xk_off=xk_off: e.tensor_tensor(
                   bass.AP(scratch, xk_off, [[scratch.shape[1], 128], [256, NC], [1, 128]]),
                   mixtmp[:, 0:1024], prevbuf[:, 0:1024], op=OP.add))
        mix_mark[tc] = cnt["v_mix"]

    def ffn_matmuls(tc, after_k=None, after_tile=None):
        tc0, wch = TCHUNKS[tc]
        ntiles = wch // 128
        t0 = tc0 // 128
        cp = tc % 2
        xr_sl = scratch[:, cp * 2048:(cp + 1) * 2048]
        xk_sl = scratch[:, 4096 + cp * 2048: 4096 + (cp + 1) * 2048]
        for dj in range(NC):
            si = next_stsub()
            ps = stsub[si]
            tw = [("v_mix", mix_mark[tc]), (stsub_mark[si][0], stsub_mark[si][1])]
            if first_ffn_mm[0]:
                tw.append(("d_tw", 16 * 8))
                first_ffn_mm[0] = False

            def op_k(e, ps=ps, dj=dj, wch=wch, xk_sl=xk_sl):
                r = None
                for cj in range(NC):
                    if cj < 4:
                        wsrc2 = arena_ap((8 + cj) * 1024 + dj * 128, [[1, 128]])
                    else:
                        wsrc2 = bass.AP(wkbuf, (cj - 4) * 1024 + dj * 128,
                                        [[wkbuf.shape[1], 128], [1, 128]])
                    r = e.matmul(ps[:, 0:wch], wsrc2,
                                 xk_sl[:, cj * 256: cj * 256 + wch],
                                 start=(cj == 0), stop=(cj == NC - 1))
                return r
            ev("tensor", tw, "t_ffn", op_k)
            sw = [("t_ffn", cnt["t_ffn"]), ("v_sq", max(0, cnt["v_sq"] - 1))]
            if first_relu[0]:
                sw.append(("t_ctx", total_ctx))
                first_relu[0] = False
            pj = dj % 2
            ev("scalar", sw, "s_act",
               lambda e, ps=ps, wch=wch, pj=pj: e.activation(
                   relu_b[:, pj * 512: pj * 512 + wch], ps[:, 0:wch], AF.Relu))
            stsub_mark[si] = ("s_act", cnt["s_act"])
            ev("vector", [("s_act", cnt["s_act"])], "v_sq",
               lambda e, pj=pj, dj=dj, wch=wch: e.tensor_tensor(
                   kact[:, dj * 256: dj * 256 + wch],
                   relu_b[:, pj * 512: pj * 512 + wch],
                   relu_b[:, pj * 512: pj * 512 + wch], op=OP.mult))
        ksq_done = cnt["v_sq"]
        if after_k is not None:
            after_k()

        for tt in range(ntiles):
            if after_tile is not None and tt > 0:
                after_tile(tt - 1)
            ti = t0 + tt
            fin_before = cnt["v_fin"]
            for ds in range(2):
                si = next_stsub()
                ps = stsub[si]
                tw = [(stsub_mark[si][0], stsub_mark[si][1])]
                if tt == 0 and ds == 0:
                    tw.append(("d_tw", 16 * 16))

                def op_r(e, ps=ps, tt=tt, ds=ds, xr_sl=xr_sl):
                    r = None
                    for cj in range(NC):
                        r = e.matmul(ps[:, 0:512],
                                     xr_sl[:, cj * 256 + tt * 128: cj * 256 + (tt + 1) * 128],
                                     arena_ap(cj * 1024 + ds * 512, [[1, 512]]),
                                     start=(cj == 0), stop=(cj == NC - 1))
                    return r
                ev("tensor", tw, "t_ffn", op_r)
                ev("scalar", [("t_ffn", cnt["t_ffn"]), ("v_fin", fin_before)], "s_act",
                   lambda e, ps=ps, ds=ds: e.activation(
                       rsb[:, ds * 512:(ds + 1) * 512], ps[:, 0:512], AF.Sigmoid))
                stsub_mark[si] = ("s_act", cnt["s_act"])
            sig_done = cnt["s_act"]
            for ds in range(2):
                si = next_stsub()
                ps = stsub[si]
                tw = [("v_sq", ksq_done), (stsub_mark[si][0], stsub_mark[si][1])]
                if tt == 0 and ds == 0:
                    tw.append(("d_tw", 16 * 24))

                def op_v(e, ps=ps, tt=tt, ds=ds):
                    r = None
                    for cj in range(NC):
                        r = e.matmul(ps[:, 0:512],
                                     kact[:, cj * 256 + tt * 128: cj * 256 + (tt + 1) * 128],
                                     arena_ap((16 + cj) * 1024 + ds * 512, [[1, 512]]),
                                     start=(cj == 0), stop=(cj == NC - 1))
                    return r
                ev("tensor", tw, "t_ffn", op_v)
                ev("vector", [("t_ffn", cnt["t_ffn"]), ("s_act", sig_done),
                              ("v_self", cnt["v_self"])], "v_self",
                   lambda e, ps=ps, ds=ds: e.tensor_tensor(
                       tsb[:, 0:512], ps[:, 0:512],
                       rsb[:, ds * 512:(ds + 1) * 512], op=OP.mult))
                dst = x_res[:, ti * 1024 + ds * 512: ti * 1024 + ds * 512 + 512]
                is_last = (ds == 1)
                ev("vector", [("v_self", cnt["v_self"])],
                   "v_fin" if is_last else "v_self",
                   lambda e, dst=dst: e.tensor_tensor(dst, dst, tsb[:, 0:512], op=OP.add))
                stsub_mark[si] = ("v_fin", cnt["v_fin"] if is_last else cnt["v_fin"] + 1)
        ffn_mark[tc] = cnt["t_ffn"]

        ev("sync", [("v_fin", cnt["v_fin"])], "d_out",
           lambda e, t0=t0, ntiles=ntiles: e.dma_start(
               bass.AP(out_h, t0 * 128 * 1024,
                       [[1024, 128], [128 * 1024, ntiles], [1, 1024]]),
               ap3(x_res, t0 * 1024, [[1024, ntiles], [1, 1024]])), 16)

    emit_warmup(48, [("s_cpx", total_cpx)])
    ln2_chains(0)
    mix_part(0)
    for tc in range(len(TCHUNKS)):
        if tc + 1 < len(TCHUNKS):
            nt_next = TCHUNKS[tc + 1][1] // 128
            ln2_chains(tc + 1)
            ffn_matmuls(tc,
                        after_k=lambda tc=tc: mix_part(tc + 1, [0]),
                        after_tile=lambda tt, tc=tc, n=nt_next:
                            mix_part(tc + 1, [tt + 1]) if tt + 1 < n else None)
            if nt_next > TCHUNKS[tc][1] // 128:
                mix_part(tc + 1, list(range(TCHUNKS[tc][1] // 128, nt_next)))
        else:
            ffn_matmuls(tc)

    ev_noinc("sync", [("d_out", cnt["d_out"])], lambda e: e.nop())

    with nc.Block() as block:
        @block.sync
        def _(e):
            for fn in sched["sync"]:
                fn(e)

        @block.gpsimd
        def _(e):
            for fn in sched["gpsimd"]:
                fn(e)

        @block.tensor
        def _(e):
            for fn in sched["tensor"]:
                fn(e)

        @block.scalar
        def _(e):
            for fn in sched["scalar"]:
                fn(e)

        @block.vector
        def _(e):
            for fn in sched["vector"]:
                fn(e)

    es.close()
    return nc


def _in_maps(inputs):
    x = np.asarray(inputs["L0_attn_hidden_states"], dtype=np.float32)
    cvec = np.empty((128, 40), dtype=np.float32)
    for i, k in enumerate(["L0_attn_bq", "L0_attn_bk", "L0_attn_bv",
                           "L0_ffn_mu_r", "L0_ffn_mu_k"]):
        cvec[:, i * 8:(i + 1) * 8] = np.asarray(inputs[k], np.float32).reshape(8, 128).T
    ident = np.eye(128, dtype=ml_dtypes.bfloat16)
    identf = np.eye(128, dtype=np.float32)
    base = {
        "wq": np.asarray(inputs["L0_attn_Wq"], np.float32),
        "wk": np.asarray(inputs["L0_attn_Wk"], np.float32),
        "wv": np.asarray(inputs["L0_attn_Wv"], np.float32),
        "wr2": np.asarray(inputs["L0_ffn_W_r"], np.float32),
        "wk2": np.asarray(inputs["L0_ffn_W_k"], np.float32),
        "wv2": np.asarray(inputs["L0_ffn_W_v"], np.float32),
        "cvec": cvec, "ident": ident, "identf": identf,
    }
    maps = []
    for c in range(8):
        bi, half = c // 2, c % 2
        xb = x[bi] if half == 0 else np.roll(x[bi], -1920, axis=0)
        m = dict(base)
        m["x"] = np.ascontiguousarray(xb)
        maps.append(m)
    return maps


def kernel(**inputs):
    maps = _in_maps(inputs)
    nc = build_nc()
    res = run_bass_kernel_spmd(nc, maps, core_ids=list(range(8)))
    out = np.empty((4, 4096, 1024), dtype=np.float32)
    for c in range(8):
        bi, half = c // 2, c % 2
        o = np.asarray(res.results[c]["out"])
        if half == 0:
            out[bi, 0:2048] = o[0:2048]
        else:
            out[bi, 2048:4096] = o[128:2176]
    return out


# version 43
# speedup vs baseline: 1.0035x; 1.0027x over previous
import numpy as np
import ml_dtypes
from contextlib import ExitStack

import concourse.bass as bass
import concourse.mybir as mybir
from concourse.bass_utils import run_bass_kernel_spmd

F32 = mybir.dt.float32
BF16 = mybir.dt.bfloat16
AF = mybir.ActivationFunctionType
OP = mybir.AluOpType

D = 1024
NC = 8
NH = 8
T = 4096
NT = 32
QT = 17
NQ = QT * 128
EPS = 1e-5
A_SLICES = [(i * 256, 256) for i in range(8)] + [(2048, 128)]
Q_PROJ_SLICES = [(0, 512), (512, 512), (1024, 512), (1536, 512), (2048, 128)]
TCHUNKS = [(i * 256, 256) for i in range(8)] + [(2048, 128)]

SEM_NAMES = ("d_x", "d_cst", "d_xt0", "d_xt1", "d_xt2", "d_xt3", "d_wa", "d_spk", "d_spv", "d_spq",
             "d_q0", "d_q1", "d_k0", "d_k1", "d_v0", "d_v1", "d_wf0", "d_wf1",
             "d_out", "v_ln", "s_sd", "s_xn", "t_stw", "c_stv", "s_exp", "v_du",
             "t_proj", "v_proj", "t_st", "t_ctx", "v_ctx", "s_cpx", "t_ffn", "s_act",
             "v_sq", "v_mix", "v_fin", "v_self", "v_den", "s_rcp", "s_self", "g_rcp",
             "s_cpy", "d_ws0", "d_tw")


def build_nc():
    nc = bass.Bass("TRN2", target_bir_lowering=False)

    x_h = nc.dram_tensor("x", [T, D], F32, kind="ExternalInput")
    wq_h = nc.dram_tensor("wq", [D, D], F32, kind="ExternalInput")
    wk_h = nc.dram_tensor("wk", [D, D], F32, kind="ExternalInput")
    wv_h = nc.dram_tensor("wv", [D, D], F32, kind="ExternalInput")
    wr2_h = nc.dram_tensor("wr2", [D, D], F32, kind="ExternalInput")
    wk2_h = nc.dram_tensor("wk2", [D, D], F32, kind="ExternalInput")
    wv2_h = nc.dram_tensor("wv2", [D, D], F32, kind="ExternalInput")
    cvec_h = nc.dram_tensor("cvec", [128, 40], F32, kind="ExternalInput")
    ident_h = nc.dram_tensor("ident", [128, 128], BF16, kind="ExternalInput")
    identf_h = nc.dram_tensor("identf", [128, 128], F32, kind="ExternalInput")
    out_h = nc.dram_tensor("out", [NQ, D], F32, kind="ExternalOutput")
    wspill = nc.dram_tensor("wspill", [3 * D, D], BF16)
    kspill = nc.dram_tensor("kspill", [128, NH * T], BF16)
    vtspill = nc.dram_tensor("vtspill", [128, NH * T], BF16)
    qspill = nc.dram_tensor("qspill", [128, NH * NQ], BF16)

    es = ExitStack()

    def sbuf(name, cols, dt):
        return es.enter_context(nc.sbuf_tensor(name, [128, cols], dt))

    def psum(name, cols, dt):
        return es.enter_context(nc.psum_tensor(name, [128, cols], dt))

    x_res = sbuf("x_res", QT * 1024, F32)
    xtmp = sbuf("xtmp", 4 * 1024, BF16)
    arena = sbuf("arena", 32768, BF16)
    xn_sb = sbuf("xn_sb", 2 * 1024, F32)
    scratch = sbuf("scratch", 10368, BF16)
    scratch_f = sbuf("scratch_f", 1536, F32)
    ctxs = sbuf("ctxs", 2048, F32)
    wkbuf = sbuf("wkbuf", 4096, BF16)
    wa = sbuf("wa", 6144, BF16)
    cvec = sbuf("cvecsb", 40, F32)
    ident = sbuf("identsb", 128, BF16)
    identf = sbuf("identfsb", 128, F32)
    stats = sbuf("stats", 72, F32)
    csts = sbuf("csts", 2, F32)
    fstats = sbuf("fstats", 36, F32)
    carry = sbuf("carry", 8, BF16)

    stAB = psum("stAB", 2048, F32)
    ctxP = psum("ctxP", 2048, F32)

    sems = {n: es.enter_context(nc.semaphore(n)) for n in SEM_NAMES}
    sched = {"tensor": [], "vector": [], "scalar": [], "sync": [], "gpsimd": []}
    cnt = {n: 0 for n in SEM_NAMES}

    def ev(eng, waits, inc, op, amt=1):
        waits = [(sems[n], v) for (n, v) in waits if v > 0]
        incsem = sems[inc] if inc else None

        def fn(e, waits=tuple(waits), incsem=incsem, op=op, amt=amt):
            for s, v in waits:
                e.wait_ge(s, v)
            r = op(e)
            if incsem is not None:
                r.then_inc(incsem, amt)
        sched[eng].append(fn)
        if inc:
            cnt[inc] += amt

    def ev_noinc(eng, waits, op):
        ev(eng, waits, None, op)

    def drain(eng):
        sched[eng].append(lambda e: e.drain())

    def ap3(t_sb, offset, dims):
        return bass.AP(t_sb, offset, [[t_sb.shape[1], 128]] + dims)

    A = arena.shape[1]

    def arena_ap(offset, dims):
        return bass.AP(arena, offset, [[A, 128]] + dims)

    stsub = [stAB[:, i * 512:(i + 1) * 512] for i in range(4)]
    stsub_mark = [("d_cst", 0)] * 4
    stsub_rot = [0]

    def next_stsub():
        i = stsub_rot[0] % 4
        stsub_rot[0] += 1
        return i

    tr_mark = [0, 0]
    xn_tr_mark = {}
    ln_done = [0]

    ev("sync", [], "d_cst", lambda e: e.dma_start(cvec[:], cvec_h.ap()), 16)
    ev("sync", [], "d_cst", lambda e: e.dma_start(ident[:], ident_h.ap()), 16)
    ev("sync", [], "d_cst", lambda e: e.dma_start(identf[:], identf_h.ap()), 16)
    x_tiled = x_h.ap().rearrange("(t p) c -> p t c", p=128)
    ev("sync", [], "d_x",
       lambda e: e.dma_start(ap3(x_res, 0, [[1024, QT], [1, 1024]]), x_tiled[:, 0:QT, :]), 16)
    ev_noinc("vector", [], lambda e: e.memset(carry[:, 0:NC], 0.0))
    ev_noinc("vector", [], lambda e: e.memset(csts[:, 0:1], EPS))
    ev_noinc("vector", [], lambda e: e.memset(csts[:, 1:2], -1.0))

    def op_warm0(e):
        r = None
        for _ in range(48):
            r = e.matmul(ctxP[:, 0:128], ident[:], ident[:], start=True, stop=True)
        return r
    ev("tensor", [("d_cst", 48)], None, op_warm0)

    def emit_xtmp_dma(t):
        par = t % 4
        waits = [("s_xn", t - 3)] if t - 4 >= QT else []
        ev("gpsimd", waits, "d_xt%d" % par,
           lambda e, par=par, t=t: e.dma_start(
               xtmp[:, par * 1024:(par + 1) * 1024], x_tiled[:, t, :]), 16)

    def emit_wa_dma(g, wi, w_h):
        src = w_h.ap().rearrange("(cj p) d -> p cj d", p=128)[:, :, g * 128:(g + 1) * 128]
        if g >= 2:
            waits = [("t_proj", 21 * (g - 1) + 1)]
        elif g == 1:
            waits = [("t_proj", 1)]
        elif wi > 0:
            waits = [("d_x", 16)]
        else:
            waits = []
        ev("gpsimd", waits, "d_wa",
           lambda e, g=g, wi=wi, src=src: e.dma_start(
               bass.AP(wa, (g % 2) * 3072 + wi * 1024, [[wa.shape[1], 128], [128, NC], [1, 128]]),
               src), 16)

    emit_xtmp_dma(17)
    emit_xtmp_dma(18)
    for wi, w_h in enumerate([wq_h, wk_h, wv_h]):
        emit_wa_dma(0, wi, w_h)
    for t in range(19, NT):
        emit_xtmp_dma(t)

    def emit_ln(src_ap, src_wait, dst_ap, extra_norm_wait):
        n = ln_done[0]
        par = n % 4
        so = par * 18
        bn12 = stats[:, so:so + 12]
        mv = stats[:, so + 12:so + 14]
        sd = stats[:, so + 14:so + 15]
        inv = stats[:, so + 15:so + 16]
        waits = list(src_wait)
        if n - 4 >= 0:
            waits.append(("s_sd", n - 3))
        ev_noinc("vector", waits, lambda e: e.bn_stats(bn12[:, 0:6], src_ap[:, 0:512]))
        ev("vector", [], "v_self", lambda e: e.bn_stats(bn12[:, 6:12], src_ap[:, 512:1024]))
        ev("vector", [("v_self", cnt["v_self"])], "v_ln", lambda e: e.bn_aggr(mv, bn12))
        ev("scalar", [("v_ln", cnt["v_ln"])], "s_sd",
           lambda e: e.activation(sd, mv[:, 1:2], AF.Sqrt, bias=csts[:, 0:1]))
        ev("vector", [("s_sd", cnt["s_sd"])], "v_self", lambda e: e.reciprocal(inv, sd))
        nwaits = list(extra_norm_wait) + [("v_self", cnt["v_self"])]
        if n - 2 in xn_tr_mark:
            nwaits.append(("t_stw", xn_tr_mark[n - 2]))
        ev("vector", nwaits, "s_xn",
           lambda e: e.tensor_scalar(dst_ap, src_ap, mv[:, 0:1], inv,
                                     op0=OP.subtract, op1=OP.mult))
        ln_done[0] += 1
        return n

    def emit_tile_transposes(src_ap, half, idf, extra_waits=()):
        tw = [("c_stv", tr_mark[half])] + list(extra_waits)

        def op(e, src_ap=src_ap, half=half, idf=idf):
            r = None
            for j in range(NC):
                r = e.transpose(ctxP[:, half * 1024 + j * 128: half * 1024 + (j + 1) * 128],
                                src_ap[:, j * 128:(j + 1) * 128], idf)
            return r
        ev("tensor", tw, "t_stw", op)

    kg = scratch[:, 0:4096]
    vtb = scratch[:, 4096:8192]
    qg = scratch[:, 8192:10368]
    vsb = scratch_f[:, 0:1024]
    tile_evac_mark = {}
    vsb_tr_mark = [0, 0]

    def emit_proj_group(g, kind, dst_ap, s0, w, bias_col, wi, spill_wait=()):
        si = next_stsub()
        ps = stsub[si]
        req = (s0 + w - 1) // 128
        tw = [("d_wa", 48 * (g + 1)),
              (stsub_mark[si][0], stsub_mark[si][1]),
              ("c_stv", tile_evac_mark[req])]

        def op(e, ps=ps, s0=s0, w=w, wi=wi, g=g):
            r = None
            for cj in range(NC):
                r = e.matmul(ps[:, 0:w],
                             wa[:, (g % 2) * 3072 + wi * 1024 + cj * 128:
                                 (g % 2) * 3072 + wi * 1024 + (cj + 1) * 128],
                             arena_ap(cj * 4096 + s0, [[1, w]]),
                             start=(cj == 0), stop=(cj == NC - 1))
            return r
        ev("tensor", tw, "t_proj", op)
        vw = [("t_proj", cnt["t_proj"])] + list(spill_wait)
        if kind == "v":
            par2 = dst_ap
            vw.append(("t_stw", vsb_tr_mark[par2]))
            dst = vsb[:, par2 * 512:(par2 + 1) * 512]

            def opv(e, dst=dst, ps=ps, w=w, bias_col=bias_col):
                return e.tensor_scalar(dst, ps[:, 0:w],
                                       cvec[:, bias_col:bias_col + 1],
                                       None, op0=OP.add)
            ev("vector", vw + [("d_cst", 48)], "v_proj", opv)
            stsub_mark[si] = ("v_proj", cnt["v_proj"])
            return
        dst = dst_ap

        def opc(e, dst=dst, ps=ps, w=w):
            return e.activation(dst, ps[:, 0:w], AF.Copy)
        ev("scalar", vw, "s_cpy", opc)
        stsub_mark[si] = ("s_cpy", cnt["s_cpy"])
        ev("vector", [("s_cpy", cnt["s_cpy"]), ("d_cst", 48)], "v_proj",
           lambda e, dst=dst, bias_col=bias_col: e.tensor_scalar(
               dst, dst, cvec[:, bias_col:bias_col + 1], None, op0=OP.add))

    def emit_v_post(g, vs, vp_mark=None):
        par2 = vs % 2
        half = vs % 2
        tw = [vp_mark or ("v_proj", cnt["v_proj"]), ("c_stv", tr_mark[half])]

        def op_trv(e, par2=par2, half=half):
            r = None
            for j in range(4):
                r = e.transpose(
                    ctxP[:, half * 1024 + j * 128: half * 1024 + (j + 1) * 128],
                    vsb[:, par2 * 512 + j * 128: par2 * 512 + (j + 1) * 128],
                    identf[:])
            return r
        ev("tensor", tw, "t_stw", op_trv)
        vsb_tr_mark[par2] = cnt["t_stw"]
        ev("scalar", [("t_stw", cnt["t_stw"]), ("d_spv", 16 * g)], "c_stv",
           lambda e, vs=vs, half=half: e.activation(
               vtb[:, vs * 512:(vs + 1) * 512],
               ctxP[:, half * 1024: half * 1024 + 512], AF.Copy))
        tr_mark[half] = cnt["c_stv"]

    def emit_spills(g):
        ev("sync", [("v_proj", cnt["v_proj"])], "d_spq",
           lambda e, g=g: e.dma_start(
               bass.AP(qspill, g * NQ, [[NH * NQ, 128], [1, NQ]]), qg[:, 0:NQ]), 16)
        ev("sync", [("v_proj", cnt["v_proj"])], "d_spk",
           lambda e, g=g: e.dma_start(
               bass.AP(kspill, g * T, [[NH * T, 128], [1, T]]), kg[:, 0:T]), 16)
        ev("sync", [("c_stv", cnt["c_stv"])], "d_spv",
           lambda e, g=g: e.dma_start(
               bass.AP(vtspill, g * T, [[NH * T, 128], [1, T]]), vtb[:, 0:T]), 16)

    pending_vpost = []

    def drain_vpost():
        while pending_vpost:
            pending_vpost.pop(0)()

    def run_job(fn):
        n0 = len(pending_vpost)
        fn()
        for _ in range(min(n0, len(pending_vpost))):
            pending_vpost.pop(0)()

    def v_job(g, vs):
        emit_proj_group(g, "v", vs % 2, vs * 512, 512, 16 + g, 2,
                        spill_wait=[("d_spv", 16 * g)])
        vp_mark = ("v_proj", cnt["v_proj"])
        pending_vpost.append(lambda g=g, vs=vs, vp_mark=vp_mark:
                             emit_v_post(g, vs, vp_mark))

    jobs = []
    for g in range(NH):
        if g + 1 < NH:
            jobs.append((0, lambda g=g: [emit_wa_dma(g + 1, wi, w_h)
                                         for wi, w_h in enumerate([wq_h, wk_h, wv_h])]))
        for i in range(8):
            if i < len(Q_PROJ_SLICES):
                s0, w = Q_PROJ_SLICES[i]
                jobs.append(((s0 + w - 1) // 128,
                             lambda g=g, s0=s0, w=w: emit_proj_group(
                                 g, "q", qg[:, s0:s0 + w], s0, w, 0 + g, 0,
                                 spill_wait=[("d_spq", 16 * g)])))
            jobs.append((4 * i + 3,
                         lambda g=g, ks=i: emit_proj_group(
                             g, "k", kg[:, ks * 512:(ks + 1) * 512], ks * 512, 512,
                             8 + g, 1, spill_wait=[("d_spk", 16 * g)])))
            jobs.append((4 * i + 3, lambda g=g, vs=i: v_job(g, vs)))
        jobs.append((31, lambda g=g: (drain_vpost(), emit_spills(g))))
    jobs.reverse()

    ln_norm_mark = {}
    ln_dst = {}

    def emit_tr_evac(t):
        dst = ln_dst[t]
        half = t % 2
        extra = [("d_cst", 48), ("s_xn", ln_norm_mark[t])]
        emit_tile_transposes(dst, half, identf[:], extra)
        xn_tr_mark[t] = cnt["t_stw"]
        ev("scalar", [("t_stw", cnt["t_stw"])], "c_stv",
           lambda e, t=t, half=half: e.activation(
               arena_ap(t * 128, [[4096, NC], [1, 128]]),
               ctxP[:, half * 1024: half * 1024 + 1024], AF.Copy))
        tr_mark[half] = cnt["c_stv"]
        tile_evac_mark[t] = cnt["c_stv"]

    def emit_ln_tile(t):
        par = t % 2
        if t < QT:
            src2 = x_res[:, t * 1024:(t + 1) * 1024]
            src_wait = [("d_x", 16)]
        else:
            par4 = t % 4
            src2 = xtmp[:, par4 * 1024:(par4 + 1) * 1024]
            src_wait = [("d_xt%d" % par4, 16 * ((t - 17) // 4 + 1))]
        dst = xn_sb[:, par * 1024:(par + 1) * 1024]
        emit_ln(src2, src_wait, dst, [])
        ln_norm_mark[t] = cnt["s_xn"]
        ln_dst[t] = dst

    def emit_wcast(ci):
        wi, q4 = ci // 4, ci % 4
        w_h2 = [wr2_h, wk2_h, wv2_h][wi]
        ev("gpsimd", [], "d_ws0",
           lambda e, wi=wi, q4=q4, w_h2=w_h2: e.dma_start(
               bass.AP(wspill, (wi * 1024 + q4 * 256) * D, [[D, 256], [1, D]]),
               bass.AP(w_h2, q4 * 256 * D, [[D, 256], [1, D]])), 16)

    emit_ln_tile(0)
    emit_ln_tile(1)
    for t in range(1, NT):
        emit_tr_evac(t - 1)
        if t + 1 < NT:
            emit_ln_tile(t + 1)
        if t >= 8 and t % 2 == 0 and (t - 8) // 2 < 12:
            emit_wcast((t - 8) // 2)
        done = 0
        while jobs and jobs[-1][0] <= t - 1 and done < 8:
            run_job(jobs.pop()[1])
            done += 1
    emit_tr_evac(NT - 1)
    while jobs:
        run_job(jobs.pop()[1])
    drain_vpost()

    all_proj = cnt["t_proj"]
    all_proj_v = cnt["v_proj"]
    mark_vt_all = cnt["c_stv"]

    KQ0, VQ0, QS0, EA0, UB0 = 0, 8192, 16384, 18432, 26624
    d1 = scratch[:, 0:1024]
    d2 = scratch[:, 1024:1536]
    den = scratch_f[:, 0:512]
    lbuf = scratch_f[:, 512:768]
    rrc = scratch[:, 1536:2560]

    def emit_warmup(n_mm, waits):
        def op(e):
            r = None
            for i in range(n_mm):
                r = e.matmul(ctxP[:, 0:128], ident[:], ident[:],
                             start=True, stop=True)
            return r
        ev("tensor", waits, None, op)

    emit_warmup(48, [("c_stv", mark_vt_all), ("d_cst", 48)])

    iters = []
    for sl, (s0, w) in enumerate(A_SLICES):
        for quad in range(8):
            for mcq in range(4):
                iters.append((sl, quad, mcq, s0, w))
    NIT = len(iters)

    m_stA, m_stB, m_expA, m_expB = {}, {}, {}, {}
    m_L3, m_ln, m_exp2, m_U, m_ctx = {}, {}, {}, {}, {}
    k_load_mark, v_load_mark, q_load_mark = {}, {}, {}
    last_vec = [None]
    last_sc = [None]

    def vec_op(waits, inc, op):
        wts = list(waits)
        if last_vec[0] is not None:
            wts.append(last_vec[0])
        ev("vector", wts, inc, op)
        last_vec[0] = (inc, cnt[inc])

    def sc_op(waits, inc, op):
        wts = list(waits)
        if last_sc[0] is not None:
            wts.append(last_sc[0])
        ev("scalar", wts, inc, op)
        last_sc[0] = (inc, cnt[inc])

    def stage_loads(i):
        sl, quad, mcq, s0, w = iters[i]
        if mcq != 0:
            return
        par_k = quad % 2
        if quad == 0:
            par_q = sl % 2
            qbase = QS0 if par_q == 0 else 30720
            qw = [("d_spq", 16 * NH)]
            if sl >= 2:
                qw.append(("t_st", m_stB[(sl - 1) * 32 - 1]))
            ev("gpsimd", qw, "d_q%d" % par_q,
               lambda e, s0=s0, w=w, qbase=qbase: e.dma_start(
                   arena_ap(qbase, [[w, NH], [1, w]]),
                   bass.AP(qspill, s0, [[NH * NQ, 128], [NQ, NH], [1, w]])), 16)
            q_load_mark[sl] = ("d_q%d" % par_q, 16 * (sl // 2 + 1))
        gi0 = sl * 32 + quad * 4
        kw = [("d_spk", 16 * NH)]
        vw = [("d_spv", 16 * NH)]
        if gi0 - 8 >= 0:
            kw.append(("t_st", m_stB[gi0 - 8 + 3]))
            vw.append(("t_ctx", m_ctx[gi0 - 8 + 3]))
        ev("sync", kw, "d_k%d" % par_k,
           lambda e, quad=quad, par_k=par_k: e.dma_start(
               arena_ap(KQ0 + par_k * 4096, [[512, NH], [1, 512]]),
               bass.AP(kspill, quad * 512, [[NH * T, 128], [T, NH], [1, 512]])), 16)
        ev("gpsimd", vw, "d_v%d" % par_k,
           lambda e, quad=quad, par_k=par_k: e.dma_start(
               arena_ap(VQ0 + par_k * 4096, [[512, NH], [1, 512]]),
               bass.AP(vtspill, quad * 512, [[NH * T, 128], [T, NH], [1, 512]])), 16)
        kq_count = (sl * 8 + quad) // 2 + 1
        k_load_mark[(sl, quad)] = ("d_k%d" % par_k, 16 * kq_count)
        v_load_mark[(sl, quad)] = ("d_v%d" % par_k, 16 * kq_count)

    def stage_st(i):
        sl, quad, mcq, s0, w = iters[i]
        par_k = quad % 2
        for hf in range(2):
            tw = [k_load_mark[(sl, quad)], q_load_mark[sl]]
            if i - 1 >= 0:
                tw.append(("s_exp", (m_expA if hf == 0 else m_expB)[i - 1]))
            if i == 0:
                tw.append(("v_proj", all_proj_v))

            qbase2 = QS0 if sl % 2 == 0 else 30720

            def op_st(e, w=w, par_k=par_k, mcq=mcq, hf=hf, qbase2=qbase2):
                r = None
                for g in range(hf * 4, hf * 4 + 4):
                    r = e.matmul(
                        stAB[:, g * w:(g + 1) * w],
                        arena_ap(KQ0 + par_k * 4096 + g * 512 + mcq * 128, [[1, 128]]),
                        arena_ap(qbase2 + g * w, [[1, w]]),
                        start=True, stop=True)
                return r
            ev("tensor", tw, "t_st", op_st)
            if hf == 0:
                m_stA[i] = cnt["t_st"]
            else:
                m_stB[i] = cnt["t_st"]

    def e_ap(par, off, dims):
        if par < 4:
            return arena_ap(EA0 + par * 2048 + off, dims)
        return bass.AP(scratch, 2560 + (par - 4) * 2048 + off,
                       [[scratch.shape[1], 128]] + dims)

    def stage_expA(i):
        sl, quad, mcq, s0, w = iters[i]
        par_e = i % 6
        sw = [("t_st", m_stA[i])]
        if i - 6 >= 0:
            sw.append(("v_du", m_U[i - 6]))
        if i in (4, 5):
            sw.append(("d_spv", 16 * NH))

        def op_expA(e, w=w, par_e=par_e):
            return e.activation(
                e_ap(par_e, 0, [[1, 4 * w]]),
                stAB[:, 0:4 * w], AF.Exp, scale=1.0 / 32.0)
        ev("scalar", sw, "s_exp", op_expA)
        m_expA[i] = cnt["s_exp"]

    def stage_expB(i):
        sl, quad, mcq, s0, w = iters[i]
        par_e = i % 6

        def op_expB(e, w=w, par_e=par_e):
            return e.activation(
                e_ap(par_e, 4 * w, [[1, 4 * w]]),
                stAB[:, 4 * w:NH * w], AF.Exp, scale=1.0 / 32.0)
        ev("scalar", [("t_st", m_stB[i]), ("s_exp", m_expA[i])], "s_exp", op_expB)
        last_sc[0] = ("s_exp", cnt["s_exp"])
        m_expB[i] = cnt["s_exp"]

    def stage_den(i):
        sl, quad, mcq, s0, w = iters[i]
        par_e = i % 6
        par_d = i % 2

        def op_d1(e, w=w, par_e=par_e):
            return e.tensor_tensor(d1[:, 0:4 * w],
                                   e_ap(par_e, 0, [[1, 4 * w]]),
                                   e_ap(par_e, 4 * w, [[1, 4 * w]]),
                                   op=OP.add)
        vec_op([("s_exp", m_expB[i])], "v_self", op_d1)
        vec_op([], "v_self", lambda e, w=w: e.tensor_tensor(
            d2[:, 0:2 * w], d1[:, 0:2 * w], d1[:, 2 * w:4 * w], op=OP.add))
        w3 = [("s_self", m_ln[i - 2])] if i - 2 >= 0 else []
        vec_op(w3, "v_den", lambda e, w=w, par_d=par_d: e.tensor_tensor(
            den[:, par_d * 256: par_d * 256 + w], d2[:, 0:w], d2[:, w:2 * w], op=OP.add))
        m_L3[i] = cnt["v_den"]

    def stage_lnexp(i):
        sl, quad, mcq, s0, w = iters[i]
        par_d = i % 2
        par_r = i % 4
        lw = [("v_den", m_L3[i])]
        if i - 1 >= 0:
            lw.append(("s_rcp", m_exp2[i - 1]))
        ev("scalar", lw, "s_self",
           lambda e, w=w, par_d=par_d: e.activation(
               lbuf[:, 0:w], den[:, par_d * 256: par_d * 256 + w], AF.Ln))
        last_sc[0] = ("s_self", cnt["s_self"])
        m_ln[i] = cnt["s_self"]
        w2 = [("v_du", m_U[i - 4])] if i - 4 >= 0 else []
        sc_op(w2, "s_rcp",
              lambda e, w=w, par_r=par_r: e.activation(
                  rrc[:, par_r * 256: par_r * 256 + w], lbuf[:, 0:w], AF.Exp,
                  scale=-1.0))
        m_exp2[i] = cnt["s_rcp"]

    def stage_U(i):
        sl, quad, mcq, s0, w = iters[i]
        par_e = i % 6
        par_u = i % 2
        par_r = i % 4
        wts = [("s_rcp", m_exp2[i])]
        if i - 2 >= 0:
            wts.append(("t_ctx", m_ctx[i - 2]))
        vec_op(wts, "v_du",
               lambda e, w=w, par_u=par_u, par_e=par_e, par_r=par_r: e.tensor_tensor(
                   arena_ap(UB0 + par_u * 2048, [[1, NH * w]]),
                   e_ap(par_e, 0, [[1, NH * w]]),
                   bass.AP(scratch, 1536 + par_r * 256,
                           [[scratch.shape[1], 128], [0, NH], [1, w]]),
                   op=OP.mult))
        m_U[i] = cnt["v_du"]

    def stage_ctx(i):
        sl, quad, mcq, s0, w = iters[i]
        par_k = quad % 2
        par_u = i % 2
        nts = w // 128
        k_sl = quad * 4 + mcq
        tw = [("v_du", m_U[i]), v_load_mark[(sl, quad)]]
        if k_sl == 0:
            tw.append(("s_cpx", cnt["s_cpx"]))
            if sl == 0:
                tw.append(("c_stv", mark_vt_all))

        def op_ctx(e, w=w, par_u=par_u, par_k=par_k, mcq=mcq, nts=nts,
                   first=(k_sl == 0), last=(k_sl == 31)):
            r = None
            for g in range(NH):
                for nt in range(nts):
                    reg = g * nts + nt
                    start = first and (reg % 4 == 0)
                    stop = last and (reg % 4 == 3 or reg == NH * nts - 1)
                    r = e.matmul(
                        ctxP[:, reg * 128:(reg + 1) * 128],
                        arena_ap(UB0 + par_u * 2048 + g * w + nt * 128, [[1, 128]]),
                        arena_ap(VQ0 + par_k * 4096 + g * 512 + mcq * 128, [[1, 128]]),
                        start=start, stop=stop)
            return r
        ev("tensor", tw, "t_ctx", op_ctx)
        m_ctx[i] = cnt["t_ctx"]
        if k_sl == 31:
            cw = [("t_ctx", m_ctx[i])]
            if ctxs_free[0] is not None:
                cw.append(ctxs_free[0])
            sc_op(cw, "s_cpx",
                  lambda e, nts=nts: e.activation(
                      ctxs[:, 0:NH * nts * 128], ctxP[:, 0:NH * nts * 128], AF.Copy))
            cpx_mark = ("s_cpx", cnt["s_cpx"])
            ti0 = s0 // 128
            for j in range(4):
                if nts == 2:
                    src = bass.AP(ctxs, j * 2 * 256,
                                  [[ctxs.shape[1], 128], [256, 2], [128, 2], [1, 128]])
                    dst = bass.AP(x_res, ti0 * 1024 + j * 2 * 128,
                                  [[x_res.shape[1], 128], [128, 2], [1024, 2], [1, 128]])
                else:
                    src = bass.AP(ctxs, j * 2 * 128,
                                  [[ctxs.shape[1], 128], [128, 2], [1, 128]])
                    dst = bass.AP(x_res, ti0 * 1024 + j * 2 * 128,
                                  [[x_res.shape[1], 128], [128, 2], [1, 128]])
                wts = [cpx_mark] if j == 0 else []
                pending_evac.append((wts, j == 3,
                                     lambda e, dst=dst, src=src: e.tensor_tensor(
                                         dst, dst, src, op=OP.add)))
            for nt2 in range(nts):
                ti = s0 // 128 + nt2
                so = (ti % 4) * 18
                bn12 = stats[:, so:so + 12]
                src_t = x_res[:, ti * 1024:(ti + 1) * 1024]
                pending_fstats.append(([],
                                       lambda e, bn12=bn12, src_t=src_t: e.bn_stats(
                                           bn12[:, 0:6], src_t[:, 0:512])))
                pending_fstats.append(([],
                                       lambda e, bn12=bn12, src_t=src_t: e.bn_stats(
                                           bn12[:, 6:12], src_t[:, 512:1024])))
                pending_fstats.append(([],
                                       lambda e, ti=ti, bn12=bn12: e.bn_aggr(
                                           fstats[:, ti * 2:ti * 2 + 2], bn12)))

    pending_evac = []
    pending_fstats = []
    ctxs_free = [None]

    def drain_evac(n):
        for _ in range(min(n, len(pending_evac))):
            wts, is_last, op = pending_evac.pop(0)
            vec_op(wts, "v_ctx" if is_last else "v_self", op)
            if is_last:
                ctxs_free[0] = (last_vec[0][0], last_vec[0][1])

    def drain_fstats(n):
        if pending_evac:
            return
        for _ in range(min(n, len(pending_fstats))):
            wts, op = pending_fstats.pop(0)
            vec_op(wts, "v_self", op)

    for slot in range(NIT + 4):
        if slot < NIT:
            stage_loads(slot)
            stage_st(slot)
        if 0 <= slot - 2 < NIT:
            stage_lnexp(slot - 2)
        if slot < NIT:
            stage_expA(slot)
            stage_expB(slot)
        if 0 <= slot - 1 < NIT:
            stage_den(slot - 1)
        if 0 <= slot - 3 < NIT:
            stage_U(slot - 3)
        drain_evac(1)
        drain_fstats(1)
        if 0 <= slot - 4 < NIT:
            stage_ctx(slot - 4)

    drain_evac(len(pending_evac))
    drain_fstats(len(pending_fstats))
    total_ctx = cnt["t_ctx"]
    total_vctx = cnt["v_ctx"]
    total_cpx = cnt["s_cpx"]
    for si in range(4):
        stsub_mark[si] = ("s_exp", cnt["s_exp"])

    total_st = cnt["t_st"]
    total_du = cnt["v_du"]
    for cj in range(4, NC):
        ev("sync", [("d_ws0", 16 * 12)], "d_tw",
           lambda e, cj=cj: e.dma_start_transpose(
               bass.AP(wkbuf, (cj - 4) * 1024, [[wkbuf.shape[1], 128], [1, 1024]]),
               bass.AP(wspill, 8 * 128 * D + cj * 128, [[D, 1024], [1, 128]])), 16)
    for cj in range(4):
        ev("sync", [("t_ctx", m_ctx[NIT - 5]), ("d_ws0", 16 * 12)], "d_tw",
           lambda e, cj=cj: e.dma_start_transpose(
               arena_ap(8192 + cj * 1024, [[1, 1024]]),
               bass.AP(wspill, 8 * 128 * D + cj * 128, [[D, 1024], [1, 128]])), 16)
    for wi, gate in [(0, ("t_ffn", 1)), (2, ("t_ffn", 9))]:
        for cj in range(NC):
            gw = [("t_ctx", total_ctx), ("v_du", total_du), ("t_st", total_st),
                  ("d_ws0", 16 * 12), gate]
            ev("sync", gw, "d_tw",
               lambda e, wi=wi, cj=cj: e.dma_start_transpose(
                   arena_ap(wi * 8192 + cj * 1024, [[1, 1024]]),
                   bass.AP(wspill, wi * 8 * 128 * D + cj * 128, [[D, 1024], [1, 128]])), 16)

    ev("scalar", [("v_self", cnt["v_self"])], "s_sd",
       lambda e: e.activation(stats[:, 0:QT],
                              bass.AP(fstats, 1, [[36, 128], [2, QT]]),
                              AF.Sqrt, bias=csts[:, 0:1]))
    ev("vector", [("s_sd", cnt["s_sd"])], "v_self",
       lambda e: e.reciprocal(bass.AP(fstats, 1, [[36, 128], [2, QT]]),
                              stats[:, 0:QT]))
    fstats_done = cnt["v_self"]

    kact = scratch[:, 8192:10240]
    rsb = arena[:, 24576:25600]
    prevbuf = arena[:, 25600:26624]
    relu_b = arena[:, 26624:27648]
    t1b = arena[:, 27648:28672]
    tsb = scratch_f[:, 1024:1536]

    ffn_mark = {}
    mix_mark = {}
    first_ffn_mm = [True]
    first_relu = [True]

    ln2_pend = {}

    ln2_par = [0]

    def ln2_chains(tc):
        tc0, wch = TCHUNKS[tc]
        ntiles = wch // 128
        t0 = tc0 // 128
        pend = []
        for tt in range(ntiles):
            ti = t0 + tt
            src2 = x_res[:, ti * 1024:(ti + 1) * 1024]
            par = ln2_par[0] % 2
            ln2_par[0] += 1
            dst = xn_sb[:, par * 1024:(par + 1) * 1024]
            n = 32 + ti
            nwaits = [("v_self", fstats_done), ("t_stw", xn_tr_mark[31])]
            if n - 2 in xn_tr_mark:
                nwaits.append(("t_stw", xn_tr_mark[n - 2]))
            ev("vector", nwaits, "s_xn",
               lambda e, dst=dst, src2=src2, ti=ti: e.tensor_scalar(
                   dst, src2, fstats[:, ti * 2:ti * 2 + 1],
                   fstats[:, ti * 2 + 1:ti * 2 + 2],
                   op0=OP.subtract, op1=OP.mult))
            pend.append((n, dst, cnt["s_xn"]))
        ln2_pend[tc] = pend

    def mix_part(tc, tiles=None):
        tc0, wch = TCHUNKS[tc]
        ntiles = wch // 128
        cp = tc % 2
        xr_sl = scratch[:, cp * 2048:(cp + 1) * 2048]
        xk_sl = scratch[:, 4096 + cp * 2048: 4096 + (cp + 1) * 2048]
        for tt in (range(ntiles) if tiles is None else tiles):
            n, dst, sxn_mark = ln2_pend[tc][tt]
            half = n % 2
            emit_tile_transposes(dst, half, identf[:],
                                 [("s_xn", sxn_mark), ("v_ctx", total_vctx)])
            xn_tr_mark[n] = cnt["t_stw"]
            cbase = half * 1024
            mw = [("t_stw", cnt["t_stw"]), ("v_mix", cnt["v_mix"])]
            if tc - 2 >= 0:
                mw.append(("t_ffn", ffn_mark[tc - 2]))
            ev_noinc("vector", mw, lambda e, cbase=cbase: e.tensor_copy(
                bass.AP(arena, 25600 + 1, [[A, 128], [128, NC], [1, 127]]),
                bass.AP(ctxP, cbase, [[2048, 128], [128, NC], [1, 127]])))
            ev("vector", [], "v_self", lambda e: e.tensor_copy(
                bass.AP(arena, 25600, [[A, 128], [128, NC]]), carry[:, 0:NC]))
            ev("vector", [("v_self", cnt["v_self"]), ("s_act", cnt["s_act"])], "v_self",
               lambda e, cbase=cbase: e.tensor_tensor(
                   t1b[:, 0:1024], ctxP[:, cbase:cbase + 1024],
                   prevbuf[:, 0:1024], op=OP.subtract))
            t1_chain = cnt["v_self"]
            ev("vector", [("v_self", t1_chain)], "c_stv", lambda e, cbase=cbase: e.tensor_copy(
                carry[:, 0:NC], bass.AP(ctxP, cbase + 127, [[2048, 128], [128, NC]])))
            tr_mark[half] = cnt["c_stv"]
            mixtmp = arena[:, 28672:29696]
            xr_off = cp * 2048 + tt * 128
            xk_off = 4096 + cp * 2048 + tt * 128
            ev("vector", [("v_self", t1_chain)], "v_self",
               lambda e: e.tensor_tensor(
                   mixtmp[:, 0:1024], t1b[:, 0:1024],
                   bass.AP(cvec, 24, [[40, 128], [1, 8], [0, 128]]),
                   op=OP.mult))
            ev("vector", [("v_self", cnt["v_self"])] , "v_self",
               lambda e, xr_off=xr_off: e.tensor_tensor(
                   bass.AP(scratch, xr_off, [[scratch.shape[1], 128], [256, NC], [1, 128]]),
                   mixtmp[:, 0:1024], prevbuf[:, 0:1024], op=OP.add))
            ev("vector", [("v_self", cnt["v_self"])], "v_self",
               lambda e: e.tensor_tensor(
                   mixtmp[:, 0:1024], t1b[:, 0:1024],
                   bass.AP(cvec, 32, [[40, 128], [1, 8], [0, 128]]),
                   op=OP.mult))
            ev("vector", [("v_self", cnt["v_self"])], "v_mix",
               lambda e, xk_off=xk_off: e.tensor_tensor(
                   bass.AP(scratch, xk_off, [[scratch.shape[1], 128], [256, NC], [1, 128]]),
                   mixtmp[:, 0:1024], prevbuf[:, 0:1024], op=OP.add))
        mix_mark[tc] = cnt["v_mix"]

    def ffn_matmuls(tc, after_k=None, after_tile=None):
        tc0, wch = TCHUNKS[tc]
        ntiles = wch // 128
        t0 = tc0 // 128
        cp = tc % 2
        xr_sl = scratch[:, cp * 2048:(cp + 1) * 2048]
        xk_sl = scratch[:, 4096 + cp * 2048: 4096 + (cp + 1) * 2048]
        for dj in range(NC):
            si = next_stsub()
            ps = stsub[si]
            tw = [("v_mix", mix_mark[tc]), (stsub_mark[si][0], stsub_mark[si][1])]
            if first_ffn_mm[0]:
                tw.append(("d_tw", 16 * 8))
                first_ffn_mm[0] = False

            def op_k(e, ps=ps, dj=dj, wch=wch, xk_sl=xk_sl):
                r = None
                for cj in range(NC):
                    if cj < 4:
                        wsrc2 = arena_ap((8 + cj) * 1024 + dj * 128, [[1, 128]])
                    else:
                        wsrc2 = bass.AP(wkbuf, (cj - 4) * 1024 + dj * 128,
                                        [[wkbuf.shape[1], 128], [1, 128]])
                    r = e.matmul(ps[:, 0:wch], wsrc2,
                                 xk_sl[:, cj * 256: cj * 256 + wch],
                                 start=(cj == 0), stop=(cj == NC - 1))
                return r
            ev("tensor", tw, "t_ffn", op_k)
            sw = [("t_ffn", cnt["t_ffn"]), ("v_sq", max(0, cnt["v_sq"] - 1))]
            if first_relu[0]:
                sw.append(("t_ctx", total_ctx))
                first_relu[0] = False
            pj = dj % 2
            ev("scalar", sw, "s_act",
               lambda e, ps=ps, wch=wch, pj=pj: e.activation(
                   relu_b[:, pj * 512: pj * 512 + wch], ps[:, 0:wch], AF.Relu))
            stsub_mark[si] = ("s_act", cnt["s_act"])
            ev("vector", [("s_act", cnt["s_act"])], "v_sq",
               lambda e, pj=pj, dj=dj, wch=wch: e.tensor_tensor(
                   kact[:, dj * 256: dj * 256 + wch],
                   relu_b[:, pj * 512: pj * 512 + wch],
                   relu_b[:, pj * 512: pj * 512 + wch], op=OP.mult))
        ksq_done = cnt["v_sq"]
        if after_k is not None:
            after_k()

        for tt in range(ntiles):
            if after_tile is not None and tt > 0:
                after_tile(tt - 1)
            ti = t0 + tt
            fin_before = cnt["v_fin"]
            for ds in range(2):
                si = next_stsub()
                ps = stsub[si]
                tw = [(stsub_mark[si][0], stsub_mark[si][1])]
                if tt == 0 and ds == 0:
                    tw.append(("d_tw", 16 * 16))

                def op_r(e, ps=ps, tt=tt, ds=ds, xr_sl=xr_sl):
                    r = None
                    for cj in range(NC):
                        r = e.matmul(ps[:, 0:512],
                                     xr_sl[:, cj * 256 + tt * 128: cj * 256 + (tt + 1) * 128],
                                     arena_ap(cj * 1024 + ds * 512, [[1, 512]]),
                                     start=(cj == 0), stop=(cj == NC - 1))
                    return r
                ev("tensor", tw, "t_ffn", op_r)
                ev("scalar", [("t_ffn", cnt["t_ffn"]), ("v_fin", fin_before)], "s_act",
                   lambda e, ps=ps, ds=ds: e.activation(
                       rsb[:, ds * 512:(ds + 1) * 512], ps[:, 0:512], AF.Sigmoid))
                stsub_mark[si] = ("s_act", cnt["s_act"])
            sig_done = cnt["s_act"]
            for ds in range(2):
                si = next_stsub()
                ps = stsub[si]
                tw = [("v_sq", ksq_done), (stsub_mark[si][0], stsub_mark[si][1])]
                if tt == 0 and ds == 0:
                    tw.append(("d_tw", 16 * 24))

                def op_v(e, ps=ps, tt=tt, ds=ds):
                    r = None
                    for cj in range(NC):
                        r = e.matmul(ps[:, 0:512],
                                     kact[:, cj * 256 + tt * 128: cj * 256 + (tt + 1) * 128],
                                     arena_ap((16 + cj) * 1024 + ds * 512, [[1, 512]]),
                                     start=(cj == 0), stop=(cj == NC - 1))
                    return r
                ev("tensor", tw, "t_ffn", op_v)
                ev("vector", [("t_ffn", cnt["t_ffn"]), ("s_act", sig_done),
                              ("v_self", cnt["v_self"])], "v_self",
                   lambda e, ps=ps, ds=ds: e.tensor_tensor(
                       tsb[:, 0:512], ps[:, 0:512],
                       rsb[:, ds * 512:(ds + 1) * 512], op=OP.mult))
                dst = x_res[:, ti * 1024 + ds * 512: ti * 1024 + ds * 512 + 512]
                is_last = (ds == 1)
                ev("vector", [("v_self", cnt["v_self"])],
                   "v_fin" if is_last else "v_self",
                   lambda e, dst=dst: e.tensor_tensor(dst, dst, tsb[:, 0:512], op=OP.add))
                stsub_mark[si] = ("v_fin", cnt["v_fin"] if is_last else cnt["v_fin"] + 1)
        ffn_mark[tc] = cnt["t_ffn"]

        ev("sync", [("v_fin", cnt["v_fin"])], "d_out",
           lambda e, t0=t0, ntiles=ntiles: e.dma_start(
               bass.AP(out_h, t0 * 128 * 1024,
                       [[1024, 128], [128 * 1024, ntiles], [1, 1024]]),
               ap3(x_res, t0 * 1024, [[1024, ntiles], [1, 1024]])), 16)

    emit_warmup(48, [("s_cpx", total_cpx)])
    ln2_chains(0)
    mix_part(0)
    for tc in range(len(TCHUNKS)):
        if tc + 1 < len(TCHUNKS):
            nt_next = TCHUNKS[tc + 1][1] // 128
            ln2_chains(tc + 1)
            ffn_matmuls(tc,
                        after_k=lambda tc=tc: mix_part(tc + 1, [0]),
                        after_tile=lambda tt, tc=tc, n=nt_next:
                            mix_part(tc + 1, [tt + 1]) if tt + 1 < n else None)
            if nt_next > TCHUNKS[tc][1] // 128:
                mix_part(tc + 1, list(range(TCHUNKS[tc][1] // 128, nt_next)))
        else:
            ffn_matmuls(tc)

    ev_noinc("sync", [("d_out", cnt["d_out"])], lambda e: e.nop())

    with nc.Block() as block:
        @block.sync
        def _(e):
            for fn in sched["sync"]:
                fn(e)

        @block.gpsimd
        def _(e):
            for fn in sched["gpsimd"]:
                fn(e)

        @block.tensor
        def _(e):
            for fn in sched["tensor"]:
                fn(e)

        @block.scalar
        def _(e):
            for fn in sched["scalar"]:
                fn(e)

        @block.vector
        def _(e):
            for fn in sched["vector"]:
                fn(e)

    es.close()
    return nc


def _in_maps(inputs):
    x = np.asarray(inputs["L0_attn_hidden_states"], dtype=np.float32)
    cvec = np.empty((128, 40), dtype=np.float32)
    for i, k in enumerate(["L0_attn_bq", "L0_attn_bk", "L0_attn_bv",
                           "L0_ffn_mu_r", "L0_ffn_mu_k"]):
        cvec[:, i * 8:(i + 1) * 8] = np.asarray(inputs[k], np.float32).reshape(8, 128).T
    ident = np.eye(128, dtype=ml_dtypes.bfloat16)
    identf = np.eye(128, dtype=np.float32)
    base = {
        "wq": np.asarray(inputs["L0_attn_Wq"], np.float32),
        "wk": np.asarray(inputs["L0_attn_Wk"], np.float32),
        "wv": np.asarray(inputs["L0_attn_Wv"], np.float32),
        "wr2": np.asarray(inputs["L0_ffn_W_r"], np.float32),
        "wk2": np.asarray(inputs["L0_ffn_W_k"], np.float32),
        "wv2": np.asarray(inputs["L0_ffn_W_v"], np.float32),
        "cvec": cvec, "ident": ident, "identf": identf,
    }
    maps = []
    for c in range(8):
        bi, half = c // 2, c % 2
        xb = x[bi] if half == 0 else np.roll(x[bi], -1920, axis=0)
        m = dict(base)
        m["x"] = np.ascontiguousarray(xb)
        maps.append(m)
    return maps


def kernel(**inputs):
    maps = _in_maps(inputs)
    nc = build_nc()
    res = run_bass_kernel_spmd(nc, maps, core_ids=list(range(8)))
    out = np.empty((4, 4096, 1024), dtype=np.float32)
    for c in range(8):
        bi, half = c // 2, c % 2
        o = np.asarray(res.results[c]["out"])
        if half == 0:
            out[bi, 0:2048] = o[0:2048]
        else:
            out[bi, 2048:4096] = o[128:2176]
    return out


# version 45
# speedup vs baseline: 1.0060x; 1.0025x over previous
import numpy as np
import ml_dtypes
from contextlib import ExitStack

import concourse.bass as bass
import concourse.mybir as mybir
from concourse.bass_utils import run_bass_kernel_spmd

F32 = mybir.dt.float32
BF16 = mybir.dt.bfloat16
AF = mybir.ActivationFunctionType
OP = mybir.AluOpType

D = 1024
NC = 8
NH = 8
T = 4096
NT = 32
QT = 17
NQ = QT * 128
EPS = 1e-5
A_SLICES = [(i * 256, 256) for i in range(8)] + [(2048, 128)]
Q_PROJ_SLICES = [(0, 512), (512, 512), (1024, 512), (1536, 512), (2048, 128)]
TCHUNKS = [(i * 256, 256) for i in range(8)] + [(2048, 128)]

SEM_NAMES = ("d_x", "d_x2", "d_x3", "d_cst", "d_xt0", "d_xt1", "d_xt2", "d_xt3", "d_wa", "d_spk", "d_spv", "d_spq",
             "d_q0", "d_q1", "d_k0", "d_k1", "d_v0", "d_v1", "d_wf0", "d_wf1",
             "d_out", "v_ln", "s_sd", "s_xn", "t_stw", "c_stv", "s_exp", "v_du",
             "t_proj", "v_proj", "t_st", "t_ctx", "v_ctx", "s_cpx", "t_ffn", "s_act",
             "v_sq", "v_mix", "v_fin", "v_self", "v_den", "s_rcp", "s_self", "g_rcp",
             "s_cpy", "d_ws0", "d_tw")


def build_nc():
    nc = bass.Bass("TRN2", target_bir_lowering=False)

    x_h = nc.dram_tensor("x", [T, D], F32, kind="ExternalInput")
    wq_h = nc.dram_tensor("wq", [D, D], F32, kind="ExternalInput")
    wk_h = nc.dram_tensor("wk", [D, D], F32, kind="ExternalInput")
    wv_h = nc.dram_tensor("wv", [D, D], F32, kind="ExternalInput")
    wr2_h = nc.dram_tensor("wr2", [D, D], F32, kind="ExternalInput")
    wk2_h = nc.dram_tensor("wk2", [D, D], F32, kind="ExternalInput")
    wv2_h = nc.dram_tensor("wv2", [D, D], F32, kind="ExternalInput")
    cvec_h = nc.dram_tensor("cvec", [128, 40], F32, kind="ExternalInput")
    ident_h = nc.dram_tensor("ident", [128, 128], BF16, kind="ExternalInput")
    identf_h = nc.dram_tensor("identf", [128, 128], F32, kind="ExternalInput")
    out_h = nc.dram_tensor("out", [NQ, D], F32, kind="ExternalOutput")
    wspill = nc.dram_tensor("wspill", [3 * D, D], BF16)
    kspill = nc.dram_tensor("kspill", [128, NH * T], BF16)
    vtspill = nc.dram_tensor("vtspill", [128, NH * T], BF16)
    qspill = nc.dram_tensor("qspill", [128, NH * NQ], BF16)

    es = ExitStack()

    def sbuf(name, cols, dt):
        return es.enter_context(nc.sbuf_tensor(name, [128, cols], dt))

    def psum(name, cols, dt):
        return es.enter_context(nc.psum_tensor(name, [128, cols], dt))

    x_res = sbuf("x_res", QT * 1024, F32)
    xtmp = sbuf("xtmp", 4 * 1024, BF16)
    arena = sbuf("arena", 32768, BF16)
    xn_sb = sbuf("xn_sb", 2 * 1024, F32)
    scratch = sbuf("scratch", 10368, BF16)
    scratch_f = sbuf("scratch_f", 1536, F32)
    ctxs = sbuf("ctxs", 2048, F32)
    wkbuf = sbuf("wkbuf", 4096, BF16)
    wa = sbuf("wa", 6144, BF16)
    cvec = sbuf("cvecsb", 40, F32)
    ident = sbuf("identsb", 128, BF16)
    identf = sbuf("identfsb", 128, F32)
    stats = sbuf("stats", 72, F32)
    csts = sbuf("csts", 2, F32)
    fstats = sbuf("fstats", 36, F32)
    carry = sbuf("carry", 8, BF16)

    stAB = psum("stAB", 2048, F32)
    ctxP = psum("ctxP", 2048, F32)

    sems = {n: es.enter_context(nc.semaphore(n)) for n in SEM_NAMES}
    sched = {"tensor": [], "vector": [], "scalar": [], "sync": [], "gpsimd": []}
    cnt = {n: 0 for n in SEM_NAMES}

    def ev(eng, waits, inc, op, amt=1):
        waits = [(sems[n], v) for (n, v) in waits if v > 0]
        incsem = sems[inc] if inc else None

        def fn(e, waits=tuple(waits), incsem=incsem, op=op, amt=amt):
            for s, v in waits:
                e.wait_ge(s, v)
            r = op(e)
            if incsem is not None:
                r.then_inc(incsem, amt)
        sched[eng].append(fn)
        if inc:
            cnt[inc] += amt

    def ev_noinc(eng, waits, op):
        ev(eng, waits, None, op)

    def drain(eng):
        sched[eng].append(lambda e: e.drain())

    def ap3(t_sb, offset, dims):
        return bass.AP(t_sb, offset, [[t_sb.shape[1], 128]] + dims)

    A = arena.shape[1]

    def arena_ap(offset, dims):
        return bass.AP(arena, offset, [[A, 128]] + dims)

    stsub = [stAB[:, i * 512:(i + 1) * 512] for i in range(4)]
    stsub_mark = [("d_cst", 0)] * 4
    stsub_rot = [0]

    def next_stsub():
        i = stsub_rot[0] % 4
        stsub_rot[0] += 1
        return i

    tr_mark = [0, 0]
    xn_tr_mark = {}
    ln_done = [0]

    ev("sync", [], "d_cst", lambda e: e.dma_start(cvec[:], cvec_h.ap()), 16)
    ev("sync", [], "d_cst", lambda e: e.dma_start(ident[:], ident_h.ap()), 16)
    ev("sync", [], "d_cst", lambda e: e.dma_start(identf[:], identf_h.ap()), 16)
    x_tiled = x_h.ap().rearrange("(t p) c -> p t c", p=128)
    for sem_x, (c0, c1) in (("d_x", (0, 4)), ("d_x2", (4, 10)), ("d_x3", (10, QT))):
        ev("sync", [], sem_x,
           lambda e, c0=c0, c1=c1: e.dma_start(
               ap3(x_res, c0 * 1024, [[1024, c1 - c0], [1, 1024]]),
               x_tiled[:, c0:c1, :]), 16)
    ev_noinc("vector", [], lambda e: e.memset(carry[:, 0:NC], 0.0))
    ev_noinc("vector", [], lambda e: e.memset(csts[:, 0:1], EPS))
    ev_noinc("vector", [], lambda e: e.memset(csts[:, 1:2], -1.0))

    def op_warm0(e):
        r = None
        for _ in range(48):
            r = e.matmul(ctxP[:, 0:128], ident[:], ident[:], start=True, stop=True)
        return r
    ev("tensor", [("d_cst", 48)], None, op_warm0)

    def emit_xtmp_dma(t):
        par = t % 4
        waits = [("s_xn", t - 3)] if t - 4 >= QT else []
        ev("gpsimd", waits, "d_xt%d" % par,
           lambda e, par=par, t=t: e.dma_start(
               xtmp[:, par * 1024:(par + 1) * 1024], x_tiled[:, t, :]), 16)

    def emit_wa_dma(g, wi, w_h):
        src = w_h.ap().rearrange("(cj p) d -> p cj d", p=128)[:, :, g * 128:(g + 1) * 128]
        if g >= 2:
            waits = [("t_proj", 21 * (g - 1) + 1)]
        elif g == 1:
            waits = [("t_proj", 1)]
        elif wi > 0:
            waits = [("d_x", 16)]
        else:
            waits = []
        ev("gpsimd", waits, "d_wa",
           lambda e, g=g, wi=wi, src=src: e.dma_start(
               bass.AP(wa, (g % 2) * 3072 + wi * 1024, [[wa.shape[1], 128], [128, NC], [1, 128]]),
               src), 16)

    emit_xtmp_dma(17)
    emit_xtmp_dma(18)
    for wi, w_h in enumerate([wq_h, wk_h, wv_h]):
        emit_wa_dma(0, wi, w_h)
    for t in range(19, NT):
        emit_xtmp_dma(t)

    def emit_ln(src_ap, src_wait, dst_ap, extra_norm_wait):
        n = ln_done[0]
        par = n % 4
        so = par * 18
        bn12 = stats[:, so:so + 12]
        mv = stats[:, so + 12:so + 14]
        sd = stats[:, so + 14:so + 15]
        inv = stats[:, so + 15:so + 16]
        waits = list(src_wait)
        if n - 4 >= 0:
            waits.append(("s_sd", n - 3))
        ev_noinc("vector", waits, lambda e: e.bn_stats(bn12[:, 0:6], src_ap[:, 0:512]))
        ev("vector", [], "v_self", lambda e: e.bn_stats(bn12[:, 6:12], src_ap[:, 512:1024]))
        ev("vector", [("v_self", cnt["v_self"])], "v_ln", lambda e: e.bn_aggr(mv, bn12))
        ev("scalar", [("v_ln", cnt["v_ln"])], "s_sd",
           lambda e: e.activation(sd, mv[:, 1:2], AF.Sqrt, bias=csts[:, 0:1]))
        ev("vector", [("s_sd", cnt["s_sd"])], "v_self", lambda e: e.reciprocal(inv, sd))
        nwaits = list(extra_norm_wait) + [("v_self", cnt["v_self"])]
        if n - 2 in xn_tr_mark:
            nwaits.append(("t_stw", xn_tr_mark[n - 2]))
        ev("vector", nwaits, "s_xn",
           lambda e: e.tensor_scalar(dst_ap, src_ap, mv[:, 0:1], inv,
                                     op0=OP.subtract, op1=OP.mult))
        ln_done[0] += 1
        return n

    def emit_tile_transposes(src_ap, half, idf, extra_waits=()):
        tw = [("c_stv", tr_mark[half])] + list(extra_waits)

        def op(e, src_ap=src_ap, half=half, idf=idf):
            r = None
            for j in range(NC):
                r = e.transpose(ctxP[:, half * 1024 + j * 128: half * 1024 + (j + 1) * 128],
                                src_ap[:, j * 128:(j + 1) * 128], idf)
            return r
        ev("tensor", tw, "t_stw", op)

    kg = scratch[:, 0:4096]
    vtb = scratch[:, 4096:8192]
    qg = scratch[:, 8192:10368]
    vsb = scratch_f[:, 0:1024]
    tile_evac_mark = {}
    vsb_tr_mark = [0, 0]

    def emit_proj_group(g, kind, dst_ap, s0, w, bias_col, wi, spill_wait=()):
        si = next_stsub()
        ps = stsub[si]
        req = (s0 + w - 1) // 128
        tw = [("d_wa", 48 * (g + 1)),
              (stsub_mark[si][0], stsub_mark[si][1]),
              ("c_stv", tile_evac_mark[req])]

        def op(e, ps=ps, s0=s0, w=w, wi=wi, g=g):
            r = None
            for cj in range(NC):
                r = e.matmul(ps[:, 0:w],
                             wa[:, (g % 2) * 3072 + wi * 1024 + cj * 128:
                                 (g % 2) * 3072 + wi * 1024 + (cj + 1) * 128],
                             arena_ap(cj * 4096 + s0, [[1, w]]),
                             start=(cj == 0), stop=(cj == NC - 1))
            return r
        ev("tensor", tw, "t_proj", op)
        vw = [("t_proj", cnt["t_proj"])] + list(spill_wait)
        if kind == "v":
            par2 = dst_ap
            vw.append(("t_stw", vsb_tr_mark[par2]))
            dst = vsb[:, par2 * 512:(par2 + 1) * 512]

            def opv(e, dst=dst, ps=ps, w=w, bias_col=bias_col):
                return e.tensor_scalar(dst, ps[:, 0:w],
                                       cvec[:, bias_col:bias_col + 1],
                                       None, op0=OP.add)
            ev("vector", vw + [("d_cst", 48)], "v_proj", opv)
            stsub_mark[si] = ("v_proj", cnt["v_proj"])
            return
        dst = dst_ap

        def opc(e, dst=dst, ps=ps, w=w):
            return e.activation(dst, ps[:, 0:w], AF.Copy)
        ev("scalar", vw, "s_cpy", opc)
        stsub_mark[si] = ("s_cpy", cnt["s_cpy"])
        ev("vector", [("s_cpy", cnt["s_cpy"]), ("d_cst", 48)], "v_proj",
           lambda e, dst=dst, bias_col=bias_col: e.tensor_scalar(
               dst, dst, cvec[:, bias_col:bias_col + 1], None, op0=OP.add))

    def emit_v_post(g, vs, vp_mark=None):
        par2 = vs % 2
        half = vs % 2
        tw = [vp_mark or ("v_proj", cnt["v_proj"]), ("c_stv", tr_mark[half])]

        def op_trv(e, par2=par2, half=half):
            r = None
            for j in range(4):
                r = e.transpose(
                    ctxP[:, half * 1024 + j * 128: half * 1024 + (j + 1) * 128],
                    vsb[:, par2 * 512 + j * 128: par2 * 512 + (j + 1) * 128],
                    identf[:])
            return r
        ev("tensor", tw, "t_stw", op_trv)
        vsb_tr_mark[par2] = cnt["t_stw"]
        ev("scalar", [("t_stw", cnt["t_stw"]), ("d_spv", 16 * g)], "c_stv",
           lambda e, vs=vs, half=half: e.activation(
               vtb[:, vs * 512:(vs + 1) * 512],
               ctxP[:, half * 1024: half * 1024 + 512], AF.Copy))
        tr_mark[half] = cnt["c_stv"]

    def emit_spills(g):
        ev("sync", [("v_proj", cnt["v_proj"])], "d_spq",
           lambda e, g=g: e.dma_start(
               bass.AP(qspill, g * NQ, [[NH * NQ, 128], [1, NQ]]), qg[:, 0:NQ]), 16)
        ev("sync", [("v_proj", cnt["v_proj"])], "d_spk",
           lambda e, g=g: e.dma_start(
               bass.AP(kspill, g * T, [[NH * T, 128], [1, T]]), kg[:, 0:T]), 16)
        ev("sync", [("c_stv", cnt["c_stv"])], "d_spv",
           lambda e, g=g: e.dma_start(
               bass.AP(vtspill, g * T, [[NH * T, 128], [1, T]]), vtb[:, 0:T]), 16)

    pending_vpost = []

    def drain_vpost():
        while pending_vpost:
            pending_vpost.pop(0)()

    def run_job(fn):
        n0 = len(pending_vpost)
        fn()
        for _ in range(min(n0, len(pending_vpost))):
            pending_vpost.pop(0)()

    def v_job(g, vs):
        emit_proj_group(g, "v", vs % 2, vs * 512, 512, 16 + g, 2,
                        spill_wait=[("d_spv", 16 * g)])
        vp_mark = ("v_proj", cnt["v_proj"])
        pending_vpost.append(lambda g=g, vs=vs, vp_mark=vp_mark:
                             emit_v_post(g, vs, vp_mark))

    jobs = []
    for g in range(NH):
        if g + 1 < NH:
            jobs.append((0, lambda g=g: [emit_wa_dma(g + 1, wi, w_h)
                                         for wi, w_h in enumerate([wq_h, wk_h, wv_h])]))
        for i in range(8):
            if i < len(Q_PROJ_SLICES):
                s0, w = Q_PROJ_SLICES[i]
                jobs.append(((s0 + w - 1) // 128,
                             lambda g=g, s0=s0, w=w: emit_proj_group(
                                 g, "q", qg[:, s0:s0 + w], s0, w, 0 + g, 0,
                                 spill_wait=[("d_spq", 16 * g)])))
            jobs.append((4 * i + 3,
                         lambda g=g, ks=i: emit_proj_group(
                             g, "k", kg[:, ks * 512:(ks + 1) * 512], ks * 512, 512,
                             8 + g, 1, spill_wait=[("d_spk", 16 * g)])))
            jobs.append((4 * i + 3, lambda g=g, vs=i: v_job(g, vs)))
        jobs.append((31, lambda g=g: (drain_vpost(), emit_spills(g))))
    jobs.reverse()

    ln_norm_mark = {}
    ln_dst = {}

    def emit_tr_evac(t):
        dst = ln_dst[t]
        half = t % 2
        extra = [("d_cst", 48), ("s_xn", ln_norm_mark[t])]
        emit_tile_transposes(dst, half, identf[:], extra)
        xn_tr_mark[t] = cnt["t_stw"]
        ev("scalar", [("t_stw", cnt["t_stw"])], "c_stv",
           lambda e, t=t, half=half: e.activation(
               arena_ap(t * 128, [[4096, NC], [1, 128]]),
               ctxP[:, half * 1024: half * 1024 + 1024], AF.Copy))
        tr_mark[half] = cnt["c_stv"]
        tile_evac_mark[t] = cnt["c_stv"]

    def emit_ln_tile(t):
        par = t % 2
        if t < QT:
            src2 = x_res[:, t * 1024:(t + 1) * 1024]
            src_wait = [("d_x" if t < 4 else ("d_x2" if t < 10 else "d_x3"), 16)]
        else:
            par4 = t % 4
            src2 = xtmp[:, par4 * 1024:(par4 + 1) * 1024]
            src_wait = [("d_xt%d" % par4, 16 * ((t - 17) // 4 + 1))]
        dst = xn_sb[:, par * 1024:(par + 1) * 1024]
        emit_ln(src2, src_wait, dst, [])
        ln_norm_mark[t] = cnt["s_xn"]
        ln_dst[t] = dst

    def emit_wcast(ci):
        wi, q4 = ci // 4, ci % 4
        w_h2 = [wr2_h, wk2_h, wv2_h][wi]
        ev("gpsimd", [], "d_ws0",
           lambda e, wi=wi, q4=q4, w_h2=w_h2: e.dma_start(
               bass.AP(wspill, (wi * 1024 + q4 * 256) * D, [[D, 256], [1, D]]),
               bass.AP(w_h2, q4 * 256 * D, [[D, 256], [1, D]])), 16)

    emit_ln_tile(0)
    emit_ln_tile(1)
    for t in range(1, NT):
        emit_tr_evac(t - 1)
        if t + 1 < NT:
            emit_ln_tile(t + 1)
        if t >= 8 and t % 2 == 0 and (t - 8) // 2 < 12:
            emit_wcast((t - 8) // 2)
        done = 0
        while jobs and jobs[-1][0] <= t - 1 and done < 8:
            run_job(jobs.pop()[1])
            done += 1
    emit_tr_evac(NT - 1)
    while jobs:
        run_job(jobs.pop()[1])
    drain_vpost()

    all_proj = cnt["t_proj"]
    all_proj_v = cnt["v_proj"]
    mark_vt_all = cnt["c_stv"]

    KQ0, VQ0, QS0, EA0, UB0 = 0, 8192, 16384, 18432, 26624
    d1 = scratch[:, 0:1024]
    d2 = scratch[:, 1024:1536]
    den = scratch_f[:, 0:512]
    lbuf = scratch_f[:, 512:768]
    rrc = scratch[:, 1536:2560]

    def emit_warmup(n_mm, waits):
        def op(e):
            r = None
            for i in range(n_mm):
                r = e.matmul(ctxP[:, 0:128], ident[:], ident[:],
                             start=True, stop=True)
            return r
        ev("tensor", waits, None, op)

    emit_warmup(48, [("c_stv", mark_vt_all), ("d_cst", 48)])

    iters = []
    for sl, (s0, w) in enumerate(A_SLICES):
        for quad in range(8):
            for mcq in range(4):
                iters.append((sl, quad, mcq, s0, w))
    NIT = len(iters)

    m_stA, m_stB, m_expA, m_expB = {}, {}, {}, {}
    m_L3, m_ln, m_exp2, m_U, m_ctx = {}, {}, {}, {}, {}
    k_load_mark, v_load_mark, q_load_mark = {}, {}, {}
    last_vec = [None]
    last_sc = [None]

    def vec_op(waits, inc, op):
        wts = list(waits)
        if last_vec[0] is not None:
            wts.append(last_vec[0])
        ev("vector", wts, inc, op)
        last_vec[0] = (inc, cnt[inc])

    def sc_op(waits, inc, op):
        wts = list(waits)
        if last_sc[0] is not None:
            wts.append(last_sc[0])
        ev("scalar", wts, inc, op)
        last_sc[0] = (inc, cnt[inc])

    def stage_loads(i):
        sl, quad, mcq, s0, w = iters[i]
        if mcq != 0:
            return
        par_k = quad % 2
        if quad == 0:
            par_q = sl % 2
            qbase = QS0 if par_q == 0 else 30720
            qw = [("d_spq", 16 * NH)]
            if sl >= 2:
                qw.append(("t_st", m_stB[(sl - 1) * 32 - 1]))
            ev("gpsimd", qw, "d_q%d" % par_q,
               lambda e, s0=s0, w=w, qbase=qbase: e.dma_start(
                   arena_ap(qbase, [[w, NH], [1, w]]),
                   bass.AP(qspill, s0, [[NH * NQ, 128], [NQ, NH], [1, w]])), 16)
            q_load_mark[sl] = ("d_q%d" % par_q, 16 * (sl // 2 + 1))
        gi0 = sl * 32 + quad * 4
        kw = [("d_spk", 16 * NH)]
        vw = [("d_spv", 16 * NH)]
        if gi0 - 8 >= 0:
            kw.append(("t_st", m_stB[gi0 - 8 + 3]))
            vw.append(("t_ctx", m_ctx[gi0 - 8 + 3]))
        ev("sync", kw, "d_k%d" % par_k,
           lambda e, quad=quad, par_k=par_k: e.dma_start(
               arena_ap(KQ0 + par_k * 4096, [[512, NH], [1, 512]]),
               bass.AP(kspill, quad * 512, [[NH * T, 128], [T, NH], [1, 512]])), 16)
        ev("gpsimd", vw, "d_v%d" % par_k,
           lambda e, quad=quad, par_k=par_k: e.dma_start(
               arena_ap(VQ0 + par_k * 4096, [[512, NH], [1, 512]]),
               bass.AP(vtspill, quad * 512, [[NH * T, 128], [T, NH], [1, 512]])), 16)
        kq_count = (sl * 8 + quad) // 2 + 1
        k_load_mark[(sl, quad)] = ("d_k%d" % par_k, 16 * kq_count)
        v_load_mark[(sl, quad)] = ("d_v%d" % par_k, 16 * kq_count)

    def stage_st(i):
        sl, quad, mcq, s0, w = iters[i]
        par_k = quad % 2
        for hf in range(2):
            tw = [k_load_mark[(sl, quad)], q_load_mark[sl]]
            if i - 1 >= 0:
                tw.append(("s_exp", (m_expA if hf == 0 else m_expB)[i - 1]))
            if i == 0:
                tw.append(("v_proj", all_proj_v))

            qbase2 = QS0 if sl % 2 == 0 else 30720

            def op_st(e, w=w, par_k=par_k, mcq=mcq, hf=hf, qbase2=qbase2):
                r = None
                for g in range(hf * 4, hf * 4 + 4):
                    r = e.matmul(
                        stAB[:, g * w:(g + 1) * w],
                        arena_ap(KQ0 + par_k * 4096 + g * 512 + mcq * 128, [[1, 128]]),
                        arena_ap(qbase2 + g * w, [[1, w]]),
                        start=True, stop=True)
                return r
            ev("tensor", tw, "t_st", op_st)
            if hf == 0:
                m_stA[i] = cnt["t_st"]
            else:
                m_stB[i] = cnt["t_st"]

    def e_ap(par, off, dims):
        if par < 4:
            return arena_ap(EA0 + par * 2048 + off, dims)
        return bass.AP(scratch, 2560 + (par - 4) * 2048 + off,
                       [[scratch.shape[1], 128]] + dims)

    def stage_expA(i):
        sl, quad, mcq, s0, w = iters[i]
        par_e = i % 6
        sw = [("t_st", m_stA[i])]
        if i - 6 >= 0:
            sw.append(("v_du", m_U[i - 6]))
        if i in (4, 5):
            sw.append(("d_spv", 16 * NH))

        def op_expA(e, w=w, par_e=par_e):
            return e.activation(
                e_ap(par_e, 0, [[1, 4 * w]]),
                stAB[:, 0:4 * w], AF.Exp, scale=1.0 / 32.0)
        ev("scalar", sw, "s_exp", op_expA)
        m_expA[i] = cnt["s_exp"]

    def stage_expB(i):
        sl, quad, mcq, s0, w = iters[i]
        par_e = i % 6

        def op_expB(e, w=w, par_e=par_e):
            return e.activation(
                e_ap(par_e, 4 * w, [[1, 4 * w]]),
                stAB[:, 4 * w:NH * w], AF.Exp, scale=1.0 / 32.0)
        ev("scalar", [("t_st", m_stB[i]), ("s_exp", m_expA[i])], "s_exp", op_expB)
        last_sc[0] = ("s_exp", cnt["s_exp"])
        m_expB[i] = cnt["s_exp"]

    def stage_den(i):
        sl, quad, mcq, s0, w = iters[i]
        par_e = i % 6
        par_d = i % 2

        def op_d1(e, w=w, par_e=par_e):
            return e.tensor_tensor(d1[:, 0:4 * w],
                                   e_ap(par_e, 0, [[1, 4 * w]]),
                                   e_ap(par_e, 4 * w, [[1, 4 * w]]),
                                   op=OP.add)
        vec_op([("s_exp", m_expB[i])], "v_self", op_d1)
        vec_op([], "v_self", lambda e, w=w: e.tensor_tensor(
            d2[:, 0:2 * w], d1[:, 0:2 * w], d1[:, 2 * w:4 * w], op=OP.add))
        w3 = [("s_self", m_ln[i - 2])] if i - 2 >= 0 else []
        vec_op(w3, "v_den", lambda e, w=w, par_d=par_d: e.tensor_tensor(
            den[:, par_d * 256: par_d * 256 + w], d2[:, 0:w], d2[:, w:2 * w], op=OP.add))
        m_L3[i] = cnt["v_den"]

    def stage_lnexp(i):
        sl, quad, mcq, s0, w = iters[i]
        par_d = i % 2
        par_r = i % 4
        lw = [("v_den", m_L3[i])]
        if i - 1 >= 0:
            lw.append(("s_rcp", m_exp2[i - 1]))
        ev("scalar", lw, "s_self",
           lambda e, w=w, par_d=par_d: e.activation(
               lbuf[:, 0:w], den[:, par_d * 256: par_d * 256 + w], AF.Ln))
        last_sc[0] = ("s_self", cnt["s_self"])
        m_ln[i] = cnt["s_self"]
        w2 = [("v_du", m_U[i - 4])] if i - 4 >= 0 else []
        sc_op(w2, "s_rcp",
              lambda e, w=w, par_r=par_r: e.activation(
                  rrc[:, par_r * 256: par_r * 256 + w], lbuf[:, 0:w], AF.Exp,
                  scale=-1.0))
        m_exp2[i] = cnt["s_rcp"]

    def stage_U(i):
        sl, quad, mcq, s0, w = iters[i]
        par_e = i % 6
        par_u = i % 2
        par_r = i % 4
        wts = [("s_rcp", m_exp2[i])]
        if i - 2 >= 0:
            wts.append(("t_ctx", m_ctx[i - 2]))
        vec_op(wts, "v_du",
               lambda e, w=w, par_u=par_u, par_e=par_e, par_r=par_r: e.tensor_tensor(
                   arena_ap(UB0 + par_u * 2048, [[1, NH * w]]),
                   e_ap(par_e, 0, [[1, NH * w]]),
                   bass.AP(scratch, 1536 + par_r * 256,
                           [[scratch.shape[1], 128], [0, NH], [1, w]]),
                   op=OP.mult))
        m_U[i] = cnt["v_du"]

    def stage_ctx(i):
        sl, quad, mcq, s0, w = iters[i]
        par_k = quad % 2
        par_u = i % 2
        nts = w // 128
        k_sl = quad * 4 + mcq
        tw = [("v_du", m_U[i]), v_load_mark[(sl, quad)]]
        if k_sl == 0:
            tw.append(("s_cpx", cnt["s_cpx"]))
            if sl == 0:
                tw.append(("c_stv", mark_vt_all))

        def op_ctx(e, w=w, par_u=par_u, par_k=par_k, mcq=mcq, nts=nts,
                   first=(k_sl == 0), last=(k_sl == 31)):
            r = None
            for g in range(NH):
                for nt in range(nts):
                    reg = g * nts + nt
                    start = first and (reg % 4 == 0)
                    stop = last and (reg % 4 == 3 or reg == NH * nts - 1)
                    r = e.matmul(
                        ctxP[:, reg * 128:(reg + 1) * 128],
                        arena_ap(UB0 + par_u * 2048 + g * w + nt * 128, [[1, 128]]),
                        arena_ap(VQ0 + par_k * 4096 + g * 512 + mcq * 128, [[1, 128]]),
                        start=start, stop=stop)
            return r
        ev("tensor", tw, "t_ctx", op_ctx)
        m_ctx[i] = cnt["t_ctx"]
        if k_sl == 31:
            cw = [("t_ctx", m_ctx[i])]
            if ctxs_free[0] is not None:
                cw.append(ctxs_free[0])
            sc_op(cw, "s_cpx",
                  lambda e, nts=nts: e.activation(
                      ctxs[:, 0:NH * nts * 128], ctxP[:, 0:NH * nts * 128], AF.Copy))
            cpx_mark = ("s_cpx", cnt["s_cpx"])
            ti0 = s0 // 128
            for j in range(4):
                if nts == 2:
                    src = bass.AP(ctxs, j * 2 * 256,
                                  [[ctxs.shape[1], 128], [256, 2], [128, 2], [1, 128]])
                    dst = bass.AP(x_res, ti0 * 1024 + j * 2 * 128,
                                  [[x_res.shape[1], 128], [128, 2], [1024, 2], [1, 128]])
                else:
                    src = bass.AP(ctxs, j * 2 * 128,
                                  [[ctxs.shape[1], 128], [128, 2], [1, 128]])
                    dst = bass.AP(x_res, ti0 * 1024 + j * 2 * 128,
                                  [[x_res.shape[1], 128], [128, 2], [1, 128]])
                wts = [cpx_mark] if j == 0 else []
                pending_evac.append((wts, j == 3,
                                     lambda e, dst=dst, src=src: e.tensor_tensor(
                                         dst, dst, src, op=OP.add)))
            for nt2 in range(nts):
                ti = s0 // 128 + nt2
                so = (ti % 4) * 18
                bn12 = stats[:, so:so + 12]
                src_t = x_res[:, ti * 1024:(ti + 1) * 1024]
                pending_fstats.append(([],
                                       lambda e, bn12=bn12, src_t=src_t: e.bn_stats(
                                           bn12[:, 0:6], src_t[:, 0:512])))
                pending_fstats.append(([],
                                       lambda e, bn12=bn12, src_t=src_t: e.bn_stats(
                                           bn12[:, 6:12], src_t[:, 512:1024])))
                pending_fstats.append(([],
                                       lambda e, ti=ti, bn12=bn12: e.bn_aggr(
                                           fstats[:, ti * 2:ti * 2 + 2], bn12)))

    pending_evac = []
    pending_fstats = []
    ctxs_free = [None]

    def drain_evac(n):
        for _ in range(min(n, len(pending_evac))):
            wts, is_last, op = pending_evac.pop(0)
            vec_op(wts, "v_ctx" if is_last else "v_self", op)
            if is_last:
                ctxs_free[0] = (last_vec[0][0], last_vec[0][1])

    def drain_fstats(n):
        if pending_evac:
            return
        for _ in range(min(n, len(pending_fstats))):
            wts, op = pending_fstats.pop(0)
            vec_op(wts, "v_self", op)

    for slot in range(NIT + 4):
        if slot < NIT:
            stage_loads(slot)
            stage_st(slot)
        if 0 <= slot - 2 < NIT:
            stage_lnexp(slot - 2)
        if slot < NIT:
            stage_expA(slot)
            stage_expB(slot)
        if 0 <= slot - 1 < NIT:
            stage_den(slot - 1)
        if 0 <= slot - 3 < NIT:
            stage_U(slot - 3)
        drain_evac(1)
        drain_fstats(1)
        if 0 <= slot - 4 < NIT:
            stage_ctx(slot - 4)

    drain_evac(len(pending_evac))
    drain_fstats(len(pending_fstats))
    total_ctx = cnt["t_ctx"]
    total_vctx = cnt["v_ctx"]
    total_cpx = cnt["s_cpx"]
    for si in range(4):
        stsub_mark[si] = ("s_exp", cnt["s_exp"])

    total_st = cnt["t_st"]
    total_du = cnt["v_du"]
    for cj in range(4, NC):
        ev("sync", [("d_ws0", 16 * 12)], "d_tw",
           lambda e, cj=cj: e.dma_start_transpose(
               bass.AP(wkbuf, (cj - 4) * 1024, [[wkbuf.shape[1], 128], [1, 1024]]),
               bass.AP(wspill, 8 * 128 * D + cj * 128, [[D, 1024], [1, 128]])), 16)
    for cj in range(4):
        ev("sync", [("t_ctx", m_ctx[NIT - 5]), ("d_ws0", 16 * 12)], "d_tw",
           lambda e, cj=cj: e.dma_start_transpose(
               arena_ap(8192 + cj * 1024, [[1, 1024]]),
               bass.AP(wspill, 8 * 128 * D + cj * 128, [[D, 1024], [1, 128]])), 16)
    for wi, gate in [(0, ("t_ffn", 1)), (2, ("t_ffn", 9))]:
        for cj in range(NC):
            gw = [("t_ctx", total_ctx), ("v_du", total_du), ("t_st", total_st),
                  ("d_ws0", 16 * 12), gate]
            ev("sync", gw, "d_tw",
               lambda e, wi=wi, cj=cj: e.dma_start_transpose(
                   arena_ap(wi * 8192 + cj * 1024, [[1, 1024]]),
                   bass.AP(wspill, wi * 8 * 128 * D + cj * 128, [[D, 1024], [1, 128]])), 16)

    ev("scalar", [("v_self", cnt["v_self"])], "s_sd",
       lambda e: e.activation(stats[:, 0:QT],
                              bass.AP(fstats, 1, [[36, 128], [2, QT]]),
                              AF.Sqrt, bias=csts[:, 0:1]))
    ev("vector", [("s_sd", cnt["s_sd"])], "v_self",
       lambda e: e.reciprocal(bass.AP(fstats, 1, [[36, 128], [2, QT]]),
                              stats[:, 0:QT]))
    fstats_done = cnt["v_self"]

    kact = scratch[:, 8192:10240]
    rsb = arena[:, 24576:25600]
    prevbuf = arena[:, 25600:26624]
    relu_b = arena[:, 26624:27648]
    t1b = arena[:, 27648:28672]
    tsb = scratch_f[:, 1024:1536]

    ffn_mark = {}
    mix_mark = {}
    first_ffn_mm = [True]
    first_relu = [True]

    ln2_pend = {}

    ln2_par = [0]

    def ln2_chains(tc):
        tc0, wch = TCHUNKS[tc]
        ntiles = wch // 128
        t0 = tc0 // 128
        pend = []
        for tt in range(ntiles):
            ti = t0 + tt
            src2 = x_res[:, ti * 1024:(ti + 1) * 1024]
            par = ln2_par[0] % 2
            ln2_par[0] += 1
            dst = xn_sb[:, par * 1024:(par + 1) * 1024]
            n = 32 + ti
            nwaits = [("v_self", fstats_done), ("t_stw", xn_tr_mark[31])]
            if n - 2 in xn_tr_mark:
                nwaits.append(("t_stw", xn_tr_mark[n - 2]))
            ev("vector", nwaits, "s_xn",
               lambda e, dst=dst, src2=src2, ti=ti: e.tensor_scalar(
                   dst, src2, fstats[:, ti * 2:ti * 2 + 1],
                   fstats[:, ti * 2 + 1:ti * 2 + 2],
                   op0=OP.subtract, op1=OP.mult))
            pend.append((n, dst, cnt["s_xn"]))
        ln2_pend[tc] = pend

    def mix_part(tc, tiles=None):
        tc0, wch = TCHUNKS[tc]
        ntiles = wch // 128
        cp = tc % 2
        xr_sl = scratch[:, cp * 2048:(cp + 1) * 2048]
        xk_sl = scratch[:, 4096 + cp * 2048: 4096 + (cp + 1) * 2048]
        for tt in (range(ntiles) if tiles is None else tiles):
            n, dst, sxn_mark = ln2_pend[tc][tt]
            half = n % 2
            emit_tile_transposes(dst, half, identf[:],
                                 [("s_xn", sxn_mark), ("v_ctx", total_vctx)])
            xn_tr_mark[n] = cnt["t_stw"]
            cbase = half * 1024
            mw = [("t_stw", cnt["t_stw"]), ("v_mix", cnt["v_mix"])]
            if tc - 2 >= 0:
                mw.append(("t_ffn", ffn_mark[tc - 2]))
            ev_noinc("vector", mw, lambda e, cbase=cbase: e.tensor_copy(
                bass.AP(arena, 25600 + 1, [[A, 128], [128, NC], [1, 127]]),
                bass.AP(ctxP, cbase, [[2048, 128], [128, NC], [1, 127]])))
            ev("vector", [], "v_self", lambda e: e.tensor_copy(
                bass.AP(arena, 25600, [[A, 128], [128, NC]]), carry[:, 0:NC]))
            ev("vector", [("v_self", cnt["v_self"]), ("s_act", cnt["s_act"])], "v_self",
               lambda e, cbase=cbase: e.tensor_tensor(
                   t1b[:, 0:1024], ctxP[:, cbase:cbase + 1024],
                   prevbuf[:, 0:1024], op=OP.subtract))
            t1_chain = cnt["v_self"]
            ev("vector", [("v_self", t1_chain)], "c_stv", lambda e, cbase=cbase: e.tensor_copy(
                carry[:, 0:NC], bass.AP(ctxP, cbase + 127, [[2048, 128], [128, NC]])))
            tr_mark[half] = cnt["c_stv"]
            mixtmp = arena[:, 28672:29696]
            xr_off = cp * 2048 + tt * 128
            xk_off = 4096 + cp * 2048 + tt * 128
            ev("vector", [("v_self", t1_chain)], "v_self",
               lambda e: e.tensor_tensor(
                   mixtmp[:, 0:1024], t1b[:, 0:1024],
                   bass.AP(cvec, 24, [[40, 128], [1, 8], [0, 128]]),
                   op=OP.mult))
            ev("vector", [("v_self", cnt["v_self"])] , "v_self",
               lambda e, xr_off=xr_off: e.tensor_tensor(
                   bass.AP(scratch, xr_off, [[scratch.shape[1], 128], [256, NC], [1, 128]]),
                   mixtmp[:, 0:1024], prevbuf[:, 0:1024], op=OP.add))
            ev("vector", [("v_self", cnt["v_self"])], "v_self",
               lambda e: e.tensor_tensor(
                   mixtmp[:, 0:1024], t1b[:, 0:1024],
                   bass.AP(cvec, 32, [[40, 128], [1, 8], [0, 128]]),
                   op=OP.mult))
            ev("vector", [("v_self", cnt["v_self"])], "v_mix",
               lambda e, xk_off=xk_off: e.tensor_tensor(
                   bass.AP(scratch, xk_off, [[scratch.shape[1], 128], [256, NC], [1, 128]]),
                   mixtmp[:, 0:1024], prevbuf[:, 0:1024], op=OP.add))
        mix_mark[tc] = cnt["v_mix"]

    def ffn_matmuls(tc, after_k=None, after_tile=None):
        tc0, wch = TCHUNKS[tc]
        ntiles = wch // 128
        t0 = tc0 // 128
        cp = tc % 2
        xr_sl = scratch[:, cp * 2048:(cp + 1) * 2048]
        xk_sl = scratch[:, 4096 + cp * 2048: 4096 + (cp + 1) * 2048]
        for dj in range(NC):
            si = next_stsub()
            ps = stsub[si]
            tw = [("v_mix", mix_mark[tc]), (stsub_mark[si][0], stsub_mark[si][1])]
            if first_ffn_mm[0]:
                tw.append(("d_tw", 16 * 8))
                first_ffn_mm[0] = False

            def op_k(e, ps=ps, dj=dj, wch=wch, xk_sl=xk_sl):
                r = None
                for cj in range(NC):
                    if cj < 4:
                        wsrc2 = arena_ap((8 + cj) * 1024 + dj * 128, [[1, 128]])
                    else:
                        wsrc2 = bass.AP(wkbuf, (cj - 4) * 1024 + dj * 128,
                                        [[wkbuf.shape[1], 128], [1, 128]])
                    r = e.matmul(ps[:, 0:wch], wsrc2,
                                 xk_sl[:, cj * 256: cj * 256 + wch],
                                 start=(cj == 0), stop=(cj == NC - 1))
                return r
            ev("tensor", tw, "t_ffn", op_k)
            sw = [("t_ffn", cnt["t_ffn"]), ("v_sq", max(0, cnt["v_sq"] - 1))]
            if first_relu[0]:
                sw.append(("t_ctx", total_ctx))
                first_relu[0] = False
            pj = dj % 2
            ev("scalar", sw, "s_act",
               lambda e, ps=ps, wch=wch, pj=pj: e.activation(
                   relu_b[:, pj * 512: pj * 512 + wch], ps[:, 0:wch], AF.Relu))
            stsub_mark[si] = ("s_act", cnt["s_act"])
            ev("vector", [("s_act", cnt["s_act"])], "v_sq",
               lambda e, pj=pj, dj=dj, wch=wch: e.tensor_tensor(
                   kact[:, dj * 256: dj * 256 + wch],
                   relu_b[:, pj * 512: pj * 512 + wch],
                   relu_b[:, pj * 512: pj * 512 + wch], op=OP.mult))
        ksq_done = cnt["v_sq"]
        if after_k is not None:
            after_k()

        for tt in range(ntiles):
            if after_tile is not None and tt > 0:
                after_tile(tt - 1)
            ti = t0 + tt
            fin_before = cnt["v_fin"]
            for ds in range(2):
                si = next_stsub()
                ps = stsub[si]
                tw = [(stsub_mark[si][0], stsub_mark[si][1])]
                if tt == 0 and ds == 0:
                    tw.append(("d_tw", 16 * 16))

                def op_r(e, ps=ps, tt=tt, ds=ds, xr_sl=xr_sl):
                    r = None
                    for cj in range(NC):
                        r = e.matmul(ps[:, 0:512],
                                     xr_sl[:, cj * 256 + tt * 128: cj * 256 + (tt + 1) * 128],
                                     arena_ap(cj * 1024 + ds * 512, [[1, 512]]),
                                     start=(cj == 0), stop=(cj == NC - 1))
                    return r
                ev("tensor", tw, "t_ffn", op_r)
                ev("scalar", [("t_ffn", cnt["t_ffn"]), ("v_fin", fin_before)], "s_act",
                   lambda e, ps=ps, ds=ds: e.activation(
                       rsb[:, ds * 512:(ds + 1) * 512], ps[:, 0:512], AF.Sigmoid))
                stsub_mark[si] = ("s_act", cnt["s_act"])
            sig_done = cnt["s_act"]
            for ds in range(2):
                si = next_stsub()
                ps = stsub[si]
                tw = [("v_sq", ksq_done), (stsub_mark[si][0], stsub_mark[si][1])]
                if tt == 0 and ds == 0:
                    tw.append(("d_tw", 16 * 24))

                def op_v(e, ps=ps, tt=tt, ds=ds):
                    r = None
                    for cj in range(NC):
                        r = e.matmul(ps[:, 0:512],
                                     kact[:, cj * 256 + tt * 128: cj * 256 + (tt + 1) * 128],
                                     arena_ap((16 + cj) * 1024 + ds * 512, [[1, 512]]),
                                     start=(cj == 0), stop=(cj == NC - 1))
                    return r
                ev("tensor", tw, "t_ffn", op_v)
                ev("vector", [("t_ffn", cnt["t_ffn"]), ("s_act", sig_done),
                              ("v_self", cnt["v_self"])], "v_self",
                   lambda e, ps=ps, ds=ds: e.tensor_tensor(
                       tsb[:, 0:512], ps[:, 0:512],
                       rsb[:, ds * 512:(ds + 1) * 512], op=OP.mult))
                dst = x_res[:, ti * 1024 + ds * 512: ti * 1024 + ds * 512 + 512]
                is_last = (ds == 1)
                ev("vector", [("v_self", cnt["v_self"])],
                   "v_fin" if is_last else "v_self",
                   lambda e, dst=dst: e.tensor_tensor(dst, dst, tsb[:, 0:512], op=OP.add))
                stsub_mark[si] = ("v_fin", cnt["v_fin"] if is_last else cnt["v_fin"] + 1)
        ffn_mark[tc] = cnt["t_ffn"]

        ev("sync", [("v_fin", cnt["v_fin"])], "d_out",
           lambda e, t0=t0, ntiles=ntiles: e.dma_start(
               bass.AP(out_h, t0 * 128 * 1024,
                       [[1024, 128], [128 * 1024, ntiles], [1, 1024]]),
               ap3(x_res, t0 * 1024, [[1024, ntiles], [1, 1024]])), 16)

    emit_warmup(48, [("s_cpx", total_cpx)])
    ln2_chains(0)
    mix_part(0)
    for tc in range(len(TCHUNKS)):
        if tc + 1 < len(TCHUNKS):
            nt_next = TCHUNKS[tc + 1][1] // 128
            ln2_chains(tc + 1)
            ffn_matmuls(tc,
                        after_k=lambda tc=tc: mix_part(tc + 1, [0]),
                        after_tile=lambda tt, tc=tc, n=nt_next:
                            mix_part(tc + 1, [tt + 1]) if tt + 1 < n else None)
            if nt_next > TCHUNKS[tc][1] // 128:
                mix_part(tc + 1, list(range(TCHUNKS[tc][1] // 128, nt_next)))
        else:
            ffn_matmuls(tc)

    ev_noinc("sync", [("d_out", cnt["d_out"])], lambda e: e.nop())

    with nc.Block() as block:
        @block.sync
        def _(e):
            for fn in sched["sync"]:
                fn(e)

        @block.gpsimd
        def _(e):
            for fn in sched["gpsimd"]:
                fn(e)

        @block.tensor
        def _(e):
            for fn in sched["tensor"]:
                fn(e)

        @block.scalar
        def _(e):
            for fn in sched["scalar"]:
                fn(e)

        @block.vector
        def _(e):
            for fn in sched["vector"]:
                fn(e)

    es.close()
    return nc


def _in_maps(inputs):
    x = np.asarray(inputs["L0_attn_hidden_states"], dtype=np.float32)
    cvec = np.empty((128, 40), dtype=np.float32)
    for i, k in enumerate(["L0_attn_bq", "L0_attn_bk", "L0_attn_bv",
                           "L0_ffn_mu_r", "L0_ffn_mu_k"]):
        cvec[:, i * 8:(i + 1) * 8] = np.asarray(inputs[k], np.float32).reshape(8, 128).T
    ident = np.eye(128, dtype=ml_dtypes.bfloat16)
    identf = np.eye(128, dtype=np.float32)
    base = {
        "wq": np.asarray(inputs["L0_attn_Wq"], np.float32),
        "wk": np.asarray(inputs["L0_attn_Wk"], np.float32),
        "wv": np.asarray(inputs["L0_attn_Wv"], np.float32),
        "wr2": np.asarray(inputs["L0_ffn_W_r"], np.float32),
        "wk2": np.asarray(inputs["L0_ffn_W_k"], np.float32),
        "wv2": np.asarray(inputs["L0_ffn_W_v"], np.float32),
        "cvec": cvec, "ident": ident, "identf": identf,
    }
    maps = []
    for c in range(8):
        bi, half = c // 2, c % 2
        xb = x[bi] if half == 0 else np.roll(x[bi], -1920, axis=0)
        m = dict(base)
        m["x"] = np.ascontiguousarray(xb)
        maps.append(m)
    return maps


def kernel(**inputs):
    maps = _in_maps(inputs)
    nc = build_nc()
    res = run_bass_kernel_spmd(nc, maps, core_ids=list(range(8)))
    out = np.empty((4, 4096, 1024), dtype=np.float32)
    for c in range(8):
        bi, half = c // 2, c % 2
        o = np.asarray(res.results[c]["out"])
        if half == 0:
            out[bi, 0:2048] = o[0:2048]
        else:
            out[bi, 2048:4096] = o[128:2176]
    return out


# version 46 (final)
# speedup vs baseline: 1.0113x; 1.0052x over previous
import numpy as np
import ml_dtypes
from contextlib import ExitStack

import concourse.bass as bass
import concourse.mybir as mybir
from concourse.bass_utils import run_bass_kernel_spmd

F32 = mybir.dt.float32
BF16 = mybir.dt.bfloat16
AF = mybir.ActivationFunctionType
OP = mybir.AluOpType

D = 1024
NC = 8
NH = 8
T = 4096
NT = 32
QT = 17
NQ = QT * 128
EPS = 1e-5
A_SLICES = [(i * 256, 256) for i in range(8)] + [(2048, 128)]
Q_PROJ_SLICES = [(0, 512), (512, 512), (1024, 512), (1536, 512), (2048, 128)]
TCHUNKS = [(i * 256, 256) for i in range(8)] + [(2048, 128)]

SEM_NAMES = ("d_x", "d_x2", "d_x3", "d_cst", "d_xt0", "d_xt1", "d_xt2", "d_xt3", "d_wa", "d_spk", "d_spv", "d_spq",
             "d_q0", "d_q1", "d_k0", "d_k1", "d_v0", "d_v1", "d_wf0", "d_wf1",
             "d_out", "v_ln", "s_sd", "s_xn", "t_stw", "c_stv", "s_exp", "v_du",
             "t_proj", "v_proj", "t_st", "t_ctx", "v_ctx", "s_cpx", "t_ffn", "s_act",
             "v_sq", "v_mix", "v_fin", "v_self", "v_den", "s_rcp", "s_self", "g_rcp",
             "s_cpy", "d_ws0", "d_tw")


def build_nc():
    nc = bass.Bass("TRN2", target_bir_lowering=False)

    x_h = nc.dram_tensor("x", [T, D], F32, kind="ExternalInput")
    wq_h = nc.dram_tensor("wq", [D, D], F32, kind="ExternalInput")
    wk_h = nc.dram_tensor("wk", [D, D], F32, kind="ExternalInput")
    wv_h = nc.dram_tensor("wv", [D, D], F32, kind="ExternalInput")
    wr2_h = nc.dram_tensor("wr2", [D, D], F32, kind="ExternalInput")
    wk2_h = nc.dram_tensor("wk2", [D, D], F32, kind="ExternalInput")
    wv2_h = nc.dram_tensor("wv2", [D, D], F32, kind="ExternalInput")
    cvec_h = nc.dram_tensor("cvec", [128, 40], F32, kind="ExternalInput")
    ident_h = nc.dram_tensor("ident", [128, 128], BF16, kind="ExternalInput")
    identf_h = nc.dram_tensor("identf", [128, 128], F32, kind="ExternalInput")
    out_h = nc.dram_tensor("out", [NQ, D], F32, kind="ExternalOutput")
    wspill = nc.dram_tensor("wspill", [3 * D, D], BF16)
    kspill = nc.dram_tensor("kspill", [128, NH * T], BF16)
    vtspill = nc.dram_tensor("vtspill", [128, NH * T], BF16)
    qspill = nc.dram_tensor("qspill", [128, NH * NQ], BF16)

    es = ExitStack()

    def sbuf(name, cols, dt):
        return es.enter_context(nc.sbuf_tensor(name, [128, cols], dt))

    def psum(name, cols, dt):
        return es.enter_context(nc.psum_tensor(name, [128, cols], dt))

    x_res = sbuf("x_res", QT * 1024, F32)
    xtmp = sbuf("xtmp", 4 * 1024, BF16)
    arena = sbuf("arena", 32768, BF16)
    xn_sb = sbuf("xn_sb", 2 * 1024, F32)
    scratch = sbuf("scratch", 10368, BF16)
    scratch_f = sbuf("scratch_f", 1536, F32)
    ctxs = sbuf("ctxs", 2048, F32)
    wkbuf = sbuf("wkbuf", 4096, BF16)
    wa = sbuf("wa", 6144, BF16)
    cvec = sbuf("cvecsb", 40, F32)
    ident = sbuf("identsb", 128, BF16)
    identf = sbuf("identfsb", 128, F32)
    stats = sbuf("stats", 72, F32)
    csts = sbuf("csts", 2, F32)
    fstats = sbuf("fstats", 36, F32)
    carry = sbuf("carry", 8, BF16)

    stAB = psum("stAB", 2048, F32)
    ctxP = psum("ctxP", 2048, F32)

    sems = {n: es.enter_context(nc.semaphore(n)) for n in SEM_NAMES}
    sched = {"tensor": [], "vector": [], "scalar": [], "sync": [], "gpsimd": []}
    cnt = {n: 0 for n in SEM_NAMES}

    def ev(eng, waits, inc, op, amt=1):
        waits = [(sems[n], v) for (n, v) in waits if v > 0]
        incsem = sems[inc] if inc else None

        def fn(e, waits=tuple(waits), incsem=incsem, op=op, amt=amt):
            for s, v in waits:
                e.wait_ge(s, v)
            r = op(e)
            if incsem is not None:
                r.then_inc(incsem, amt)
        sched[eng].append(fn)
        if inc:
            cnt[inc] += amt

    def ev_noinc(eng, waits, op):
        ev(eng, waits, None, op)

    def drain(eng):
        sched[eng].append(lambda e: e.drain())

    def ap3(t_sb, offset, dims):
        return bass.AP(t_sb, offset, [[t_sb.shape[1], 128]] + dims)

    A = arena.shape[1]

    def arena_ap(offset, dims):
        return bass.AP(arena, offset, [[A, 128]] + dims)

    stsub = [stAB[:, i * 512:(i + 1) * 512] for i in range(4)]
    stsub_mark = [("d_cst", 0)] * 4
    stsub_rot = [0]

    def next_stsub():
        i = stsub_rot[0] % 4
        stsub_rot[0] += 1
        return i

    tr_mark = [0, 0]
    xn_tr_mark = {}
    ln_done = [0]

    ev("sync", [], "d_cst", lambda e: e.dma_start(cvec[:], cvec_h.ap()), 16)
    ev("sync", [], "d_cst", lambda e: e.dma_start(ident[:], ident_h.ap()), 16)
    ev("sync", [], "d_cst", lambda e: e.dma_start(identf[:], identf_h.ap()), 16)
    x_tiled = x_h.ap().rearrange("(t p) c -> p t c", p=128)
    for sem_x, (c0, c1) in (("d_x", (0, 4)), ("d_x2", (4, 10)), ("d_x3", (10, QT))):
        ev("sync", [], sem_x,
           lambda e, c0=c0, c1=c1: e.dma_start(
               ap3(x_res, c0 * 1024, [[1024, c1 - c0], [1, 1024]]),
               x_tiled[:, c0:c1, :]), 16)
    ev_noinc("vector", [], lambda e: e.memset(carry[:, 0:NC], 0.0))
    ev_noinc("vector", [], lambda e: e.memset(csts[:, 0:1], EPS))
    ev_noinc("vector", [], lambda e: e.memset(csts[:, 1:2], -1.0))

    def op_warm0(e):
        r = None
        for _ in range(48):
            r = e.matmul(ctxP[:, 0:128], ident[:], ident[:], start=True, stop=True)
        return r
    ev("tensor", [("d_cst", 48)], None, op_warm0)

    def emit_xtmp_dma(t):
        par = t % 4
        waits = [("s_xn", t - 3)] if t - 4 >= QT else []
        ev("gpsimd", waits, "d_xt%d" % par,
           lambda e, par=par, t=t: e.dma_start(
               xtmp[:, par * 1024:(par + 1) * 1024], x_tiled[:, t, :]), 16)

    def emit_wa_dma(g, wi, w_h):
        src = w_h.ap().rearrange("(cj p) d -> p cj d", p=128)[:, :, g * 128:(g + 1) * 128]
        if g >= 2:
            waits = [("t_proj", 21 * (g - 1) + 1)]
        elif g == 1:
            waits = [("t_proj", 1)]
        elif wi > 0:
            waits = [("d_x", 16)]
        else:
            waits = []
        ev("gpsimd", waits, "d_wa",
           lambda e, g=g, wi=wi, src=src: e.dma_start(
               bass.AP(wa, (g % 2) * 3072 + wi * 1024, [[wa.shape[1], 128], [128, NC], [1, 128]]),
               src), 16)

    emit_xtmp_dma(17)
    emit_xtmp_dma(18)
    for wi, w_h in enumerate([wq_h, wk_h, wv_h]):
        emit_wa_dma(0, wi, w_h)
    for t in range(19, NT):
        emit_xtmp_dma(t)

    def emit_ln(src_ap, src_wait, dst_ap, extra_norm_wait):
        n = ln_done[0]
        par = n % 4
        so = par * 18
        bn12 = stats[:, so:so + 12]
        mv = stats[:, so + 12:so + 14]
        sd = stats[:, so + 14:so + 15]
        inv = stats[:, so + 15:so + 16]
        waits = list(src_wait)
        if n - 4 >= 0:
            waits.append(("s_sd", n - 3))
        ev_noinc("vector", waits, lambda e: e.bn_stats(bn12[:, 0:6], src_ap[:, 0:512]))
        ev("vector", [], "v_self", lambda e: e.bn_stats(bn12[:, 6:12], src_ap[:, 512:1024]))
        ev("vector", [("v_self", cnt["v_self"])], "v_ln", lambda e: e.bn_aggr(mv, bn12))
        ev("scalar", [("v_ln", cnt["v_ln"])], "s_sd",
           lambda e: e.activation(sd, mv[:, 1:2], AF.Sqrt, bias=csts[:, 0:1]))
        ev("vector", [("s_sd", cnt["s_sd"])], "v_self", lambda e: e.reciprocal(inv, sd))
        nwaits = list(extra_norm_wait) + [("v_self", cnt["v_self"])]
        if n - 2 in xn_tr_mark:
            nwaits.append(("t_stw", xn_tr_mark[n - 2]))
        ev("vector", nwaits, "s_xn",
           lambda e: e.tensor_scalar(dst_ap, src_ap, mv[:, 0:1], inv,
                                     op0=OP.subtract, op1=OP.mult))
        ln_done[0] += 1
        return n

    def emit_tile_transposes(src_ap, half, idf, extra_waits=()):
        tw = [("c_stv", tr_mark[half])] + list(extra_waits)

        def op(e, src_ap=src_ap, half=half, idf=idf):
            r = None
            for j in range(NC):
                r = e.transpose(ctxP[:, half * 1024 + j * 128: half * 1024 + (j + 1) * 128],
                                src_ap[:, j * 128:(j + 1) * 128], idf)
            return r
        ev("tensor", tw, "t_stw", op)

    kg = scratch[:, 0:4096]
    vtb = scratch[:, 4096:8192]
    qg = scratch[:, 8192:10368]
    vsb = scratch_f[:, 0:1024]
    tile_evac_mark = {}
    vsb_tr_mark = [0, 0]

    def emit_proj_group(g, kind, dst_ap, s0, w, bias_col, wi, spill_wait=()):
        si = next_stsub()
        ps = stsub[si]
        req = (s0 + w - 1) // 128
        tw = [("d_wa", 48 * (g + 1)),
              (stsub_mark[si][0], stsub_mark[si][1]),
              ("c_stv", tile_evac_mark[req])]

        def op(e, ps=ps, s0=s0, w=w, wi=wi, g=g):
            r = None
            for cj in range(NC):
                r = e.matmul(ps[:, 0:w],
                             wa[:, (g % 2) * 3072 + wi * 1024 + cj * 128:
                                 (g % 2) * 3072 + wi * 1024 + (cj + 1) * 128],
                             arena_ap(cj * 4096 + s0, [[1, w]]),
                             start=(cj == 0), stop=(cj == NC - 1))
            return r
        ev("tensor", tw, "t_proj", op)
        vw = [("t_proj", cnt["t_proj"])] + list(spill_wait)
        if kind == "v":
            par2 = dst_ap
            vw.append(("t_stw", vsb_tr_mark[par2]))
            dst = vsb[:, par2 * 512:(par2 + 1) * 512]

            def opv(e, dst=dst, ps=ps, w=w, bias_col=bias_col):
                return e.tensor_scalar(dst, ps[:, 0:w],
                                       cvec[:, bias_col:bias_col + 1],
                                       None, op0=OP.add)
            ev("vector", vw + [("d_cst", 48)], "v_proj", opv)
            stsub_mark[si] = ("v_proj", cnt["v_proj"])
            return
        dst = dst_ap

        def opc(e, dst=dst, ps=ps, w=w):
            return e.activation(dst, ps[:, 0:w], AF.Copy)
        ev("scalar", vw, "s_cpy", opc)
        stsub_mark[si] = ("s_cpy", cnt["s_cpy"])
        ev("vector", [("s_cpy", cnt["s_cpy"]), ("d_cst", 48)], "v_proj",
           lambda e, dst=dst, bias_col=bias_col: e.tensor_scalar(
               dst, dst, cvec[:, bias_col:bias_col + 1], None, op0=OP.add))

    def emit_v_post(g, vs, vp_mark=None):
        par2 = vs % 2
        half = vs % 2
        tw = [vp_mark or ("v_proj", cnt["v_proj"]), ("c_stv", tr_mark[half])]

        def op_trv(e, par2=par2, half=half):
            r = None
            for j in range(4):
                r = e.transpose(
                    ctxP[:, half * 1024 + j * 128: half * 1024 + (j + 1) * 128],
                    vsb[:, par2 * 512 + j * 128: par2 * 512 + (j + 1) * 128],
                    identf[:])
            return r
        ev("tensor", tw, "t_stw", op_trv)
        vsb_tr_mark[par2] = cnt["t_stw"]
        ev("scalar", [("t_stw", cnt["t_stw"]), ("d_spv", 16 * g)], "c_stv",
           lambda e, vs=vs, half=half: e.activation(
               vtb[:, vs * 512:(vs + 1) * 512],
               ctxP[:, half * 1024: half * 1024 + 512], AF.Copy))
        tr_mark[half] = cnt["c_stv"]

    def emit_spills(g):
        ev("sync", [("v_proj", cnt["v_proj"])], "d_spq",
           lambda e, g=g: e.dma_start(
               bass.AP(qspill, g * NQ, [[NH * NQ, 128], [1, NQ]]), qg[:, 0:NQ]), 16)
        ev("sync", [("v_proj", cnt["v_proj"])], "d_spk",
           lambda e, g=g: e.dma_start(
               bass.AP(kspill, g * T, [[NH * T, 128], [1, T]]), kg[:, 0:T]), 16)
        ev("sync", [("c_stv", cnt["c_stv"])], "d_spv",
           lambda e, g=g: e.dma_start(
               bass.AP(vtspill, g * T, [[NH * T, 128], [1, T]]), vtb[:, 0:T]), 16)

    pending_vpost = []

    def drain_vpost():
        while pending_vpost:
            pending_vpost.pop(0)()

    def run_job(fn):
        n0 = len(pending_vpost)
        fn()
        for _ in range(min(n0, len(pending_vpost))):
            pending_vpost.pop(0)()

    def v_job(g, vs):
        emit_proj_group(g, "v", vs % 2, vs * 512, 512, 16 + g, 2,
                        spill_wait=[("d_spv", 16 * g)])
        vp_mark = ("v_proj", cnt["v_proj"])
        pending_vpost.append(lambda g=g, vs=vs, vp_mark=vp_mark:
                             emit_v_post(g, vs, vp_mark))

    jobs = []
    for g in range(NH):
        if g + 1 < NH:
            jobs.append((0, lambda g=g: [emit_wa_dma(g + 1, wi, w_h)
                                         for wi, w_h in enumerate([wq_h, wk_h, wv_h])]))
        for i in range(8):
            if i < len(Q_PROJ_SLICES):
                s0, w = Q_PROJ_SLICES[i]
                jobs.append(((s0 + w - 1) // 128,
                             lambda g=g, s0=s0, w=w: emit_proj_group(
                                 g, "q", qg[:, s0:s0 + w], s0, w, 0 + g, 0,
                                 spill_wait=[("d_spq", 16 * g)])))
            jobs.append((4 * i + 3,
                         lambda g=g, ks=i: emit_proj_group(
                             g, "k", kg[:, ks * 512:(ks + 1) * 512], ks * 512, 512,
                             8 + g, 1, spill_wait=[("d_spk", 16 * g)])))
            jobs.append((4 * i + 3, lambda g=g, vs=i: v_job(g, vs)))
        jobs.append((31, lambda g=g: (drain_vpost(), emit_spills(g))))
    jobs.reverse()

    ln_norm_mark = {}
    ln_dst = {}

    def emit_tr_evac(t):
        dst = ln_dst[t]
        half = t % 2
        extra = [("d_cst", 48), ("s_xn", ln_norm_mark[t])]
        emit_tile_transposes(dst, half, identf[:], extra)
        xn_tr_mark[t] = cnt["t_stw"]
        ev("scalar", [("t_stw", cnt["t_stw"])], "c_stv",
           lambda e, t=t, half=half: e.activation(
               arena_ap(t * 128, [[4096, NC], [1, 128]]),
               ctxP[:, half * 1024: half * 1024 + 1024], AF.Copy))
        tr_mark[half] = cnt["c_stv"]
        tile_evac_mark[t] = cnt["c_stv"]

    def emit_ln_tile(t):
        par = t % 2
        if t < QT:
            src2 = x_res[:, t * 1024:(t + 1) * 1024]
            src_wait = [("d_x" if t < 4 else ("d_x2" if t < 10 else "d_x3"), 16)]
        else:
            par4 = t % 4
            src2 = xtmp[:, par4 * 1024:(par4 + 1) * 1024]
            src_wait = [("d_xt%d" % par4, 16 * ((t - 17) // 4 + 1))]
        dst = xn_sb[:, par * 1024:(par + 1) * 1024]
        emit_ln(src2, src_wait, dst, [])
        ln_norm_mark[t] = cnt["s_xn"]
        ln_dst[t] = dst

    def emit_wcast(ci):
        wi, q4 = ci // 4, ci % 4
        w_h2 = [wr2_h, wk2_h, wv2_h][wi]
        ev("gpsimd", [], "d_ws0",
           lambda e, wi=wi, q4=q4, w_h2=w_h2: e.dma_start(
               bass.AP(wspill, (wi * 1024 + q4 * 256) * D, [[D, 256], [1, D]]),
               bass.AP(w_h2, q4 * 256 * D, [[D, 256], [1, D]])), 16)

    emit_ln_tile(0)
    emit_ln_tile(1)
    for t in range(1, NT):
        emit_tr_evac(t - 1)
        if t + 1 < NT:
            emit_ln_tile(t + 1)
        if t >= 8 and t % 2 == 0 and (t - 8) // 2 < 12:
            emit_wcast((t - 8) // 2)
        done = 0
        while jobs and jobs[-1][0] <= t - 1 and done < 12:
            run_job(jobs.pop()[1])
            done += 1
    emit_tr_evac(NT - 1)
    while jobs:
        run_job(jobs.pop()[1])
    drain_vpost()

    all_proj = cnt["t_proj"]
    all_proj_v = cnt["v_proj"]
    mark_vt_all = cnt["c_stv"]

    KQ0, VQ0, QS0, EA0, UB0 = 0, 8192, 16384, 18432, 26624
    d1 = scratch[:, 0:1024]
    d2 = scratch[:, 1024:1536]
    den = scratch_f[:, 0:512]
    lbuf = scratch_f[:, 512:768]
    rrc = scratch[:, 1536:2560]

    def emit_warmup(n_mm, waits):
        def op(e):
            r = None
            for i in range(n_mm):
                r = e.matmul(ctxP[:, 0:128], ident[:], ident[:],
                             start=True, stop=True)
            return r
        ev("tensor", waits, None, op)

    emit_warmup(48, [("c_stv", mark_vt_all), ("d_cst", 48)])

    iters = []
    for sl, (s0, w) in enumerate(A_SLICES):
        for quad in range(8):
            for mcq in range(4):
                iters.append((sl, quad, mcq, s0, w))
    NIT = len(iters)

    m_stA, m_stB, m_expA, m_expB = {}, {}, {}, {}
    m_L3, m_ln, m_exp2, m_U, m_ctx = {}, {}, {}, {}, {}
    k_load_mark, v_load_mark, q_load_mark = {}, {}, {}
    last_vec = [None]
    last_sc = [None]

    def vec_op(waits, inc, op):
        wts = list(waits)
        if last_vec[0] is not None:
            wts.append(last_vec[0])
        ev("vector", wts, inc, op)
        last_vec[0] = (inc, cnt[inc])

    def sc_op(waits, inc, op):
        wts = list(waits)
        if last_sc[0] is not None:
            wts.append(last_sc[0])
        ev("scalar", wts, inc, op)
        last_sc[0] = (inc, cnt[inc])

    def stage_loads(i):
        sl, quad, mcq, s0, w = iters[i]
        if mcq != 0:
            return
        par_k = quad % 2
        if quad == 0:
            par_q = sl % 2
            qbase = QS0 if par_q == 0 else 30720
            qw = [("d_spq", 16 * NH)]
            if sl >= 2:
                qw.append(("t_st", m_stB[(sl - 1) * 32 - 1]))
            ev("gpsimd", qw, "d_q%d" % par_q,
               lambda e, s0=s0, w=w, qbase=qbase: e.dma_start(
                   arena_ap(qbase, [[w, NH], [1, w]]),
                   bass.AP(qspill, s0, [[NH * NQ, 128], [NQ, NH], [1, w]])), 16)
            q_load_mark[sl] = ("d_q%d" % par_q, 16 * (sl // 2 + 1))
        gi0 = sl * 32 + quad * 4
        kw = [("d_spk", 16 * NH)]
        vw = [("d_spv", 16 * NH)]
        if gi0 - 8 >= 0:
            kw.append(("t_st", m_stB[gi0 - 8 + 3]))
            vw.append(("t_ctx", m_ctx[gi0 - 8 + 3]))
        ev("sync", kw, "d_k%d" % par_k,
           lambda e, quad=quad, par_k=par_k: e.dma_start(
               arena_ap(KQ0 + par_k * 4096, [[512, NH], [1, 512]]),
               bass.AP(kspill, quad * 512, [[NH * T, 128], [T, NH], [1, 512]])), 16)
        ev("gpsimd", vw, "d_v%d" % par_k,
           lambda e, quad=quad, par_k=par_k: e.dma_start(
               arena_ap(VQ0 + par_k * 4096, [[512, NH], [1, 512]]),
               bass.AP(vtspill, quad * 512, [[NH * T, 128], [T, NH], [1, 512]])), 16)
        kq_count = (sl * 8 + quad) // 2 + 1
        k_load_mark[(sl, quad)] = ("d_k%d" % par_k, 16 * kq_count)
        v_load_mark[(sl, quad)] = ("d_v%d" % par_k, 16 * kq_count)

    def stage_st(i):
        sl, quad, mcq, s0, w = iters[i]
        par_k = quad % 2
        for hf in range(2):
            tw = [k_load_mark[(sl, quad)], q_load_mark[sl]]
            if i - 1 >= 0:
                tw.append(("s_exp", (m_expA if hf == 0 else m_expB)[i - 1]))
            if i == 0:
                tw.append(("v_proj", all_proj_v))

            qbase2 = QS0 if sl % 2 == 0 else 30720

            def op_st(e, w=w, par_k=par_k, mcq=mcq, hf=hf, qbase2=qbase2):
                r = None
                for g in range(hf * 4, hf * 4 + 4):
                    r = e.matmul(
                        stAB[:, g * w:(g + 1) * w],
                        arena_ap(KQ0 + par_k * 4096 + g * 512 + mcq * 128, [[1, 128]]),
                        arena_ap(qbase2 + g * w, [[1, w]]),
                        start=True, stop=True)
                return r
            ev("tensor", tw, "t_st", op_st)
            if hf == 0:
                m_stA[i] = cnt["t_st"]
            else:
                m_stB[i] = cnt["t_st"]

    def e_ap(par, off, dims):
        if par < 4:
            return arena_ap(EA0 + par * 2048 + off, dims)
        return bass.AP(scratch, 2560 + (par - 4) * 2048 + off,
                       [[scratch.shape[1], 128]] + dims)

    def stage_expA(i):
        sl, quad, mcq, s0, w = iters[i]
        par_e = i % 6
        sw = [("t_st", m_stA[i])]
        if i - 6 >= 0:
            sw.append(("v_du", m_U[i - 6]))
        if i in (4, 5):
            sw.append(("d_spv", 16 * NH))

        def op_expA(e, w=w, par_e=par_e):
            return e.activation(
                e_ap(par_e, 0, [[1, 4 * w]]),
                stAB[:, 0:4 * w], AF.Exp, scale=1.0 / 32.0)
        ev("scalar", sw, "s_exp", op_expA)
        m_expA[i] = cnt["s_exp"]

    def stage_expB(i):
        sl, quad, mcq, s0, w = iters[i]
        par_e = i % 6

        def op_expB(e, w=w, par_e=par_e):
            return e.activation(
                e_ap(par_e, 4 * w, [[1, 4 * w]]),
                stAB[:, 4 * w:NH * w], AF.Exp, scale=1.0 / 32.0)
        ev("scalar", [("t_st", m_stB[i]), ("s_exp", m_expA[i])], "s_exp", op_expB)
        last_sc[0] = ("s_exp", cnt["s_exp"])
        m_expB[i] = cnt["s_exp"]

    def stage_den(i):
        sl, quad, mcq, s0, w = iters[i]
        par_e = i % 6
        par_d = i % 2

        def op_d1(e, w=w, par_e=par_e):
            return e.tensor_tensor(d1[:, 0:4 * w],
                                   e_ap(par_e, 0, [[1, 4 * w]]),
                                   e_ap(par_e, 4 * w, [[1, 4 * w]]),
                                   op=OP.add)
        vec_op([("s_exp", m_expB[i])], "v_self", op_d1)
        vec_op([], "v_self", lambda e, w=w: e.tensor_tensor(
            d2[:, 0:2 * w], d1[:, 0:2 * w], d1[:, 2 * w:4 * w], op=OP.add))
        w3 = [("s_self", m_ln[i - 2])] if i - 2 >= 0 else []
        vec_op(w3, "v_den", lambda e, w=w, par_d=par_d: e.tensor_tensor(
            den[:, par_d * 256: par_d * 256 + w], d2[:, 0:w], d2[:, w:2 * w], op=OP.add))
        m_L3[i] = cnt["v_den"]

    def stage_lnexp(i):
        sl, quad, mcq, s0, w = iters[i]
        par_d = i % 2
        par_r = i % 4
        lw = [("v_den", m_L3[i])]
        if i - 1 >= 0:
            lw.append(("s_rcp", m_exp2[i - 1]))
        ev("scalar", lw, "s_self",
           lambda e, w=w, par_d=par_d: e.activation(
               lbuf[:, 0:w], den[:, par_d * 256: par_d * 256 + w], AF.Ln))
        last_sc[0] = ("s_self", cnt["s_self"])
        m_ln[i] = cnt["s_self"]
        w2 = [("v_du", m_U[i - 4])] if i - 4 >= 0 else []
        sc_op(w2, "s_rcp",
              lambda e, w=w, par_r=par_r: e.activation(
                  rrc[:, par_r * 256: par_r * 256 + w], lbuf[:, 0:w], AF.Exp,
                  scale=-1.0))
        m_exp2[i] = cnt["s_rcp"]

    def stage_U(i):
        sl, quad, mcq, s0, w = iters[i]
        par_e = i % 6
        par_u = i % 2
        par_r = i % 4
        wts = [("s_rcp", m_exp2[i])]
        if i - 2 >= 0:
            wts.append(("t_ctx", m_ctx[i - 2]))
        vec_op(wts, "v_du",
               lambda e, w=w, par_u=par_u, par_e=par_e, par_r=par_r: e.tensor_tensor(
                   arena_ap(UB0 + par_u * 2048, [[1, NH * w]]),
                   e_ap(par_e, 0, [[1, NH * w]]),
                   bass.AP(scratch, 1536 + par_r * 256,
                           [[scratch.shape[1], 128], [0, NH], [1, w]]),
                   op=OP.mult))
        m_U[i] = cnt["v_du"]

    def stage_ctx(i):
        sl, quad, mcq, s0, w = iters[i]
        par_k = quad % 2
        par_u = i % 2
        nts = w // 128
        k_sl = quad * 4 + mcq
        tw = [("v_du", m_U[i]), v_load_mark[(sl, quad)]]
        if k_sl == 0:
            tw.append(("s_cpx", cnt["s_cpx"]))
            if sl == 0:
                tw.append(("c_stv", mark_vt_all))

        def op_ctx(e, w=w, par_u=par_u, par_k=par_k, mcq=mcq, nts=nts,
                   first=(k_sl == 0), last=(k_sl == 31)):
            r = None
            for g in range(NH):
                for nt in range(nts):
                    reg = g * nts + nt
                    start = first and (reg % 4 == 0)
                    stop = last and (reg % 4 == 3 or reg == NH * nts - 1)
                    r = e.matmul(
                        ctxP[:, reg * 128:(reg + 1) * 128],
                        arena_ap(UB0 + par_u * 2048 + g * w + nt * 128, [[1, 128]]),
                        arena_ap(VQ0 + par_k * 4096 + g * 512 + mcq * 128, [[1, 128]]),
                        start=start, stop=stop)
            return r
        ev("tensor", tw, "t_ctx", op_ctx)
        m_ctx[i] = cnt["t_ctx"]
        if k_sl == 31:
            cw = [("t_ctx", m_ctx[i])]
            if ctxs_free[0] is not None:
                cw.append(ctxs_free[0])
            sc_op(cw, "s_cpx",
                  lambda e, nts=nts: e.activation(
                      ctxs[:, 0:NH * nts * 128], ctxP[:, 0:NH * nts * 128], AF.Copy))
            cpx_mark = ("s_cpx", cnt["s_cpx"])
            ti0 = s0 // 128
            for j in range(4):
                if nts == 2:
                    src = bass.AP(ctxs, j * 2 * 256,
                                  [[ctxs.shape[1], 128], [256, 2], [128, 2], [1, 128]])
                    dst = bass.AP(x_res, ti0 * 1024 + j * 2 * 128,
                                  [[x_res.shape[1], 128], [128, 2], [1024, 2], [1, 128]])
                else:
                    src = bass.AP(ctxs, j * 2 * 128,
                                  [[ctxs.shape[1], 128], [128, 2], [1, 128]])
                    dst = bass.AP(x_res, ti0 * 1024 + j * 2 * 128,
                                  [[x_res.shape[1], 128], [128, 2], [1, 128]])
                wts = [cpx_mark] if j == 0 else []
                pending_evac.append((wts, j == 3,
                                     lambda e, dst=dst, src=src: e.tensor_tensor(
                                         dst, dst, src, op=OP.add)))
            for nt2 in range(nts):
                ti = s0 // 128 + nt2
                so = (ti % 4) * 18
                bn12 = stats[:, so:so + 12]
                src_t = x_res[:, ti * 1024:(ti + 1) * 1024]
                pending_fstats.append(([],
                                       lambda e, bn12=bn12, src_t=src_t: e.bn_stats(
                                           bn12[:, 0:6], src_t[:, 0:512])))
                pending_fstats.append(([],
                                       lambda e, bn12=bn12, src_t=src_t: e.bn_stats(
                                           bn12[:, 6:12], src_t[:, 512:1024])))
                pending_fstats.append(([],
                                       lambda e, ti=ti, bn12=bn12: e.bn_aggr(
                                           fstats[:, ti * 2:ti * 2 + 2], bn12)))

    pending_evac = []
    pending_fstats = []
    ctxs_free = [None]

    def drain_evac(n):
        for _ in range(min(n, len(pending_evac))):
            wts, is_last, op = pending_evac.pop(0)
            vec_op(wts, "v_ctx" if is_last else "v_self", op)
            if is_last:
                ctxs_free[0] = (last_vec[0][0], last_vec[0][1])

    def drain_fstats(n):
        if pending_evac:
            return
        for _ in range(min(n, len(pending_fstats))):
            wts, op = pending_fstats.pop(0)
            vec_op(wts, "v_self", op)

    for slot in range(NIT + 4):
        if slot < NIT:
            stage_loads(slot)
            stage_st(slot)
        if 0 <= slot - 2 < NIT:
            stage_lnexp(slot - 2)
        if slot < NIT:
            stage_expA(slot)
            stage_expB(slot)
        if 0 <= slot - 1 < NIT:
            stage_den(slot - 1)
        if 0 <= slot - 3 < NIT:
            stage_U(slot - 3)
        drain_evac(1)
        drain_fstats(1)
        if 0 <= slot - 4 < NIT:
            stage_ctx(slot - 4)

    drain_evac(len(pending_evac))
    drain_fstats(len(pending_fstats))
    total_ctx = cnt["t_ctx"]
    total_vctx = cnt["v_ctx"]
    total_cpx = cnt["s_cpx"]
    for si in range(4):
        stsub_mark[si] = ("s_exp", cnt["s_exp"])

    total_st = cnt["t_st"]
    total_du = cnt["v_du"]
    for cj in range(4, NC):
        ev("sync", [("d_ws0", 16 * 12)], "d_tw",
           lambda e, cj=cj: e.dma_start_transpose(
               bass.AP(wkbuf, (cj - 4) * 1024, [[wkbuf.shape[1], 128], [1, 1024]]),
               bass.AP(wspill, 8 * 128 * D + cj * 128, [[D, 1024], [1, 128]])), 16)
    for cj in range(4):
        ev("sync", [("t_ctx", m_ctx[NIT - 5]), ("d_ws0", 16 * 12)], "d_tw",
           lambda e, cj=cj: e.dma_start_transpose(
               arena_ap(8192 + cj * 1024, [[1, 1024]]),
               bass.AP(wspill, 8 * 128 * D + cj * 128, [[D, 1024], [1, 128]])), 16)
    for wi, gate in [(0, ("t_ffn", 1)), (2, ("t_ffn", 9))]:
        for cj in range(NC):
            gw = [("t_ctx", total_ctx), ("v_du", total_du), ("t_st", total_st),
                  ("d_ws0", 16 * 12), gate]
            ev("sync", gw, "d_tw",
               lambda e, wi=wi, cj=cj: e.dma_start_transpose(
                   arena_ap(wi * 8192 + cj * 1024, [[1, 1024]]),
                   bass.AP(wspill, wi * 8 * 128 * D + cj * 128, [[D, 1024], [1, 128]])), 16)

    ev("scalar", [("v_self", cnt["v_self"])], "s_sd",
       lambda e: e.activation(stats[:, 0:QT],
                              bass.AP(fstats, 1, [[36, 128], [2, QT]]),
                              AF.Sqrt, bias=csts[:, 0:1]))
    ev("vector", [("s_sd", cnt["s_sd"])], "v_self",
       lambda e: e.reciprocal(bass.AP(fstats, 1, [[36, 128], [2, QT]]),
                              stats[:, 0:QT]))
    fstats_done = cnt["v_self"]

    kact = scratch[:, 8192:10240]
    rsb = arena[:, 24576:25600]
    prevbuf = arena[:, 25600:26624]
    relu_b = arena[:, 26624:27648]
    t1b = arena[:, 27648:28672]
    tsb = scratch_f[:, 1024:1536]

    ffn_mark = {}
    mix_mark = {}
    first_ffn_mm = [True]
    first_relu = [True]

    ln2_pend = {}

    ln2_par = [0]

    def ln2_chains(tc):
        tc0, wch = TCHUNKS[tc]
        ntiles = wch // 128
        t0 = tc0 // 128
        pend = []
        for tt in range(ntiles):
            ti = t0 + tt
            src2 = x_res[:, ti * 1024:(ti + 1) * 1024]
            par = ln2_par[0] % 2
            ln2_par[0] += 1
            dst = xn_sb[:, par * 1024:(par + 1) * 1024]
            n = 32 + ti
            nwaits = [("v_self", fstats_done), ("t_stw", xn_tr_mark[31])]
            if n - 2 in xn_tr_mark:
                nwaits.append(("t_stw", xn_tr_mark[n - 2]))
            ev("vector", nwaits, "s_xn",
               lambda e, dst=dst, src2=src2, ti=ti: e.tensor_scalar(
                   dst, src2, fstats[:, ti * 2:ti * 2 + 1],
                   fstats[:, ti * 2 + 1:ti * 2 + 2],
                   op0=OP.subtract, op1=OP.mult))
            pend.append((n, dst, cnt["s_xn"]))
        ln2_pend[tc] = pend

    def mix_part(tc, tiles=None):
        tc0, wch = TCHUNKS[tc]
        ntiles = wch // 128
        cp = tc % 2
        xr_sl = scratch[:, cp * 2048:(cp + 1) * 2048]
        xk_sl = scratch[:, 4096 + cp * 2048: 4096 + (cp + 1) * 2048]
        for tt in (range(ntiles) if tiles is None else tiles):
            n, dst, sxn_mark = ln2_pend[tc][tt]
            half = n % 2
            emit_tile_transposes(dst, half, identf[:],
                                 [("s_xn", sxn_mark), ("v_ctx", total_vctx)])
            xn_tr_mark[n] = cnt["t_stw"]
            cbase = half * 1024
            mw = [("t_stw", cnt["t_stw"]), ("v_mix", cnt["v_mix"])]
            if tc - 2 >= 0:
                mw.append(("t_ffn", ffn_mark[tc - 2]))
            ev_noinc("vector", mw, lambda e, cbase=cbase: e.tensor_copy(
                bass.AP(arena, 25600 + 1, [[A, 128], [128, NC], [1, 127]]),
                bass.AP(ctxP, cbase, [[2048, 128], [128, NC], [1, 127]])))
            ev("vector", [], "v_self", lambda e: e.tensor_copy(
                bass.AP(arena, 25600, [[A, 128], [128, NC]]), carry[:, 0:NC]))
            ev("vector", [("v_self", cnt["v_self"]), ("s_act", cnt["s_act"])], "v_self",
               lambda e, cbase=cbase: e.tensor_tensor(
                   t1b[:, 0:1024], ctxP[:, cbase:cbase + 1024],
                   prevbuf[:, 0:1024], op=OP.subtract))
            t1_chain = cnt["v_self"]
            ev("vector", [("v_self", t1_chain)], "c_stv", lambda e, cbase=cbase: e.tensor_copy(
                carry[:, 0:NC], bass.AP(ctxP, cbase + 127, [[2048, 128], [128, NC]])))
            tr_mark[half] = cnt["c_stv"]
            mixtmp = arena[:, 28672:29696]
            xr_off = cp * 2048 + tt * 128
            xk_off = 4096 + cp * 2048 + tt * 128
            ev("vector", [("v_self", t1_chain)], "v_self",
               lambda e: e.tensor_tensor(
                   mixtmp[:, 0:1024], t1b[:, 0:1024],
                   bass.AP(cvec, 24, [[40, 128], [1, 8], [0, 128]]),
                   op=OP.mult))
            ev("vector", [("v_self", cnt["v_self"])] , "v_self",
               lambda e, xr_off=xr_off: e.tensor_tensor(
                   bass.AP(scratch, xr_off, [[scratch.shape[1], 128], [256, NC], [1, 128]]),
                   mixtmp[:, 0:1024], prevbuf[:, 0:1024], op=OP.add))
            ev("vector", [("v_self", cnt["v_self"])], "v_self",
               lambda e: e.tensor_tensor(
                   mixtmp[:, 0:1024], t1b[:, 0:1024],
                   bass.AP(cvec, 32, [[40, 128], [1, 8], [0, 128]]),
                   op=OP.mult))
            ev("vector", [("v_self", cnt["v_self"])], "v_mix",
               lambda e, xk_off=xk_off: e.tensor_tensor(
                   bass.AP(scratch, xk_off, [[scratch.shape[1], 128], [256, NC], [1, 128]]),
                   mixtmp[:, 0:1024], prevbuf[:, 0:1024], op=OP.add))
        mix_mark[tc] = cnt["v_mix"]

    def ffn_matmuls(tc, after_k=None, after_tile=None):
        tc0, wch = TCHUNKS[tc]
        ntiles = wch // 128
        t0 = tc0 // 128
        cp = tc % 2
        xr_sl = scratch[:, cp * 2048:(cp + 1) * 2048]
        xk_sl = scratch[:, 4096 + cp * 2048: 4096 + (cp + 1) * 2048]
        for dj in range(NC):
            si = next_stsub()
            ps = stsub[si]
            tw = [("v_mix", mix_mark[tc]), (stsub_mark[si][0], stsub_mark[si][1])]
            if first_ffn_mm[0]:
                tw.append(("d_tw", 16 * 8))
                first_ffn_mm[0] = False

            def op_k(e, ps=ps, dj=dj, wch=wch, xk_sl=xk_sl):
                r = None
                for cj in range(NC):
                    if cj < 4:
                        wsrc2 = arena_ap((8 + cj) * 1024 + dj * 128, [[1, 128]])
                    else:
                        wsrc2 = bass.AP(wkbuf, (cj - 4) * 1024 + dj * 128,
                                        [[wkbuf.shape[1], 128], [1, 128]])
                    r = e.matmul(ps[:, 0:wch], wsrc2,
                                 xk_sl[:, cj * 256: cj * 256 + wch],
                                 start=(cj == 0), stop=(cj == NC - 1))
                return r
            ev("tensor", tw, "t_ffn", op_k)
            sw = [("t_ffn", cnt["t_ffn"]), ("v_sq", max(0, cnt["v_sq"] - 1))]
            if first_relu[0]:
                sw.append(("t_ctx", total_ctx))
                first_relu[0] = False
            pj = dj % 2
            ev("scalar", sw, "s_act",
               lambda e, ps=ps, wch=wch, pj=pj: e.activation(
                   relu_b[:, pj * 512: pj * 512 + wch], ps[:, 0:wch], AF.Relu))
            stsub_mark[si] = ("s_act", cnt["s_act"])
            ev("vector", [("s_act", cnt["s_act"])], "v_sq",
               lambda e, pj=pj, dj=dj, wch=wch: e.tensor_tensor(
                   kact[:, dj * 256: dj * 256 + wch],
                   relu_b[:, pj * 512: pj * 512 + wch],
                   relu_b[:, pj * 512: pj * 512 + wch], op=OP.mult))
        ksq_done = cnt["v_sq"]
        if after_k is not None:
            after_k()

        for tt in range(ntiles):
            if after_tile is not None and tt > 0:
                after_tile(tt - 1)
            ti = t0 + tt
            fin_before = cnt["v_fin"]
            for ds in range(2):
                si = next_stsub()
                ps = stsub[si]
                tw = [(stsub_mark[si][0], stsub_mark[si][1])]
                if tt == 0 and ds == 0:
                    tw.append(("d_tw", 16 * 16))

                def op_r(e, ps=ps, tt=tt, ds=ds, xr_sl=xr_sl):
                    r = None
                    for cj in range(NC):
                        r = e.matmul(ps[:, 0:512],
                                     xr_sl[:, cj * 256 + tt * 128: cj * 256 + (tt + 1) * 128],
                                     arena_ap(cj * 1024 + ds * 512, [[1, 512]]),
                                     start=(cj == 0), stop=(cj == NC - 1))
                    return r
                ev("tensor", tw, "t_ffn", op_r)
                ev("scalar", [("t_ffn", cnt["t_ffn"]), ("v_fin", fin_before)], "s_act",
                   lambda e, ps=ps, ds=ds: e.activation(
                       rsb[:, ds * 512:(ds + 1) * 512], ps[:, 0:512], AF.Sigmoid))
                stsub_mark[si] = ("s_act", cnt["s_act"])
            sig_done = cnt["s_act"]
            for ds in range(2):
                si = next_stsub()
                ps = stsub[si]
                tw = [("v_sq", ksq_done), (stsub_mark[si][0], stsub_mark[si][1])]
                if tt == 0 and ds == 0:
                    tw.append(("d_tw", 16 * 24))

                def op_v(e, ps=ps, tt=tt, ds=ds):
                    r = None
                    for cj in range(NC):
                        r = e.matmul(ps[:, 0:512],
                                     kact[:, cj * 256 + tt * 128: cj * 256 + (tt + 1) * 128],
                                     arena_ap((16 + cj) * 1024 + ds * 512, [[1, 512]]),
                                     start=(cj == 0), stop=(cj == NC - 1))
                    return r
                ev("tensor", tw, "t_ffn", op_v)
                ev("vector", [("t_ffn", cnt["t_ffn"]), ("s_act", sig_done),
                              ("v_self", cnt["v_self"])], "v_self",
                   lambda e, ps=ps, ds=ds: e.tensor_tensor(
                       tsb[:, 0:512], ps[:, 0:512],
                       rsb[:, ds * 512:(ds + 1) * 512], op=OP.mult))
                dst = x_res[:, ti * 1024 + ds * 512: ti * 1024 + ds * 512 + 512]
                is_last = (ds == 1)
                ev("vector", [("v_self", cnt["v_self"])],
                   "v_fin" if is_last else "v_self",
                   lambda e, dst=dst: e.tensor_tensor(dst, dst, tsb[:, 0:512], op=OP.add))
                stsub_mark[si] = ("v_fin", cnt["v_fin"] if is_last else cnt["v_fin"] + 1)
        ffn_mark[tc] = cnt["t_ffn"]

        ev("sync", [("v_fin", cnt["v_fin"])], "d_out",
           lambda e, t0=t0, ntiles=ntiles: e.dma_start(
               bass.AP(out_h, t0 * 128 * 1024,
                       [[1024, 128], [128 * 1024, ntiles], [1, 1024]]),
               ap3(x_res, t0 * 1024, [[1024, ntiles], [1, 1024]])), 16)

    emit_warmup(48, [("s_cpx", total_cpx)])
    ln2_chains(0)
    mix_part(0)
    for tc in range(len(TCHUNKS)):
        if tc + 1 < len(TCHUNKS):
            nt_next = TCHUNKS[tc + 1][1] // 128
            ln2_chains(tc + 1)
            ffn_matmuls(tc,
                        after_k=lambda tc=tc: mix_part(tc + 1, [0]),
                        after_tile=lambda tt, tc=tc, n=nt_next:
                            mix_part(tc + 1, [tt + 1]) if tt + 1 < n else None)
            if nt_next > TCHUNKS[tc][1] // 128:
                mix_part(tc + 1, list(range(TCHUNKS[tc][1] // 128, nt_next)))
        else:
            ffn_matmuls(tc)

    ev_noinc("sync", [("d_out", cnt["d_out"])], lambda e: e.nop())

    with nc.Block() as block:
        @block.sync
        def _(e):
            for fn in sched["sync"]:
                fn(e)

        @block.gpsimd
        def _(e):
            for fn in sched["gpsimd"]:
                fn(e)

        @block.tensor
        def _(e):
            for fn in sched["tensor"]:
                fn(e)

        @block.scalar
        def _(e):
            for fn in sched["scalar"]:
                fn(e)

        @block.vector
        def _(e):
            for fn in sched["vector"]:
                fn(e)

    es.close()
    return nc


def _in_maps(inputs):
    x = np.asarray(inputs["L0_attn_hidden_states"], dtype=np.float32)
    cvec = np.empty((128, 40), dtype=np.float32)
    for i, k in enumerate(["L0_attn_bq", "L0_attn_bk", "L0_attn_bv",
                           "L0_ffn_mu_r", "L0_ffn_mu_k"]):
        cvec[:, i * 8:(i + 1) * 8] = np.asarray(inputs[k], np.float32).reshape(8, 128).T
    ident = np.eye(128, dtype=ml_dtypes.bfloat16)
    identf = np.eye(128, dtype=np.float32)
    base = {
        "wq": np.asarray(inputs["L0_attn_Wq"], np.float32),
        "wk": np.asarray(inputs["L0_attn_Wk"], np.float32),
        "wv": np.asarray(inputs["L0_attn_Wv"], np.float32),
        "wr2": np.asarray(inputs["L0_ffn_W_r"], np.float32),
        "wk2": np.asarray(inputs["L0_ffn_W_k"], np.float32),
        "wv2": np.asarray(inputs["L0_ffn_W_v"], np.float32),
        "cvec": cvec, "ident": ident, "identf": identf,
    }
    maps = []
    for c in range(8):
        bi, half = c // 2, c % 2
        xb = x[bi] if half == 0 else np.roll(x[bi], -1920, axis=0)
        m = dict(base)
        m["x"] = np.ascontiguousarray(xb)
        maps.append(m)
    return maps


def kernel(**inputs):
    maps = _in_maps(inputs)
    nc = build_nc()
    res = run_bass_kernel_spmd(nc, maps, core_ids=list(range(8)))
    out = np.empty((4, 4096, 1024), dtype=np.float32)
    for c in range(8):
        bi, half = c // 2, c % 2
        o = np.asarray(res.results[c]["out"])
        if half == 0:
            out[bi, 0:2048] = o[0:2048]
        else:
            out[bi, 2048:4096] = o[128:2176]
    return out
